# version 02
# baseline (speedup 1.0000x reference)
from contextlib import ExitStack

import numpy as np
import ml_dtypes

import concourse.bass as bass
import concourse.mybir as mybir
from concourse.bass_utils import run_bass_kernel_spmd

F32 = mybir.dt.float32
BF16 = mybir.dt.bfloat16
AF = mybir.ActivationFunctionType
ALU = mybir.AluOpType
AX = mybir.AxisListType

S, D, HD, II, NT, NSLOT = 4096, 2048, 64, 8192, 1024, 8
PFX = [4, 8, 12, 16, 20, 24, 28, 32]
MASK_NEG = -1.0e30
INV_D = 1.0 / 2048.0
INV_SQ1 = float(2048.0 ** -0.5)
INV_SQ2 = float(8192.0 ** -0.5)
ARENA = 52224

O_H128, O_ID, O_P2, O_ONES1, O_ONESC = 0, 128, 256, 384, 512
O_SVAL = 516
O_SG = 520
O_KM = 528
O_KN = 536
O_CG = 568
O_IKQ = 576
O_INDQ = 640
O_INDR = 1152
O_INDG = 3200
O_BIAS = 4224
O_CT = 4288
O_ROW1 = 4544
O_ROW2 = 5568
O_ROWA = 6592
O_ROWB = 7616
O_GACC = 8640
O_FS1 = 9472
O_FS2 = 9984
O_FS3 = 10496
O_KVW = 11008
O_XS = 27500
O_CK, O_SK, O_VSTG = 35800, 36312, 36824
O_KA, O_KB, O_KC, O_KNS = 37400, 37912, 38424, 38936
O_QT = 11008
O_XMH = 27500
O_WQS = 35800
O_CQ, O_SQ = 39900, 40924
O_QN = 41950
O_SA, O_SB, O_SC = 42980, 43492, 44004
O_OSB = 27500
O_MASK = 35800
O_KG = 39900
O_VG = 44000
O_YSB = 39900
O_PB = 48200
O_EV = 49224
O_DEN = 49740
O_P1 = 50176
O_P1D = 51200
O_XQ1 = 11008
O_WG, O_WU = 15104, 17152
O_SH = 19200
O_STGV, O_STGU, O_STGM = 13056, 13568, 14080
O_STGQ = 14592
O_DNS = 11008
O_WDS = 27500
O_XQ2B = 31700
O_X1B = 32800
O_OUTA = 34900
O_LNS = 37000


def _hadamard(n):
    h = np.array([[1.0]])
    while h.shape[0] < n:
        h = np.block([[h, h], [h, -h]])
    return h


def _qperm():
    order = []
    for G in range(4):
        for r in range(4):
            for h in (8 * G + r, 8 * G + 4 + r):
                order.extend(range(h * HD, (h + 1) * HD))
    return np.array(order)


def _core_qtiles(c):
    return [c, 7 - c, 8 + c, 15 - c, 16 + c, 23 - c, 24 + c, 31 - c]


def _strips(aT, pi=128):
    K, M = aT.shape
    t = aT.reshape(K // 128, 128, M // pi, pi)
    return np.ascontiguousarray(
        t.transpose(2, 1, 0, 3).reshape(M // pi, 128, (K // 128) * pi))


def _rope_tables():
    inv = 1.0 / (10000.0 ** (np.arange(0, HD, 2, dtype=np.float64) / HD))
    pos = np.arange(S, dtype=np.float64)
    fr = pos[:, None] * inv[None, :]
    emb = np.concatenate([fr, fr], axis=1)
    cos = np.cos(emb).T.astype(np.float32)
    sin = np.sin(emb).T.astype(np.float32)
    return (np.ascontiguousarray(np.concatenate([cos, cos], 0)),
            np.ascontiguousarray(np.concatenate([sin, sin], 0)))


def _masks_for_core(c):
    qt = _core_qtiles(c)
    m = np.zeros((NSLOT, 4, 128, 512), dtype=np.float32)
    for s in range(NSLOT):
        for j in range(4):
            kt = PFX[s] - 4 + j
            kpos = kt * 128 + np.arange(128)[:, None]
            qpos = qt[s] * 128 + np.arange(128)[None, :]
            m[s, j] = np.tile(np.where(kpos > qpos, MASK_NEG, 0.0), (1, 4))
    return m.astype(ml_dtypes.bfloat16)


class Sem:
    def __init__(self, h):
        self.h = h
        self.n = 0

    def bump(self, k=1):
        self.n += k
        return self.n


class E:
    def __init__(self, name):
        self.name = name
        self.ops = []
        self.seen = {}

    def wait(self, sem, val):
        if val is None or val <= 0:
            return
        if self.seen.get(id(sem), -1) >= val:
            return
        self.seen[id(sem)] = val
        self.ops.append(lambda e, s=sem.h, v=val: e.wait_ge(s, v))

    def waitev(self, ev):
        if ev is not None:
            self.wait(ev[0], ev[1])

    def op(self, fn, inc=None, k=1):
        if inc is None:
            self.ops.append(fn)
            return None
        t = inc.bump(k)
        self.ops.append(lambda e, f=fn, s=inc.h, kk=k: f(e).then_inc(s, kk))
        return (inc, t)

    def dma(self, dst, src, sem):
        return self.op(lambda e, d=dst, s=src: e.dma_start(out=d, in_=s),
                       inc=sem, k=16)


DBG_SCRATCH = [False]


def build_nc():
    nc = bass.Bass("TRN2", target_bir_lowering=False)

    def dinp(name, shape, dt=F32):
        return nc.dram_tensor(name, list(shape), dt, kind="ExternalInput").ap()

    xTm = dinp("xTm", [16, 128, NT])
    xTf = dinp("xTf", [16, 128, S])
    wqt = dinp("wqt", [16, 128, 2048])
    wkt = dinp("wkt", [4, 128, 2048])
    wvt = dinp("wvt", [4, 128, 2048])
    wot = dinp("wot", [16, 128, 2048])
    wgt = dinp("wgt", [64, 128, 2048])
    wut = dinp("wut", [64, 128, 2048])
    wdt = dinp("wdt", [16, 128, 8192])
    cosq = dinp("cosq", [128, NT])
    sinq = dinp("sinq", [128, NT])
    cosk = dinp("cosk", [128, S])
    sink = dinp("sink", [128, S])
    masksI = dinp("masks", [NSLOT, 4, 128, 512], BF16)
    c_h128 = dinp("h128", [128, 128])
    c_ident = dinp("ident", [128, 128])
    c_p2 = dinp("p2", [128, 128])
    c_ones1 = dinp("ones1", [1, 128])
    c_onescol = dinp("onescol", [128, 1])
    c_indq = dinp("indq", [128, 512])
    c_indk = dinp("indk", [128, 32])
    c_indkq = dinp("indkq", [8, 32])
    c_indrep = dinp("indrep", [32, 2048])
    c_indgrp = dinp("indgrp", [8, 1024])

    kd = "ExternalOutput" if DBG_SCRATCH[0] else "Internal"
    kt_sc = nc.dram_tensor("kt_sc", [4, 128, S], F32, kind=kd).ap()
    vt_sc = nc.dram_tensor("vt_sc", [4, 128, 32 * 130], F32, kind=kd).ap()
    x1_sc = nc.dram_tensor("x1_sc", [16, 128, NT], F32, kind=kd).ap()
    xq1_sc = nc.dram_tensor("xq1_sc", [16, 128, NT], BF16, kind=kd).ap()
    xq2_sc = nc.dram_tensor("xq2_sc", [64, 128, NT], BF16, kind=kd).ap()
    out = nc.dram_tensor("out", [NT, D], F32, kind="ExternalOutput").ap()
    dbgA = nc.dram_tensor("dbgA", [128, 4096], F32,
                          kind=("ExternalOutput" if DBG_SCRATCH[0] else "Internal")).ap()

    ctx = ExitStack()
    ar = ctx.enter_context(nc.sbuf_tensor("arena", [128, ARENA], F32))
    s_indk = ctx.enter_context(nc.sbuf_tensor("s_indk", [128, 32], F32))
    arB = ar[:].bitcast(BF16)

    def A(off, cols, p0=0, pn=128):
        return ar[p0:p0 + pn, off:off + cols]

    def Ab(off_f32, cols_bf16, p0=0, pn=128):
        return arB[p0:p0 + pn, 2 * off_f32:2 * off_f32 + cols_bf16]

    pp = [ctx.enter_context(nc.psum_tensor(f"pp{j}", [128, 512], F32))
          for j in range(8)]

    sems = {k: Sem(ctx.enter_context(nc.semaphore(k)))
            for k in ("ld", "st", "pe", "vec", "act", "gp", "gld")}
    ld, st, pe, vec, act, gp, gld = (sems[k] for k in
                                     ("ld", "st", "pe", "vec", "act", "gp", "gld"))

    Es, Ep, Ev, Ea, Eg = E("sync"), E("tensor"), E("vector"), E("scalar"), E("gpsimd")
    ALL_E = (Es, Ep, Ev, Ea, Eg)

    guard = {}

    def g_wait(Eng, key):
        for ev in guard.get(key, ()):
            Eng.waitev(ev)

    def g_set(key, *evs):
        guard[key] = [e for e in evs if e is not None]

    def barrier():
        evs = [(pe, pe.n), (vec, vec.n), (act, act.n), (st, st.n),
               (ld, ld.n), (gp, gp.n)]
        for Eng in ALL_E:
            for ev in evs:
                Eng.waitev(ev)

    def mm(out_ap, lhsT, rhs, start=True, stop=True):
        return lambda e: e.matmul(out_ap, lhsT, rhs, start=start, stop=stop)

    def cp(dst, src):
        return lambda e: e.tensor_copy(dst, src)

    def tt(dst, a, b, op):
        return lambda e: e.tensor_tensor(dst, a, b, op)

    def ts(dst, a, s1, op0, s2=None, op1=None):
        if op1 is None:
            return lambda e: e.tensor_scalar(dst, a, s1, None, op0)
        return lambda e: e.tensor_scalar(dst, a, s1, s2, op0, op1)

    def stt(dst, in0, scalar, in1, op0, op1):
        return lambda e: e.scalar_tensor_tensor(dst, in0, scalar, in1, op0, op1)

    def actv(dst, src, func, bias=0.0, scale=1.0):
        return lambda e: e.activation(out=dst, in_=src, func=func,
                                      bias=bias, scale=scale)

    def red(dst, src, op, absval=False, axis=None):
        ax = AX.X if axis is None else axis
        return lambda e: e.tensor_reduce(dst, src, ax, op,
                                         apply_absolute_value=absval)

    def recip(dst, src):
        return lambda e: e.reciprocal(dst, src)

    def trn(out_ap, in_ap):
        return lambda e: e.transpose(out_ap, in_ap, ar[0:128, O_ID:O_ID + 128])

    def vsep():
        Ev.op(cp(A(O_CT + 254, 1), A(O_ONESC, 1)))

    H128ap = A(O_H128, 128)
    P2ap = A(O_P2, 128)
    ONES1 = A(O_ONES1, 128, 0, 1)
    ONESC = A(O_ONESC, 1)

    for dst, src in [(A(O_H128, 128), c_h128), (A(O_ID, 128), c_ident),
                     (A(O_P2, 128), c_p2), (ONES1, c_ones1), (ONESC, c_onescol),
                     (A(O_INDQ, 512), c_indq), (s_indk[:], c_indk),
                     (A(O_IKQ, 32, 0, 8), c_indkq),
                     (A(O_INDR, 2048, 0, 32), c_indrep),
                     (A(O_INDG, 1024, 0, 8), c_indgrp)]:
        Es.dma(dst, src[:], ld)
    t_const = ld.n
    for r in range(4):
        Eg.op(lambda e, rr=r: e.memset(A(O_VSTG + rr * 130 + 64, 1), 1.0))
        Eg.op(lambda e, rr=r: e.memset(A(O_VSTG + rr * 130 + 129, 1), 1.0))
    Eg.op(lambda e: e.memset(A(O_GACC, 768), 0.0), inc=gp)

    for j, src in enumerate([wkt, wvt]):
        for o in range(4):
            Es.dma(A(O_KVW + (j * 4 + o) * 2048, 2048), src[o], ld)
    t_kvw = ld.n

    def kvw(is_v, o, k):
        return A(O_KVW + ((4 if is_v else 0) + o) * 2048 + k * 128, 128)

    p1_loads = []
    for wsrc, nslab, wcols in ((wgt, 64, 2048), (wut, 64, 2048), (wdt, 16, 8192)):
        for sl in range(nslab):
            for hh in range(wcols // 512):
                p1_loads.append(wsrc[sl][:, hh * 512:hh * 512 + 512])
    p1_state = {"ld": 0, "cons": 0, "consev": {}, "ldev": {}}

    def actv_acc(dst, srcq, func, accum):
        return lambda e: e.activation(out=dst, in_=srcq, func=func,
                                      bias=0.0, scale=1.0, accum_out=accum)

    def pump_pass1(n=1):
        for _ in range(n):
            i = p1_state["ld"]
            if i < len(p1_loads):
                if i >= 2:
                    Eg.waitev(p1_state["consev"].get(i - 2))
                p1_state["ldev"][i] = Eg.dma(A(O_P1 + (i % 2) * 512, 512),
                                             p1_loads[i], gld)
                p1_state["ld"] += 1
            j = p1_state["cons"]
            if j < len(p1_loads) and j < p1_state["ld"]:
                Ea.waitev(p1_state["ldev"][j])
                ev = Ea.op(actv_acc(A(O_P1D, 512), A(O_P1 + (j % 2) * 512, 512),
                                    AF.Abs, A(O_GACC + j, 1)), inc=act)
                p1_state["consev"][j] = ev
                p1_state["cons"] += 1

    def xchunk(k):
        return A(O_XS + (k // 8) * 4096 + (k % 8) * 512, 512)

    Ep.wait(ld, t_kvw)
    last_x_use = None
    for tb in range(8):
        Es.waitev(last_x_use)
        for k in range(16):
            t_x = Es.dma(xchunk(k), xTf[k][:, tb * 512:tb * 512 + 512], ld)
        g_wait(Es, "cksk")
        Es.dma(A(O_CK, 512), cosk[:, tb * 512:tb * 512 + 512], ld)
        t_cs = Es.dma(A(O_SK, 512), sink[:, tb * 512:tb * 512 + 512], ld)
        for o8 in range(8):
            pump_pass1(2)
            is_v = o8 >= 4
            o = o8 % 4
            ppj = pp[o8 % 2]
            g_wait(Ep, f"pp{o8 % 2}")
            Ep.wait(ld, t_x[1])
            for k in range(16):
                t_pe = Ep.op(mm(ppj[:, 0:512], kvw(is_v, o, k), xchunk(k),
                                start=(k == 0), stop=(k == 15)),
                             inc=(pe if k == 15 else None))
            if not is_v:
                g_wait(Ev, "KA")
                Ev.waitev(t_pe)
                t_ev = Ev.op(cp(A(O_KA, 512), ppj[:, 0:512]), inc=vec)
                g_set(f"pp{o8 % 2}", t_ev)
                g_wait(Ep, "pp23")
                Ep.waitev(t_ev)
                t_pr = Ep.op(mm(pp[2][:, 0:512], P2ap, A(O_KA, 512)), inc=pe)
                Ev.wait(ld, t_cs[1])
                Ev.op(tt(A(O_KB, 512), A(O_KA, 512), A(O_CK, 512), ALU.mult))
                Ev.waitev(t_pr)
                Ev.op(tt(A(O_KC, 512), pp[2][:, 0:512], A(O_SK, 512), ALU.mult))
                t_rope = Ev.op(tt(A(O_KA, 512), A(O_KB, 512), A(O_KC, 512),
                                  ALU.add), inc=vec)
                g_set("pp23", t_rope)
                g_set("cksk", t_rope)
                Ea.waitev(t_rope)
                t_sp = Ea.dma(kt_sc[o][:, tb * 512:tb * 512 + 512],
                              A(O_KA, 512), st)
                t_sq = Ea.op(actv(A(O_KNS, 512), A(O_KA, 512), AF.Square),
                             inc=act)
                g_set("KA", (st, t_sp[1]), t_sq)
                g_wait(Ep, "pp4")
                Ep.waitev(t_sq)
                t_kn = Ep.op(mm(pp[4][0:8, 0:512], s_indk[:, o * 8:o * 8 + 8],
                                A(O_KNS, 512)), inc=pe)
                Ev.waitev(t_kn)
                Ev.op(cp(A(O_KB, 512, 0, 8), pp[4][0:8, 0:512]))
                vsep()
                t_knr = Ev.op(red(A(O_KN + tb * 4, 4, 0, 8),
                                  A(O_KB, 512, 0, 8).rearrange(
                                      "p (a b) -> p a b", a=4),
                                  ALU.max), inc=vec)
                g_set("pp4", t_knr)
            else:
                g_wait(Ev, "KB")
                Ev.waitev(t_pe)
                t_ev = Ev.op(cp(A(O_KB, 512), ppj[:, 0:512]), inc=vec)
                g_set(f"pp{o8 % 2}", t_ev)
                g_wait(Ev, "vstg")
                g_wait(Ep, "pp6")
                Ep.waitev(t_ev)
                for r in range(4):
                    t_tr = Ep.op(trn(pp[6][:, r * 128:r * 128 + 128],
                                     A(O_KB + r * 128, 128)),
                                 inc=(pe if r == 3 else None))
                Ev.waitev(t_tr)
                for r in range(4):
                    Ev.op(cp(A(O_VSTG + r * 130, 64),
                             pp[6][:, r * 128:r * 128 + 64]))
                    t_vc = Ev.op(cp(A(O_VSTG + r * 130 + 65, 64),
                                    pp[6][:, r * 128 + 64:r * 128 + 128]),
                                 inc=vec)
                g_set("pp6", t_vc)
                g_set("KB", t_vc)
                Ea.waitev(t_vc)
                t_vs = Ea.dma(vt_sc[o][:, tb * 4 * 130:(tb * 4 + 4) * 130],
                              A(O_VSTG, 520), st)
                g_set("vstg", (st, t_vs[1]))
        last_x_use = (pe, pe.n)
    t_c_spill = st.n

    barrier()
    Es.dma(A(O_CQ, 1024), cosq[:], ld)
    t_cq = Es.dma(A(O_SQ, 1024), sinq[:], ld)

    def QT(j):
        return A(O_QT + j * 1024, 1024)

    t_qn = None
    for half in range(2):
        g_wait(Es, "xmh")
        for k in range(16):
            t_xm = Es.dma(A(O_XMH + k * 512, 512),
                          xTm[k][:, half * 512:half * 512 + 512], ld)
        t_xm_last_use = None
        for o in range(16):
            pump_pass1(4)
            wbuf = A(O_WQS + (o % 2) * 2048, 2048)
            g_wait(Es, f"wqs{o % 2}")
            t_w = Es.dma(wbuf, wqt[o], ld)
            ppj = pp[o % 2]
            g_wait(Ep, f"pp{o % 2}")
            Ep.wait(ld, t_xm[1])
            Ep.wait(ld, t_w[1])
            for k in range(16):
                t_pe = Ep.op(mm(ppj[:, 0:512], wbuf[:, k * 128:k * 128 + 128],
                                A(O_XMH + k * 512, 512),
                                start=(k == 0), stop=(k == 15)),
                             inc=(pe if k == 15 else None))
            g_set(f"wqs{o % 2}", t_pe)
            t_xm_last_use = t_pe
            qdst = QT(o)[:, half * 512:half * 512 + 512]
            Ev.waitev(t_pe)
            t_ev = Ev.op(cp(qdst, ppj[:, 0:512]), inc=vec)
            g_set(f"pp{o % 2}", t_ev)
            g_wait(Ep, "pp23")
            Ep.waitev(t_ev)
            t_pr = Ep.op(mm(pp[2][:, 0:512], P2ap, qdst), inc=pe)
            Ev.wait(ld, t_cq[1])
            Ev.op(tt(A(O_SA, 512), qdst, A(O_CQ + half * 512, 512), ALU.mult))
            Ev.waitev(t_pr)
            Ev.op(tt(A(O_SB, 512), pp[2][:, 0:512],
                     A(O_SQ + half * 512, 512), ALU.mult))
            t_rope = Ev.op(tt(qdst, A(O_SA, 512), A(O_SB, 512), ALU.add),
                           inc=vec)
            g_set("pp23", t_rope)
            g_wait(Ea, "SC")
            Ea.waitev(t_rope)
            t_sq = Ea.op(actv(A(O_SC, 512), qdst, AF.Square), inc=act)
            if o == 0:
                g_wait(Ep, f"pp{4 + half}")
            Ep.waitev(t_sq)
            t_qn = Ep.op(mm(pp[4 + half][0:32, 0:512],
                            A(O_INDQ + o * 32, 32), A(O_SC, 512),
                            start=(o == 0), stop=(o == 15)), inc=pe)
            g_set("SC", t_qn)
        g_set("xmh", t_xm_last_use)
        Ev.waitev(t_qn)
        t_qe = Ev.op(cp(A(O_QN + half * 512, 512, 0, 32),
                        pp[4 + half][0:32, 0:512]), inc=vec)
        g_set(f"pp{4 + half}", t_qe)

    t_qm = Ev.op(red(A(O_CG, 8, 0, 32),
                     A(O_QN, 1024, 0, 32).rearrange("p (a b) -> p a b", a=8),
                     ALU.max), inc=vec)
    Ea.waitev(t_qm)
    t_qs = Ea.op(actv(A(O_CT, 8, 0, 32), A(O_CG, 8, 0, 32), AF.Sqrt), inc=act)
    for s in range(NSLOT):
        tkr = Ev.op(red(A(O_KM + s, 1, 0, 8), A(O_KN, PFX[s], 0, 8), ALU.max),
                    inc=(vec if s == NSLOT - 1 else None))
    Ea.waitev(tkr)
    t_ks = Ea.op(actv(A(O_CT + 8, 8, 0, 8), A(O_KM, 8, 0, 8), AF.Sqrt), inc=act)
    g_wait(Ep, "pp6")
    Ep.waitev(t_ks)
    t_kx = Ep.op(mm(pp[6][0:32, 0:8], A(O_IKQ, 32, 0, 8), A(O_CT + 8, 8, 0, 8)),
                 inc=pe)
    Ev.waitev(t_kx)
    Ev.waitev(t_qs)
    Ev.op(lambda e: e.memset(A(O_CT + 16, 128), 0.0))
    t_c1 = Ev.op(tt(A(O_CT + 16, 8, 0, 32), A(O_CT, 8, 0, 32), pp[6][0:32, 0:8],
                    ALU.mult), inc=vec)
    g_set("pp6", t_c1)
    g_wait(Ep, "pp6")
    Ep.waitev(t_c1)
    t_tr1 = Ep.op(trn(pp[6][:, 0:128], A(O_CT + 16, 128)), inc=pe)
    Ev.waitev(t_tr1)
    Ev.op(lambda e: e.memset(A(O_CT + 16, 128), 0.0))
    t_gm = Ev.op(red(A(O_CT + 16, 8, 0, 8),
                     pp[6][0:8, 0:32].rearrange("p (a b) -> p a b", a=8),
                     ALU.max), inc=vec)
    g_set("pp6", t_gm)
    g_wait(Ep, "pp6")
    Ep.waitev(t_gm)
    t_tr2 = Ep.op(trn(pp[6][:, 0:128], A(O_CT + 16, 128)), inc=pe)
    Ev.waitev(t_tr2)
    t_c2 = Ev.op(ts(A(O_CT + 144, 8, 0, 8), pp[6][0:8, 0:8], -0.125,
                    ALU.mult), inc=vec)
    g_set("pp6", t_c2)
    g_wait(Ep, "pp7")
    Ep.waitev(t_c2)
    for g8 in range(8):
        t_bb = Ep.op(mm(pp[7][:, g8 * 8:g8 * 8 + 8],
                        A(O_INDG + g8 * 128, 128, 0, 8),
                        A(O_CT + 144, 8, 0, 8)),
                     inc=(pe if g8 == 7 else None))
    Ev.waitev(t_bb)
    t_bias = Ev.op(cp(A(O_BIAS, 64), pp[7][:, 0:64]), inc=vec)
    g_set("pp7", t_bias)

    if DBG_SCRATCH[0]:
        Eg.wait(vec, vec.n)
        Eg.wait(pe, pe.n)
        Eg.dma(dbgA[:, 0:64], A(O_BIAS, 64), st)
        Eg.dma(dbgA[0:8, 192:200], A(O_KM, 8, 0, 8), st)
        Eg.dma(dbgA[0:8, 200:232], A(O_KN, 32, 0, 8), st)
        Eg.dma(dbgA[0:32, 232:240], A(O_CG, 8, 0, 32), st)
        Eg.dma(dbgA[:, 704:1728], A(O_QT, 1024), st)
    barrier()
    qt_my = None
    ocnt = 0
    for sg in range(2):
        barrier()
        g_wait(Es, "maskarea")
        for si in range(4):
            for j in range(4):
                t_mk = Es.dma(Ab(O_MASK + (si * 4 + j) * 256, 512),
                              masksI[sg * 4 + si][j], ld)
        if DBG_SCRATCH[0] and sg == 0:
            Es.wait(ld, t_mk[1])
            Es.dma(dbgA[:, 2752:3776], A(O_MASK, 1024), st)
        t_kgvg = None
        for G in range(4):
            g_wait(Es, "kgvg")
            Es.wait(st, t_c_spill)
            Es.dma(A(O_KG, 4096), kt_sc[G], ld)
            t_kgvg = Es.dma(A(O_VG, 4160), vt_sc[G], ld)
            for grp in range(2):
                for si in range(4):
                    pump_pass1(8)
                    s = sg * 4 + si
                    po = pp[4 + ocnt % 2]
                    g_wait(Ep, f"pp{4 + ocnt % 2}")
                    for kt in range(PFX[s]):
                        Lp = pp[kt % 2]
                        g_wait(Ep, f"pp{kt % 2}")
                        Ep.wait(ld, t_kgvg[1])
                        Ep.wait(ld, t_mk[1])
                        for r in range(4):
                            t_L = Ep.op(
                                mm(Lp[:, r * 128:r * 128 + 128],
                                   A(O_KG + kt * 128, 128, grp * 64, 64),
                                   A(O_QT + (4 * G + r) * 1024 + s * 128, 128,
                                     grp * 64, 64)),
                                inc=(pe if r == 3 else None))
                        ev_exp_pre = t_L
                        if kt >= PFX[s] - 4:
                            j = kt - (PFX[s] - 4)
                            Ev.waitev(t_L)
                            ev_exp_pre = Ev.op(
                                tt(Lp[:, 0:512], Lp[:, 0:512],
                                   Ab(O_MASK + (si * 4 + j) * 256, 512),
                                   ALU.add), inc=vec)
                        g_wait(Ea, f"pbuf{kt % 2}")
                        Ea.waitev(ev_exp_pre)
                        t_exp = Ea.op(
                            actv(A(O_PB + (kt % 2) * 512, 512), Lp[:, 0:512],
                                 AF.Exp,
                                 bias=A(O_BIAS + (G * 2 + grp) * 8 + s, 1),
                                 scale=0.125), inc=act)
                        g_set(f"pp{kt % 2}", t_exp)
                        Ep.waitev(t_exp)
                        for r in range(4):
                            t_av = Ep.op(
                                mm(po[0:65, r * 128:r * 128 + 128],
                                   A(O_VG + kt * 130 + grp * 65, 65),
                                   A(O_PB + (kt % 2) * 512 + r * 128, 128),
                                   start=(kt == 0 and r == 0),
                                   stop=(kt == PFX[s] - 1)),
                                inc=(pe if r == 3 else None))
                        g_set(f"pbuf{kt % 2}", t_av)
                    g_wait(Ev, "oev")
                    Ev.waitev(t_av)
                    t_oe = Ev.op(cp(A(O_EV, 512, 0, 65), po[0:65, 0:512]),
                                 inc=vec)
                    g_set(f"pp{4 + ocnt % 2}", t_oe)
                    for r in range(4):
                        h = grp * 4 + r
                        j = 4 * G + r
                        dst_cols = (O_OSB + j * 512 + si * 128, 128)
                        if grp == 0:
                            t_op = Ev.op(cp(A(dst_cols[0], 128, 0, 64),
                                            A(O_EV + r * 128, 128, 0, 64)),
                                         inc=vec)
                        else:
                            Ea.waitev(t_oe)
                            t_op = Ea.dma(A(dst_cols[0], 128, 64, 64),
                                          A(O_EV + r * 128, 128, 0, 64), st)
                        g_wait(Ep, "pp6")
                        Ep.waitev(t_oe)
                        t_tr = Ep.op(trn(pp[6][:, r * 128:r * 128 + 128],
                                         A(O_EV + r * 128, 128)),
                                     inc=(pe if r == 3 else None))
                    Ev.waitev(t_tr)
                    t_dens = None
                    for r in range(4):
                        h = 8 * G + grp * 4 + r
                        t_dens = Ev.op(cp(A(O_DEN + si * 32 + h, 1),
                                          pp[6][:, r * 128 + 64:r * 128 + 65]),
                                       inc=(vec if r == 3 else None))
                    g_set("pp6", t_dens)
                    g_set("oev", t_dens, t_op)
                    ocnt += 1
            g_set("kgvg", (pe, pe.n))
        for si in range(4):
            Ev.op(lambda e: e.memset(A(O_CT + 32, 96), 0.0))
            vsep()
            t_rc = Ev.op(recip(A(O_CT, 32), A(O_DEN + si * 32, 32)), inc=vec)
            vsep()
            g_wait(Ep, "pp6")
            Ep.waitev(t_rc)
            t_tr = Ep.op(trn(pp[6][:, 0:128], A(O_CT, 128)), inc=pe)
            Ev.waitev(t_tr)
            t_dT = Ev.op(cp(A(O_CT + 128, 128, 0, 32), pp[6][0:32, 0:128]),
                         inc=vec)
            g_set("pp6", t_dT)
            for j in range(16):
                g_wait(Ep, "pp7")
                Ep.waitev(t_dT)
                t_rp = Ep.op(mm(pp[7][:, 0:128], A(O_INDR + j * 128, 128, 0, 32),
                                A(O_CT + 128, 128, 0, 32)), inc=pe)
                Ev.waitev(t_rp)
                t_on = Ev.op(tt(A(O_OSB + j * 512 + si * 128, 128),
                                A(O_OSB + j * 512 + si * 128, 128),
                                pp[7][:, 0:128], ALU.mult), inc=vec)
                g_set("pp7", t_on)
        if DBG_SCRATCH[0] and sg == 0:
            Eg.wait(vec, vec.n)
            Eg.dma(dbgA[:, 64:192], A(O_DEN, 128), st)
            Eg.dma(dbgA[:, 1728:2240], A(O_OSB, 512), st)
            Eg.dma(dbgA[:, 2240:2752], A(O_OSB + 512, 512), st)
        t_y_last = None
        for o in range(16):
            wbuf = A(O_WQS + (o % 2) * 2048, 2048)
            g_wait(Es, f"wos{o % 2}")
            Es.wait(vec, vec.n)
            t_w = Es.dma(wbuf, wot[o], ld)
            ppj = pp[o % 2]
            g_wait(Ep, f"pp{o % 2}")
            Ep.wait(ld, t_w[1])
            for k in range(16):
                t_pe = Ep.op(mm(ppj[:, 0:512], wbuf[:, k * 128:k * 128 + 128],
                                A(O_OSB + k * 512, 512),
                                start=(k == 0), stop=(k == 15)),
                             inc=(pe if k == 15 else None))
            g_set(f"wos{o % 2}", t_pe)
            xb = A(O_PB + (o % 2) * 512, 512)
            g_wait(Es, f"xres{o % 2}")
            t_xr = Es.dma(xb, xTm[o][:, sg * 512:sg * 512 + 512], ld)
            Ev.waitev(t_pe)
            Ev.wait(ld, t_xr[1])
            g_wait(Ev, "dbgtap")
            t_v = Ev.op(tt(A(O_YSB + o * 512, 512), ppj[:, 0:512], xb, ALU.add),
                        inc=vec)
            g_set(f"pp{o % 2}", t_v)
            g_set(f"xres{o % 2}", t_v)
            Ea.waitev(t_v)
            g_wait(Ea, "EVsq")
            t_sq = Ea.op(actv(A(O_EV, 512), A(O_YSB + o * 512, 512), AF.Square),
                         inc=act)
            if o == 0:
                g_wait(Ep, "pp2")
                g_wait(Ep, "pp3")
            Ep.waitev(t_v)
            Ep.op(mm(pp[2][0:1, 0:512], ONESC, A(O_YSB + o * 512, 512),
                     start=(o == 0), stop=(o == 15)))
            Ep.waitev(t_sq)
            t_st2 = Ep.op(mm(pp[3][0:1, 0:512], ONESC, A(O_EV, 512),
                             start=(o == 0), stop=(o == 15)), inc=pe)
            g_set("EVsq", t_st2)
            t_y_last = t_st2
        Ev.waitev(t_y_last)
        Ev.op(ts(A(O_ROWA, 512, 0, 1), pp[2][0:1, 0:512], INV_D, ALU.mult))
        Ev.op(ts(A(O_ROWB, 512, 0, 1), pp[3][0:1, 0:512], INV_D, ALU.mult))
        Ev.op(tt(A(O_ROWB + 512, 512, 0, 1), A(O_ROWA, 512, 0, 1),
                 A(O_ROWA, 512, 0, 1), ALU.mult))
        Ev.op(tt(A(O_ROWB, 512, 0, 1), A(O_ROWB, 512, 0, 1),
                 A(O_ROWB + 512, 512, 0, 1), ALU.subtract))
        t_var = Ev.op(ts(A(O_ROWB, 512, 0, 1), A(O_ROWB, 512, 0, 1), 1e-5,
                         ALU.add), inc=vec)
        Ea.waitev(t_var)
        t_sd = Ea.op(actv(A(O_ROWA + 512, 512, 0, 1), A(O_ROWB, 512, 0, 1),
                          AF.Sqrt), inc=act)
        Ev.waitev(t_sd)
        Ev.op(recip(A(O_ROWA + 512, 512, 0, 1), A(O_ROWA + 512, 512, 0, 1)))
        Ev.op(tt(A(O_ROWB + 512, 512, 0, 1), A(O_ROWA + 512, 512, 0, 1),
                 A(O_ROWA + 512, 512, 0, 1), ALU.mult))
        Ev.op(tt(A(O_ROWB + 512, 512, 0, 1), A(O_ROWB + 512, 512, 0, 1),
                 A(O_ROWB, 512, 0, 1), ALU.mult))
        Ev.op(ts(A(O_ROWB + 512, 512, 0, 1), A(O_ROWB + 512, 512, 0, 1),
                 -0.5, ALU.mult, 1.5, ALU.add))
        t_rst = Ev.op(tt(A(O_ROWA + 512, 512, 0, 1), A(O_ROWA + 512, 512, 0, 1),
                         A(O_ROWB + 512, 512, 0, 1), ALU.mult), inc=vec)
        g_wait(Ep, "pp4")
        g_wait(Ep, "pp5")
        Ep.waitev(t_rst)
        t_mr = Ep.op(mm(pp[4][:, 0:512], ONES1, A(O_ROWA, 512, 0, 1)), inc=pe)
        t_rr = Ep.op(mm(pp[5][:, 0:512], ONES1, A(O_ROWA + 512, 512, 0, 1)),
                     inc=pe)
        t_x1 = None
        for o in range(16):
            Ev.waitev(t_rr)
            Ev.op(tt(A(O_YSB + o * 512, 512), A(O_YSB + o * 512, 512),
                     pp[4][:, 0:512], ALU.subtract))
            t_x1 = Ev.op(tt(A(O_YSB + o * 512, 512), A(O_YSB + o * 512, 512),
                            pp[5][:, 0:512], ALU.mult), inc=vec)
            Ea.waitev(t_x1)
            Ea.dma(x1_sc[o][:, sg * 512:sg * 512 + 512], A(O_YSB + o * 512, 512),
                   st)
        g_set("pp4", t_x1)
        g_set("pp5", t_x1)
        Ev.wait(st, st.n)
        for cch in range(16):
            ppj = pp[cch % 2]
            g_wait(Ep, f"pp{cch % 2}")
            Ep.waitev(t_x1)
            t_hh = Ep.op(mm(ppj[:, 0:512], H128ap, A(O_YSB + cch * 512, 512)),
                         inc=pe)
            Ev.waitev(t_hh)
            t_hc = Ev.op(cp(A(O_YSB + cch * 512, 512), ppj[:, 0:512]), inc=vec)
            g_set(f"pp{cch % 2}", t_hc)
        for stg in range(4):
            step = 1 << stg
            for cch in range(16):
                if cch & step:
                    continue
                a = A(O_YSB + cch * 512, 512)
                b = A(O_YSB + (cch + step) * 512, 512)
                Ev.op(tt(a, a, b, ALU.add))
                t_bf = Ev.op(stt(b, b, -2.0, a, ALU.mult, ALU.add), inc=vec)
        t_am = Ev.op(red(A(O_EV, 512), A(O_YSB, 8192).rearrange(
            "p (c t) -> p t c", c=16), ALU.max, absval=True), inc=vec)
        for j in range(4):
            g_wait(Ep, "pp6")
            Ep.waitev(t_am)
            t_tr = Ep.op(trn(pp[6][:, 0:128], A(O_EV + j * 128, 128)), inc=pe)
            Ev.waitev(t_tr)
            t_ac = Ev.op(red(A(O_CT + j, 1), pp[6][:, 0:128], ALU.max),
                         inc=vec)
            g_set("pp6", t_ac)
        vsep()
        Ev.op(ts(A(O_CT + 4, 4), A(O_CT, 4), 1e-5, ALU.max,
                 1.0 / 127.0, ALU.mult))
        vsep()
        t_sc1 = Ev.op(recip(A(O_CT + 8, 4), A(O_CT + 4, 4)), inc=vec)
        vsep()
        for j in range(4):
            g_wait(Ep, "pp6")
            Ep.waitev(t_sc1)
            t_tr = Ep.op(trn(pp[6][:, 0:128], A(O_CT + 8 + j, 128)), inc=pe)
            Ev.waitev(t_tr)
            t_rw = Ev.op(cp(A(O_ROW1 + sg * 512 + j * 128, 128, 0, 1),
                            pp[6][0:1, 0:128]), inc=vec)
            g_set("pp6", t_rw)
        g_wait(Ep, "pp7")
        Ep.waitev(t_rw)
        t_rep = Ep.op(mm(pp[7][:, 0:512], ONES1,
                         A(O_ROW1 + sg * 512, 512, 0, 1)), inc=pe)
        t_q1 = None
        for cch in range(16):
            Ev.waitev(t_rep)
            Ev.op(tt(A(O_PB, 512), A(O_YSB + cch * 512, 512), pp[7][:, 0:512],
                     ALU.mult))
            Ev.op(ts(A(O_PB + 512, 512), A(O_PB, 512), 128.5, ALU.add))
            for j in range(7, -1, -1):
                pw = float(1 << j)
                Ev.op(ts(A(O_EV, 512), A(O_PB + 512, 512), pw, ALU.is_ge,
                         -pw, ALU.mult))
                Ev.op(tt(A(O_PB + 512, 512), A(O_PB + 512, 512), A(O_EV, 512),
                         ALU.add))
            Ev.op(stt(A(O_EV, 512), A(O_PB + 512, 512), -1.0, A(O_PB, 512),
                      ALU.mult, ALU.add))
            t_q1 = Ev.op(ts(Ab(O_MASK + cch * 256, 512), A(O_EV, 512), 0.5,
                            ALU.add), inc=vec)
            Ea.waitev(t_q1)
            Ea.dma(xq1_sc[cch][:, sg * 512:sg * 512 + 512],
                   Ab(O_MASK + cch * 256, 512), st)
        g_set("pp7", t_q1)
        if DBG_SCRATCH[0] and sg == 0:
            Eg.wait(vec, t_q1[1])
            t_tap = Eg.dma(dbgA[:, 3776:3792], A(O_CT, 16), st)
            t_tap = Eg.dma(dbgA[0:1, 3264:3776], A(O_ROW1, 512, 0, 1), st)
            t_tap = Eg.dma(dbgA[:, 2752:3264], A(O_YSB, 512), st)
            g_set("dbgtap", t_tap)
        g_set("maskarea", (st, st.n), t_q1)
    t_d_done = st.n

    pump_pass1(2 * len(p1_loads))
    barrier()
    ev_r = Ev.op(red(A(O_CT, 3),
                     A(O_GACC, 768).rearrange("p (a b) -> p a b", a=3),
                     ALU.add), inc=vec)
    vsep()
    g_wait(Ep, "pp6")
    Ep.waitev(ev_r)
    t_cmb = Ep.op(mm(pp[6][0:1, 0:3], ONESC, A(O_CT, 3)), inc=pe)
    Ev.waitev(t_cmb)
    t_mn = Ev.op(ts(A(O_SVAL, 3, 0, 1), pp[6][0:1, 0:3], 1.0 / (II * D),
                    ALU.mult, 1e-5, ALU.max), inc=vec)
    Ep.waitev(t_mn)
    t_bc = Ep.op(mm(pp[6][:, 0:3], ONES1, A(O_SVAL, 3, 0, 1)), inc=pe)
    Ev.waitev(t_bc)
    Ev.op(ts(A(O_SG, 3), pp[6][:, 0:3], 0.5, ALU.mult))
    t_sg = Ev.op(ts(A(O_SG + 3, 3), pp[6][:, 0:3], -0.5, ALU.mult), inc=vec)
    g_set("pp6", t_sg)
    Ev.op(recip(A(O_ROWA, 1024, 0, 1), A(O_ROW1, 1024, 0, 1)))
    Ev.op(ts(A(O_ROWB, 1024, 0, 1), A(O_ROWA, 1024, 0, 1),
             A(O_SVAL + 1, 1, 0, 1), ALU.mult, INV_SQ1, ALU.mult))
    t_csg = Ev.op(ts(A(O_ROWA, 1024, 0, 1), A(O_ROWA, 1024, 0, 1),
                     A(O_SVAL, 1, 0, 1), ALU.mult, INV_SQ1, ALU.mult), inc=vec)

    if DBG_SCRATCH[0]:
        Eg.wait(vec, vec.n)
        Eg.wait(act, act.n)
        Eg.dma(dbgA[:, 896:1664], A(O_GACC, 768), st)
        Eg.dma(dbgA[:, 1664:1680], A(O_CT, 16), st)
        Eg.dma(dbgA[0:1, 3776:3780], A(O_SVAL, 4, 0, 1), st)
        Eg.dma(dbgA[:, 3780:3788], A(O_SG, 8), st)
        Eg.dma(dbgA[0:1, 3788:3800], A(O_ROWA, 12, 0, 1), st)
        Eg.dma(dbgA[0:1, 3800:3812], A(O_ROWB, 12, 0, 1), st)
        Eg.dma(dbgA[0:1, 3812:3824], A(O_ROW1, 12, 0, 1), st)
    barrier()
    for ha in range(2):
        barrier()
        for k in range(16):
            t_xq = Es.dma(Ab(O_XQ1 + k * 256, 512),
                          xq1_sc[k][:, ha * 512:ha * 512 + 512], ld)
        g_wait(Ep, "pp4")
        Ep.waitev(t_csg)
        t_rg = Ep.op(mm(pp[4][:, 0:512], ONES1,
                        A(O_ROWA + ha * 512, 512, 0, 1)), inc=pe)
        Ev.waitev(t_rg)
        t_rgc = Ev.op(cp(A(O_FS1, 512), pp[4][:, 0:512]), inc=vec)
        g_set("pp4", t_rgc)
        for ic in range(64):
            wg = A(O_WG, 2048)
            g_wait(Es, "wg")
            t_wg = Es.dma(wg, wgt[ic], ld)
            g_wait(Es, "wu")
            t_wu = Es.dma(A(O_WU, 2048), wut[ic], ld)
            for mat in range(2):
                src = wg if mat == 0 else A(O_WU, 2048)
                toff = O_WG if mat == 0 else O_WU
                sgc = 0 if mat == 0 else 1
                Ev.wait(ld, (t_wg if mat == 0 else t_wu)[1])
                Ev.waitev(t_sg)
                for q in range(4):
                    Ev.op(ts(pp[6][:, 0:512], src[:, q * 512:q * 512 + 512],
                             A(O_SG + sgc, 1), ALU.is_gt))
                    Ev.op(ts(A(O_FS3, 512), src[:, q * 512:q * 512 + 512],
                             A(O_SG + 3 + sgc, 1), ALU.is_ge))
                    t_tn = Ev.op(stt(Ab(toff + q * 256, 512), pp[6][:, 0:512],
                                     -1.0, A(O_FS3, 512), ALU.add, ALU.add),
                                 inc=vec)
                g_set(f"tern{mat}", t_tn)
            t_tg = guard["tern0"][0]
            t_tu = guard["tern1"][0]
            ppg = pp[ic % 2]
            ppu = pp[2 + (ic % 2)]
            g_wait(Ep, f"ppg{ic % 2}")
            g_wait(Ep, f"ppu{ic % 2}")
            Ep.waitev(t_tg)
            Ep.waitev(t_tu)
            Ep.wait(ld, t_xq[1])
            for k in range(16):
                Ep.op(mm(ppg[:, 0:512],
                         Ab(O_WG + k * 64, 128),
                         Ab(O_XQ1 + k * 256, 512),
                         start=(k == 0), stop=(k == 15)),
                      inc=(pe if k == 15 else None))
            t_gd = (pe, pe.n)
            for k in range(16):
                Ep.op(mm(ppu[:, 0:512], Ab(O_WU + k * 64, 128),
                         Ab(O_XQ1 + k * 256, 512),
                         start=(k == 0), stop=(k == 15)),
                      inc=(pe if k == 15 else None))
            t_ud = (pe, pe.n)
            g_set("wg", t_gd)
            g_set("wu", t_ud)
            Ev.waitev(t_gd)
            Ev.waitev(t_rgc)
            t_gs = Ev.op(tt(A(O_SH + ic * 512, 512), ppg[:, 0:512],
                            A(O_FS1, 512), ALU.mult), inc=vec)
            g_set(f"ppg{ic % 2}", t_gs)
            Ea.waitev(t_gs)
            g_wait(Ea, "FS2")
            t_si = Ea.op(actv(A(O_FS2, 512), A(O_SH + ic * 512, 512), AF.Silu),
                         inc=act)
            Ev.waitev(t_si)
            Ev.waitev(t_ud)
            t_h = Ev.op(tt(A(O_SH + ic * 512, 512), A(O_FS2, 512),
                           ppu[:, 0:512], ALU.mult), inc=vec)
            g_set(f"ppu{ic % 2}", t_h)
            g_set("FS2", t_h)
        if DBG_SCRATCH[0] and ha == 0:
            Eg.wait(vec, vec.n)
            t_htap = Eg.dma(dbgA[:, 3264:3776], A(O_SH, 512), st)
            g_set("htap", t_htap)
        Ev.wait(pe, pe.n)
        g_wait(Ev, "htap")
        for cch in range(64):
            ppj = pp[4 + cch % 2]
            g_wait(Ep, f"pp{4 + cch % 2}")
            Ep.wait(vec, vec.n)
            t_hh = Ep.op(mm(ppj[:, 0:512], H128ap, A(O_SH + cch * 512, 512)),
                         inc=pe)
            Ev.waitev(t_hh)
            t_hc = Ev.op(cp(A(O_SH + cch * 512, 512), ppj[:, 0:512]), inc=vec)
            g_set(f"pp{4 + cch % 2}", t_hc)
        for stg in range(6):
            step = 1 << stg
            for cch in range(64):
                if cch & step:
                    continue
                a = A(O_SH + cch * 512, 512)
                b = A(O_SH + (cch + step) * 512, 512)
                Ev.op(tt(a, a, b, ALU.add))
                t_bf = Ev.op(stt(b, b, -2.0, a, ALU.mult, ALU.add), inc=vec)
        Ev.wait(pe, pe.n)
        t_am = Ev.op(red(A(O_STGV, 512), A(O_SH, 32768).rearrange(
            "p (c t) -> p t c", c=64), ALU.max, absval=True), inc=vec)
        for j in range(4):
            g_wait(Ep, "pp6")
            Ep.waitev(t_am)
            t_tr = Ep.op(trn(pp[6][:, 0:128], A(O_STGV + j * 128, 128)), inc=pe)
            Ev.waitev(t_tr)
            t_ac = Ev.op(red(A(O_CT + j, 1), pp[6][:, 0:128], ALU.max), inc=vec)
            g_set("pp6", t_ac)
        vsep()
        Ev.op(ts(A(O_CT + 4, 4), A(O_CT, 4), 1e-5, ALU.max,
                 1.0 / 127.0, ALU.mult))
        vsep()
        t_sc2 = Ev.op(recip(A(O_CT + 8, 4), A(O_CT + 4, 4)), inc=vec)
        vsep()
        for j in range(4):
            g_wait(Ep, "pp6")
            Ep.waitev(t_sc2)
            t_tr = Ep.op(trn(pp[6][:, 0:128], A(O_CT + 8 + j, 128)), inc=pe)
            Ev.waitev(t_tr)
            t_rw = Ev.op(cp(A(O_ROW2 + ha * 512 + j * 128, 128, 0, 1),
                            pp[6][0:1, 0:128]), inc=vec)
            g_set("pp6", t_rw)
        g_wait(Ep, "pp7")
        Ep.waitev(t_rw)
        t_rep = Ep.op(mm(pp[7][:, 0:512], ONES1,
                         A(O_ROW2 + ha * 512, 512, 0, 1)), inc=pe)
        Ev.waitev(t_rep)
        t_repc = Ev.op(cp(A(O_FS1, 512), pp[7][:, 0:512]), inc=vec)
        g_set("pp7", t_repc)
        for cch in range(64):
            Ev.waitev(t_repc)
            Ev.op(tt(A(O_STGV, 512), A(O_SH + cch * 512, 512), A(O_FS1, 512),
                     ALU.mult))
            Ev.op(ts(A(O_STGU, 512), A(O_STGV, 512), 128.5, ALU.add))
            for j in range(7, -1, -1):
                pw = float(1 << j)
                Ev.op(ts(A(O_STGM, 512), A(O_STGU, 512), pw, ALU.is_ge,
                         -pw, ALU.mult))
                Ev.op(tt(A(O_STGU, 512), A(O_STGU, 512), A(O_STGM, 512),
                         ALU.add))
            Ev.op(stt(A(O_STGM, 512), A(O_STGU, 512), -1.0, A(O_STGV, 512),
                      ALU.mult, ALU.add))
            g_wait(Ev, f"stgq{cch % 2}")
            t_q2 = Ev.op(ts(Ab(O_STGQ + (cch % 2) * 256, 512), A(O_STGM, 512),
                            0.5, ALU.add), inc=vec)
            Ea.waitev(t_q2)
            t_q2s = Ea.dma(xq2_sc[cch][:, ha * 512:ha * 512 + 512],
                           Ab(O_STGQ + (cch % 2) * 256, 512), st)
            g_set(f"stgq{cch % 2}", (st, t_q2s[1]))
    t_f2_done = st.n

    barrier()
    Ev.op(recip(A(O_ROWA, 1024, 0, 1), A(O_ROW2, 1024, 0, 1)))
    Ev.op(ts(A(O_ROWA, 1024, 0, 1), A(O_ROWA, 1024, 0, 1),
             A(O_SVAL + 2, 1, 0, 1), ALU.mult, INV_SQ2, ALU.mult))
    t_cd = Ev.op(tt(A(O_ROWA, 1024, 0, 1), A(O_ROWA, 1024, 0, 1),
                    A(O_ROWB, 1024, 0, 1), ALU.mult), inc=vec)

    for oblk in range(4):
        for kg in range(16):
            wd = A(O_WDS + (kg % 2) * 2048, 2048)
            g_wait(Es, f"wds{kg % 2}")
            t_wd = None
            for oo in range(4):
                t_wd = Es.dma(wd[:, oo * 512:oo * 512 + 512],
                              wdt[oblk * 4 + oo][:, kg * 512:kg * 512 + 512], ld)
            Ev.wait(ld, t_wd[1])
            Ev.waitev(t_sg)
            for oo in range(4):
                g_wait(Ev, "lnst")
                Ev.op(ts(A(O_LNS, 512), wd[:, oo * 512:oo * 512 + 512],
                         A(O_SG + 2, 1), ALU.is_gt))
                Ev.op(ts(A(O_LNS + 512, 512), wd[:, oo * 512:oo * 512 + 512],
                         A(O_SG + 5, 1), ALU.is_ge))
                t_tn = Ev.op(stt(Ab(O_WDS + (kg % 2) * 2048 + oo * 512, 512),
                                 A(O_LNS, 512), -1.0, A(O_LNS + 512, 512),
                                 ALU.add, ALU.add), inc=vec)
            g_set(f"terd{kg % 2}", t_tn)
            t_mmlast = None
            for kk in range(4):
                xb = Ab(O_XQ2B + (kk % 2) * 512, 1024)
                g_wait(Es, f"xq2b{kk % 2}")
                t_xq2 = Es.dma(xb, xq2_sc[kg * 4 + kk], ld)
                Ep.wait(ld, t_xq2[1])
                Ep.waitev(t_tn)
                if kg == 0 and kk == 0:
                    for jj in range(8):
                        g_wait(Ep, f"pp{jj}")
                for oo in range(4):
                    for half in range(2):
                        t_mmlast = Ep.op(
                            mm(pp[oo * 2 + half][:, 0:512],
                               Ab(O_WDS + (kg % 2) * 2048 + oo * 512
                                  + kk * 64, 128),
                               xb[:, half * 512:half * 512 + 512],
                               start=(kg == 0 and kk == 0),
                               stop=(kg == 15 and kk == 3)),
                            inc=(pe if (oo == 3 and half == 1) else None))
                g_set(f"xq2b{kk % 2}", t_mmlast)
            g_set(f"wds{kg % 2}", t_mmlast)
        for oo in range(4):
            o = oblk * 4 + oo
            Ev.waitev(t_mmlast)
            for half in range(2):
                t_dn = Ev.op(cp(A(O_DNS + o * 1024 + half * 512, 512),
                                pp[oo * 2 + half][:, 0:512]), inc=vec)
        for jj in range(8):
            g_set(f"pp{jj}", t_dn)
    g_wait(Ep, "pp4")
    Ep.waitev(t_cd)
    Ep.wait(vec, vec.n)
    t_cr = Ep.op(mm(pp[4][:, 0:512], ONES1, A(O_ROWA, 512, 0, 1)), inc=pe)
    t_cr2 = Ep.op(mm(pp[5][:, 0:512], ONES1, A(O_ROWA + 512, 512, 0, 1)),
                  inc=pe)
    t_stats = None
    for o in range(16):
        xb = A(O_X1B + (o % 2) * 1024, 1024)
        g_wait(Es, f"x1b{o % 2}")
        t_x1r = Es.dma(xb, x1_sc[o], ld)
        Ev.waitev(t_cr2)
        Ev.wait(ld, t_x1r[1])
        for half in range(2):
            dn = A(O_DNS + o * 1024 + half * 512, 512)
            Ev.op(tt(dn, dn, pp[4 + half][:, 0:512], ALU.mult))
            t_v2 = Ev.op(tt(dn, dn, xb[:, half * 512:half * 512 + 512],
                            ALU.add), inc=vec)
            g_set(f"x1b{o % 2}", t_v2)
            Ea.waitev(t_v2)
            g_wait(Ea, "lnsq")
            t_sq = Ea.op(actv(A(O_LNS, 512), dn, AF.Square), inc=act)
            if o == 0 and half == 0:
                g_wait(Ep, "pp2")
                g_wait(Ep, "pp3")
                g_wait(Ep, "pp6")
                g_wait(Ep, "pp7")
            Ep.waitev(t_v2)
            Ep.op(mm(pp[2 + half][0:1, 0:512], ONESC, dn,
                     start=(o == 0), stop=(o == 15)))
            Ep.waitev(t_sq)
            t_stats = Ep.op(mm(pp[6 + half][0:1, 0:512], ONESC, A(O_LNS, 512),
                               start=(o == 0), stop=(o == 15)), inc=pe)
            g_set("lnsq", t_stats)
    Ev.waitev(t_stats)
    for half in range(2):
        ro = half * 512
        Ev.op(ts(A(O_ROWA + ro, 512, 0, 1), pp[2 + half][0:1, 0:512], INV_D,
                 ALU.mult))
        Ev.op(ts(A(O_ROWB + ro, 512, 0, 1), pp[6 + half][0:1, 0:512], INV_D,
                 ALU.mult))
        Ev.op(tt(A(O_ROW1 + ro, 512, 0, 1), A(O_ROWA + ro, 512, 0, 1),
                 A(O_ROWA + ro, 512, 0, 1), ALU.mult))
        Ev.op(tt(A(O_ROWB + ro, 512, 0, 1), A(O_ROWB + ro, 512, 0, 1),
                 A(O_ROW1 + ro, 512, 0, 1), ALU.subtract))
        t_var = Ev.op(ts(A(O_ROWB + ro, 512, 0, 1), A(O_ROWB + ro, 512, 0, 1),
                         1e-5, ALU.add), inc=vec)
        Ea.waitev(t_var)
        t_sd = Ea.op(actv(A(O_ROW2 + ro, 512, 0, 1), A(O_ROWB + ro, 512, 0, 1),
                          AF.Sqrt), inc=act)
        Ev.waitev(t_sd)
        Ev.op(recip(A(O_ROW2 + ro, 512, 0, 1), A(O_ROW2 + ro, 512, 0, 1)))
        Ev.op(tt(A(O_ROW1 + ro, 512, 0, 1), A(O_ROW2 + ro, 512, 0, 1),
                 A(O_ROW2 + ro, 512, 0, 1), ALU.mult))
        Ev.op(tt(A(O_ROW1 + ro, 512, 0, 1), A(O_ROW1 + ro, 512, 0, 1),
                 A(O_ROWB + ro, 512, 0, 1), ALU.mult))
        Ev.op(ts(A(O_ROW1 + ro, 512, 0, 1), A(O_ROW1 + ro, 512, 0, 1),
                 -0.5, ALU.mult, 1.5, ALU.add))
        t_rst = Ev.op(tt(A(O_ROW2 + ro, 512, 0, 1), A(O_ROW2 + ro, 512, 0, 1),
                         A(O_ROW1 + ro, 512, 0, 1), ALU.mult), inc=vec)
    g_set("pp2", t_rst)
    g_set("pp3", t_rst)
    g_set("pp6", t_rst)
    g_set("pp7", t_rst)
    g_wait(Ep, "pp2")
    Ep.waitev(t_rst)
    t_mr0 = Ep.op(mm(pp[2][:, 0:512], ONES1, A(O_ROWA, 512, 0, 1)), inc=pe)
    t_mr1 = Ep.op(mm(pp[3][:, 0:512], ONES1, A(O_ROWA + 512, 512, 0, 1)),
                  inc=pe)
    t_rr0 = Ep.op(mm(pp[6][:, 0:512], ONES1, A(O_ROW2, 512, 0, 1)), inc=pe)
    t_rr1 = Ep.op(mm(pp[7][:, 0:512], ONES1, A(O_ROW2 + 512, 512, 0, 1)),
                  inc=pe)
    t_x2 = None
    for o in range(16):
        Ev.waitev(t_rr1)
        for half in range(2):
            dn = A(O_DNS + o * 1024 + half * 512, 512)
            Ev.op(tt(dn, dn, pp[2 + half][:, 0:512], ALU.subtract))
            t_x2 = Ev.op(tt(dn, dn, pp[6 + half][:, 0:512], ALU.mult), inc=vec)
    barrier()
    t_out = None
    for tts in range(8):
        g_wait(Ev, "outa")
        for o in range(16):
            ppj = pp[o % 2]
            g_wait(Ep, f"ppt{o % 2}")
            Ep.waitev(t_x2)
            t_tr = Ep.op(trn(ppj[:, 0:128],
                             A(O_DNS + o * 1024 + tts * 128, 128)), inc=pe)
            Ev.waitev(t_tr)
            t_oc = Ev.op(cp(A(O_OUTA + o * 128, 128), ppj[:, 0:128]), inc=vec)
            g_set(f"ppt{o % 2}", t_oc)
        Eg.waitev(t_oc)
        t_out = Eg.dma(out[tts * 128:tts * 128 + 128, :], A(O_OUTA, 2048), st)
        g_set("outa", t_out)
    Eg.wait(st, st.n)

    with nc.Block() as block:
        @block.sync
        def _(eng):
            for f in Es.ops:
                f(eng)

        @block.tensor
        def _(eng):
            for f in Ep.ops:
                f(eng)

        @block.vector
        def _(eng):
            for f in Ev.ops:
                f(eng)

        @block.scalar
        def _(eng):
            for f in Ea.ops:
                f(eng)

        @block.gpsimd
        def _(eng):
            for f in Eg.ops:
                f(eng)

    ctx.close()
    return nc


_NC_CACHE = []


def kernel(**inputs):
    x = np.asarray(inputs["L0_attn_hidden_states"], dtype=np.float32)
    Wq = np.asarray(inputs["L0_attn_Wq"], dtype=np.float32)
    Wk = np.asarray(inputs["L0_attn_Wk"], dtype=np.float32)
    Wv = np.asarray(inputs["L0_attn_Wv"], dtype=np.float32)
    Wo = np.asarray(inputs["L0_attn_Wo"], dtype=np.float32)
    wg = np.asarray(inputs["L0_ffn_w_gate"], dtype=np.float32)
    wu = np.asarray(inputs["L0_ffn_w_up"], dtype=np.float32)
    wd = np.asarray(inputs["L0_ffn_w_down"], dtype=np.float32)

    nc = build_nc()
    in_maps, order_all = _stage(x, Wq, Wk, Wv, Wo, wg, wu, wd)
    res = run_bass_kernel_spmd(nc, in_maps, core_ids=list(range(8)))
    y = np.zeros((2, S, D), dtype=np.float32)
    for core in range(8):
        b, cols = order_all[core]
        y[b, cols, :] = np.asarray(res.results[core]["out"])
    return y


def _stage(x, Wq, Wk, Wv, Wo, wg, wu, wd):
    perm = _qperm()
    wqtS = _strips(np.ascontiguousarray(Wq.T[:, perm]))
    wktS = _strips(np.ascontiguousarray(Wk.T))
    wvtS = _strips(np.ascontiguousarray(Wv.T))
    wotS = _strips(np.ascontiguousarray(Wo.T[perm, :]))
    wgtS = _strips(np.ascontiguousarray(wg.T))
    wutS = _strips(np.ascontiguousarray(wu.T))
    wdtS = _strips(np.ascontiguousarray(wd.T))
    cosA, sinA = _rope_tables()
    h128 = _hadamard(128).astype(np.float32)
    ident = np.eye(128, dtype=np.float32)
    p64 = np.zeros((64, 64), dtype=np.float32)
    p64[:32, 32:] = -np.eye(32)
    p64[32:, :32] = np.eye(32)
    p2 = np.zeros((128, 128), dtype=np.float32)
    p2[:64, :64] = p64.T
    p2[64:, 64:] = p64.T
    ones1 = np.ones((1, 128), dtype=np.float32)
    onescol = np.ones((128, 1), dtype=np.float32)
    indq = np.zeros((128, 512), dtype=np.float32)
    for j in range(16):
        G, r = j // 4, j % 4
        hA, hB = 8 * G + r, 8 * G + 4 + r
        indq[0:64, j * 32 + hA] = 1.0
        indq[64:128, j * 32 + hB] = 1.0
    indk = np.zeros((128, 32), dtype=np.float32)
    for o in range(4):
        indk[0:64, o * 8 + 2 * o] = 1.0
        indk[64:128, o * 8 + 2 * o + 1] = 1.0
    indkq = np.zeros((8, 32), dtype=np.float32)
    for h in range(32):
        indkq[h // 4, h] = 1.0
    indrep = np.zeros((32, 2048), dtype=np.float32)
    for j in range(16):
        G, r = j // 4, j % 4
        hA, hB = 8 * G + r, 8 * G + 4 + r
        indrep[hA, j * 128 + np.arange(0, 64)] = 1.0
        indrep[hB, j * 128 + np.arange(64, 128)] = 1.0
    indgrp = np.zeros((8, 1024), dtype=np.float32)
    for g8 in range(8):
        indgrp[g8, g8 * 128:(g8 + 1) * 128] = 1.0
    in_maps = []
    order_all = []
    for core in range(8):
        b, c = core // 4, core % 4
        qt = _core_qtiles(c)
        cols = np.concatenate([np.arange(t * 128, (t + 1) * 128) for t in qt])
        order_all.append((b, cols))
        xTb = np.ascontiguousarray(x[b].T)
        xTm = np.ascontiguousarray(xTb[:, cols]).reshape(16, 128, NT)
        xTf = xTb.reshape(16, 128, S)
        in_maps.append({
            "xTm": xTm, "xTf": xTf,
            "wqt": wqtS, "wkt": wktS, "wvt": wvtS, "wot": wotS,
            "wgt": wgtS, "wut": wutS, "wdt": wdtS,
            "cosq": np.ascontiguousarray(cosA[:, cols]),
            "sinq": np.ascontiguousarray(sinA[:, cols]),
            "cosk": cosA, "sink": sinA,
            "masks": _masks_for_core(c),
            "h128": h128, "ident": ident, "p2": p2,
            "ones1": ones1, "onescol": onescol,
            "indq": indq, "indk": indk, "indkq": indkq,
            "indrep": indrep, "indgrp": indgrp,
        })
    return in_maps, order_all


# version 03
# speedup vs baseline: 1.1224x; 1.1224x over previous
from contextlib import ExitStack

import numpy as np
import ml_dtypes

import concourse.bass as bass
import concourse.mybir as mybir
from concourse.bass_utils import run_bass_kernel_spmd

F32 = mybir.dt.float32
BF16 = mybir.dt.bfloat16
AF = mybir.ActivationFunctionType
ALU = mybir.AluOpType
AX = mybir.AxisListType

S, D, HD, II, NT, NSLOT = 4096, 2048, 64, 8192, 1024, 8
PFX = [4, 8, 12, 16, 20, 24, 28, 32]
MASK_NEG = -1.0e30
INV_D = 1.0 / 2048.0
INV_SQ1 = float(2048.0 ** -0.5)
INV_SQ2 = float(8192.0 ** -0.5)
ARENA = 52224

O_H128, O_ID, O_P2, O_ONES1, O_ONESC = 0, 128, 256, 384, 512
O_SVAL = 516
O_SG = 520
O_KM = 528
O_KN = 536
O_CG = 568
O_IKQ = 576
O_INDQ = 640
O_INDR = 1152
O_INDG = 3200
O_BIAS = 4224
O_CT = 4288
O_ROW1 = 4544
O_ROW2 = 5568
O_ROWA = 6592
O_ROWB = 7616
O_GACC = 8640
O_FS1 = 9472
O_FS2 = 9984
O_FS3 = 10496
O_KVW = 11008
O_XS = 27500
O_CK, O_SK, O_VSTG = 35800, 36312, 36824
O_KA, O_KB, O_KC, O_KNS = 37400, 37912, 38424, 38936
O_QT = 11008
O_XMH = 27500
O_WQS = 35800
O_CQ, O_SQ = 39900, 40924
O_QN = 41950
O_SA, O_SB, O_SC = 42980, 43492, 44004
O_OSB = 27500
O_MASK = 35800
O_KG = 39900
O_VG = 44000
O_YSB = 39900
O_PB = 48200
O_EV = 49224
O_DEN = 49740
O_P1 = 50176
O_P1D = 51200
O_XQ1 = 11008
O_WG, O_WU = 15104, 17152
O_WU1 = 3200
O_SH = 19200
O_STGV, O_STGU, O_STGM = 13056, 13568, 14080
O_STGQ = 14592
O_DNS = 11008
O_WDS = 27500
O_XQ2B = 31700
O_X1B = 32800
O_OUTA = 34900
O_LNS = 37000


def _hadamard(n):
    h = np.array([[1.0]])
    while h.shape[0] < n:
        h = np.block([[h, h], [h, -h]])
    return h


def _qperm():
    order = []
    for G in range(4):
        for r in range(4):
            for h in (8 * G + r, 8 * G + 4 + r):
                order.extend(range(h * HD, (h + 1) * HD))
    return np.array(order)


def _core_qtiles(c):
    return [c, 7 - c, 8 + c, 15 - c, 16 + c, 23 - c, 24 + c, 31 - c]


def _strips(aT, pi=128):
    K, M = aT.shape
    t = aT.reshape(K // 128, 128, M // pi, pi)
    return np.ascontiguousarray(
        t.transpose(2, 1, 0, 3).reshape(M // pi, 128, (K // 128) * pi))


def _rope_tables():
    inv = 1.0 / (10000.0 ** (np.arange(0, HD, 2, dtype=np.float64) / HD))
    pos = np.arange(S, dtype=np.float64)
    fr = pos[:, None] * inv[None, :]
    emb = np.concatenate([fr, fr], axis=1)
    cos = np.cos(emb).T.astype(np.float32)
    sin = np.sin(emb).T.astype(np.float32)
    return (np.ascontiguousarray(np.concatenate([cos, cos], 0)),
            np.ascontiguousarray(np.concatenate([sin, sin], 0)))


def _masks_for_core(c):
    qt = _core_qtiles(c)
    m = np.zeros((NSLOT, 4, 128, 512), dtype=np.float32)
    for s in range(NSLOT):
        for j in range(4):
            kt = PFX[s] - 4 + j
            kpos = kt * 128 + np.arange(128)[:, None]
            qpos = qt[s] * 128 + np.arange(128)[None, :]
            m[s, j] = np.tile(np.where(kpos > qpos, MASK_NEG, 0.0), (1, 4))
    return m.astype(ml_dtypes.bfloat16)


class Sem:
    def __init__(self, h):
        self.h = h
        self.n = 0

    def bump(self, k=1):
        self.n += k
        return self.n


class E:
    def __init__(self, name):
        self.name = name
        self.ops = []
        self.seen = {}

    def wait(self, sem, val):
        if val is None or val <= 0:
            return
        if self.seen.get(id(sem), -1) >= val:
            return
        self.seen[id(sem)] = val
        self.ops.append(lambda e, s=sem.h, v=val: e.wait_ge(s, v))

    def waitev(self, ev):
        if ev is not None:
            self.wait(ev[0], ev[1])

    def op(self, fn, inc=None, k=1):
        if inc is None:
            self.ops.append(fn)
            return None
        t = inc.bump(k)
        self.ops.append(lambda e, f=fn, s=inc.h, kk=k: f(e).then_inc(s, kk))
        return (inc, t)

    def dma(self, dst, src, sem):
        return self.op(lambda e, d=dst, s=src: e.dma_start(out=d, in_=s),
                       inc=sem, k=16)


DBG_SCRATCH = [False]


def build_nc():
    nc = bass.Bass("TRN2", target_bir_lowering=False)

    def dinp(name, shape, dt=F32):
        return nc.dram_tensor(name, list(shape), dt, kind="ExternalInput").ap()

    xTm = dinp("xTm", [16, 128, NT])
    xTf = dinp("xTf", [16, 128, S])
    wqt = dinp("wqt", [16, 128, 2048])
    wkt = dinp("wkt", [4, 128, 2048])
    wvt = dinp("wvt", [4, 128, 2048])
    wot = dinp("wot", [16, 128, 2048])
    wgt = dinp("wgt", [64, 128, 2048])
    wut = dinp("wut", [64, 128, 2048])
    wdt = dinp("wdt", [16, 128, 8192])
    cosq = dinp("cosq", [128, NT])
    sinq = dinp("sinq", [128, NT])
    cosk = dinp("cosk", [128, S])
    sink = dinp("sink", [128, S])
    masksI = dinp("masks", [NSLOT, 4, 128, 512], BF16)
    c_h128 = dinp("h128", [128, 128])
    c_ident = dinp("ident", [128, 128])
    c_p2 = dinp("p2", [128, 128])
    c_ones1 = dinp("ones1", [1, 128])
    c_onescol = dinp("onescol", [128, 1])
    c_indq = dinp("indq", [128, 512])
    c_indk = dinp("indk", [128, 32])
    c_indkq = dinp("indkq", [8, 32])
    c_indrep = dinp("indrep", [32, 2048])
    c_indgrp = dinp("indgrp", [8, 1024])

    kd = "ExternalOutput" if DBG_SCRATCH[0] else "Internal"
    kt_sc = nc.dram_tensor("kt_sc", [4, 128, S], F32, kind=kd).ap()
    vt_sc = nc.dram_tensor("vt_sc", [4, 128, 32 * 130], F32, kind=kd).ap()
    x1_sc = nc.dram_tensor("x1_sc", [16, 128, NT], F32, kind=kd).ap()
    xq1_sc = nc.dram_tensor("xq1_sc", [16, 128, NT], BF16, kind=kd).ap()
    xq2_sc = nc.dram_tensor("xq2_sc", [64, 128, NT], BF16, kind=kd).ap()
    out = nc.dram_tensor("out", [NT, D], F32, kind="ExternalOutput").ap()
    dbgA = nc.dram_tensor("dbgA", [128, 4096], F32,
                          kind=("ExternalOutput" if DBG_SCRATCH[0] else "Internal")).ap()

    ctx = ExitStack()
    ar = ctx.enter_context(nc.sbuf_tensor("arena", [128, ARENA], F32))
    s_indk = ctx.enter_context(nc.sbuf_tensor("s_indk", [128, 32], F32))
    arB = ar[:].bitcast(BF16)

    def A(off, cols, p0=0, pn=128):
        return ar[p0:p0 + pn, off:off + cols]

    def Ab(off_f32, cols_bf16, p0=0, pn=128):
        return arB[p0:p0 + pn, 2 * off_f32:2 * off_f32 + cols_bf16]

    pp = [ctx.enter_context(nc.psum_tensor(f"pp{j}", [128, 512], F32))
          for j in range(8)]

    sems = {k: Sem(ctx.enter_context(nc.semaphore(k)))
            for k in ("ld", "st", "pe", "vec", "act", "gp", "gld")}
    ld, st, pe, vec, act, gp, gld = (sems[k] for k in
                                     ("ld", "st", "pe", "vec", "act", "gp", "gld"))

    Es, Ep, Ev, Ea, Eg = E("sync"), E("tensor"), E("vector"), E("scalar"), E("gpsimd")
    ALL_E = (Es, Ep, Ev, Ea, Eg)

    guard = {}

    def g_wait(Eng, key):
        for ev in guard.get(key, ()):
            Eng.waitev(ev)

    def g_set(key, *evs):
        guard[key] = [e for e in evs if e is not None]

    def barrier():
        evs = [(pe, pe.n), (vec, vec.n), (act, act.n), (st, st.n),
               (ld, ld.n), (gp, gp.n)]
        for Eng in ALL_E:
            for ev in evs:
                Eng.waitev(ev)

    def mm(out_ap, lhsT, rhs, start=True, stop=True):
        return lambda e: e.matmul(out_ap, lhsT, rhs, start=start, stop=stop)

    def cp(dst, src):
        return lambda e: e.tensor_copy(dst, src)

    def tt(dst, a, b, op):
        return lambda e: e.tensor_tensor(dst, a, b, op)

    def ts(dst, a, s1, op0, s2=None, op1=None):
        if op1 is None:
            return lambda e: e.tensor_scalar(dst, a, s1, None, op0)
        return lambda e: e.tensor_scalar(dst, a, s1, s2, op0, op1)

    def stt(dst, in0, scalar, in1, op0, op1):
        return lambda e: e.scalar_tensor_tensor(dst, in0, scalar, in1, op0, op1)

    def actv(dst, src, func, bias=0.0, scale=1.0):
        return lambda e: e.activation(out=dst, in_=src, func=func,
                                      bias=bias, scale=scale)

    def red(dst, src, op, absval=False, axis=None):
        ax = AX.X if axis is None else axis
        return lambda e: e.tensor_reduce(dst, src, ax, op,
                                         apply_absolute_value=absval)

    def recip(dst, src):
        return lambda e: e.reciprocal(dst, src)

    def trn(out_ap, in_ap):
        return lambda e: e.transpose(out_ap, in_ap, ar[0:128, O_ID:O_ID + 128])

    def vsep():
        Ev.op(cp(A(O_CT + 254, 1), A(O_ONESC, 1)))

    H128ap = A(O_H128, 128)
    P2ap = A(O_P2, 128)
    ONES1 = A(O_ONES1, 128, 0, 1)
    ONESC = A(O_ONESC, 1)

    for dst, src in [(A(O_H128, 128), c_h128), (A(O_ID, 128), c_ident),
                     (A(O_P2, 128), c_p2), (ONES1, c_ones1), (ONESC, c_onescol),
                     (A(O_INDQ, 512), c_indq), (s_indk[:], c_indk),
                     (A(O_IKQ, 32, 0, 8), c_indkq),
                     (A(O_INDR, 2048, 0, 32), c_indrep),
                     (A(O_INDG, 1024, 0, 8), c_indgrp)]:
        Es.dma(dst, src[:], ld)
    t_const = ld.n
    for r in range(4):
        Eg.op(lambda e, rr=r: e.memset(A(O_VSTG + rr * 130 + 64, 1), 1.0))
        Eg.op(lambda e, rr=r: e.memset(A(O_VSTG + rr * 130 + 129, 1), 1.0))
    Eg.op(lambda e: e.memset(A(O_GACC, 768), 0.0), inc=gp)

    for j, src in enumerate([wkt, wvt]):
        for o in range(4):
            Es.dma(A(O_KVW + (j * 4 + o) * 2048, 2048), src[o], ld)
    t_kvw = ld.n

    def kvw(is_v, o, k):
        return A(O_KVW + ((4 if is_v else 0) + o) * 2048 + k * 128, 128)

    p1_loads = []
    for wsrc, nslab, wcols in ((wgt, 64, 2048), (wut, 64, 2048), (wdt, 16, 8192)):
        for sl in range(nslab):
            for hh in range(wcols // 512):
                p1_loads.append(wsrc[sl][:, hh * 512:hh * 512 + 512])
    p1_state = {"ld": 0, "cons": 0, "consev": {}, "ldev": {}}

    def actv_acc(dst, srcq, func, accum):
        return lambda e: e.activation(out=dst, in_=srcq, func=func,
                                      bias=0.0, scale=1.0, accum_out=accum)

    def pump_pass1(n=1):
        for _ in range(n):
            i = p1_state["ld"]
            if i < len(p1_loads):
                if i >= 2:
                    Eg.waitev(p1_state["consev"].get(i - 2))
                p1_state["ldev"][i] = Eg.dma(A(O_P1 + (i % 2) * 512, 512),
                                             p1_loads[i], gld)
                p1_state["ld"] += 1
            j = p1_state["cons"]
            if j < len(p1_loads) and j < p1_state["ld"]:
                Ea.waitev(p1_state["ldev"][j])
                ev = Ea.op(actv_acc(A(O_P1D, 512), A(O_P1 + (j % 2) * 512, 512),
                                    AF.Abs, A(O_GACC + j, 1)), inc=act)
                p1_state["consev"][j] = ev
                p1_state["cons"] += 1

    def xchunk(k):
        return A(O_XS + (k // 8) * 4096 + (k % 8) * 512, 512)

    Ep.wait(ld, t_kvw)
    last_x_use = None
    for tb in range(8):
        Es.waitev(last_x_use)
        for k in range(16):
            t_x = Es.dma(xchunk(k), xTf[k][:, tb * 512:tb * 512 + 512], ld)
        g_wait(Es, "cksk")
        Es.dma(A(O_CK, 512), cosk[:, tb * 512:tb * 512 + 512], ld)
        t_cs = Es.dma(A(O_SK, 512), sink[:, tb * 512:tb * 512 + 512], ld)
        for o8 in range(8):
            pump_pass1(2)
            is_v = o8 >= 4
            o = o8 % 4
            ppj = pp[o8 % 2]
            g_wait(Ep, f"pp{o8 % 2}")
            Ep.wait(ld, t_x[1])
            for k in range(16):
                t_pe = Ep.op(mm(ppj[:, 0:512], kvw(is_v, o, k), xchunk(k),
                                start=(k == 0), stop=(k == 15)),
                             inc=(pe if k == 15 else None))
            if not is_v:
                g_wait(Ev, "KA")
                Ev.waitev(t_pe)
                t_ev = Ev.op(cp(A(O_KA, 512), ppj[:, 0:512]), inc=vec)
                g_set(f"pp{o8 % 2}", t_ev)
                g_wait(Ep, "pp23")
                Ep.waitev(t_ev)
                t_pr = Ep.op(mm(pp[2][:, 0:512], P2ap, A(O_KA, 512)), inc=pe)
                Ev.wait(ld, t_cs[1])
                Ev.op(tt(A(O_KB, 512), A(O_KA, 512), A(O_CK, 512), ALU.mult))
                Ev.waitev(t_pr)
                Ev.op(tt(A(O_KC, 512), pp[2][:, 0:512], A(O_SK, 512), ALU.mult))
                t_rope = Ev.op(tt(A(O_KA, 512), A(O_KB, 512), A(O_KC, 512),
                                  ALU.add), inc=vec)
                g_set("pp23", t_rope)
                g_set("cksk", t_rope)
                Ea.waitev(t_rope)
                t_sp = Ea.dma(kt_sc[o][:, tb * 512:tb * 512 + 512],
                              A(O_KA, 512), st)
                t_sq = Ea.op(actv(A(O_KNS, 512), A(O_KA, 512), AF.Square),
                             inc=act)
                g_set("KA", (st, t_sp[1]), t_sq)
                g_wait(Ep, "pp4")
                Ep.waitev(t_sq)
                t_kn = Ep.op(mm(pp[4][0:8, 0:512], s_indk[:, o * 8:o * 8 + 8],
                                A(O_KNS, 512)), inc=pe)
                Ev.waitev(t_kn)
                Ev.op(cp(A(O_KB, 512, 0, 8), pp[4][0:8, 0:512]))
                vsep()
                t_knr = Ev.op(red(A(O_KN + tb * 4, 4, 0, 8),
                                  A(O_KB, 512, 0, 8).rearrange(
                                      "p (a b) -> p a b", a=4),
                                  ALU.max), inc=vec)
                g_set("pp4", t_knr)
            else:
                g_wait(Ev, "KB")
                Ev.waitev(t_pe)
                t_ev = Ev.op(cp(A(O_KB, 512), ppj[:, 0:512]), inc=vec)
                g_set(f"pp{o8 % 2}", t_ev)
                g_wait(Ev, "vstg")
                g_wait(Ep, "pp6")
                Ep.waitev(t_ev)
                for r in range(4):
                    t_tr = Ep.op(trn(pp[6][:, r * 128:r * 128 + 128],
                                     A(O_KB + r * 128, 128)),
                                 inc=(pe if r == 3 else None))
                Ev.waitev(t_tr)
                for r in range(4):
                    Ev.op(cp(A(O_VSTG + r * 130, 64),
                             pp[6][:, r * 128:r * 128 + 64]))
                    t_vc = Ev.op(cp(A(O_VSTG + r * 130 + 65, 64),
                                    pp[6][:, r * 128 + 64:r * 128 + 128]),
                                 inc=vec)
                g_set("pp6", t_vc)
                g_set("KB", t_vc)
                Ea.waitev(t_vc)
                t_vs = Ea.dma(vt_sc[o][:, tb * 4 * 130:(tb * 4 + 4) * 130],
                              A(O_VSTG, 520), st)
                g_set("vstg", (st, t_vs[1]))
        last_x_use = (pe, pe.n)
    t_c_spill = st.n

    barrier()
    Es.dma(A(O_CQ, 1024), cosq[:], ld)
    t_cq = Es.dma(A(O_SQ, 1024), sinq[:], ld)

    def QT(j):
        return A(O_QT + j * 1024, 1024)

    t_qn = None
    for half in range(2):
        g_wait(Es, "xmh")
        for k in range(16):
            t_xm = Es.dma(A(O_XMH + k * 512, 512),
                          xTm[k][:, half * 512:half * 512 + 512], ld)
        t_xm_last_use = None
        for o in range(16):
            pump_pass1(4)
            wbuf = A(O_WQS + (o % 2) * 2048, 2048)
            g_wait(Es, f"wqs{o % 2}")
            t_w = Es.dma(wbuf, wqt[o], ld)
            ppj = pp[o % 2]
            g_wait(Ep, f"pp{o % 2}")
            Ep.wait(ld, t_xm[1])
            Ep.wait(ld, t_w[1])
            for k in range(16):
                t_pe = Ep.op(mm(ppj[:, 0:512], wbuf[:, k * 128:k * 128 + 128],
                                A(O_XMH + k * 512, 512),
                                start=(k == 0), stop=(k == 15)),
                             inc=(pe if k == 15 else None))
            g_set(f"wqs{o % 2}", t_pe)
            t_xm_last_use = t_pe
            qdst = QT(o)[:, half * 512:half * 512 + 512]
            Ev.waitev(t_pe)
            t_ev = Ev.op(cp(qdst, ppj[:, 0:512]), inc=vec)
            g_set(f"pp{o % 2}", t_ev)
            g_wait(Ep, "pp23")
            Ep.waitev(t_ev)
            t_pr = Ep.op(mm(pp[2][:, 0:512], P2ap, qdst), inc=pe)
            Ev.wait(ld, t_cq[1])
            Ev.op(tt(A(O_SA, 512), qdst, A(O_CQ + half * 512, 512), ALU.mult))
            Ev.waitev(t_pr)
            Ev.op(tt(A(O_SB, 512), pp[2][:, 0:512],
                     A(O_SQ + half * 512, 512), ALU.mult))
            t_rope = Ev.op(tt(qdst, A(O_SA, 512), A(O_SB, 512), ALU.add),
                           inc=vec)
            g_set("pp23", t_rope)
            g_wait(Ea, "SC")
            Ea.waitev(t_rope)
            t_sq = Ea.op(actv(A(O_SC, 512), qdst, AF.Square), inc=act)
            if o == 0:
                g_wait(Ep, f"pp{4 + half}")
            Ep.waitev(t_sq)
            t_qn = Ep.op(mm(pp[4 + half][0:32, 0:512],
                            A(O_INDQ + o * 32, 32), A(O_SC, 512),
                            start=(o == 0), stop=(o == 15)), inc=pe)
            g_set("SC", t_qn)
        g_set("xmh", t_xm_last_use)
        Ev.waitev(t_qn)
        t_qe = Ev.op(cp(A(O_QN + half * 512, 512, 0, 32),
                        pp[4 + half][0:32, 0:512]), inc=vec)
        g_set(f"pp{4 + half}", t_qe)

    t_qm = Ev.op(red(A(O_CG, 8, 0, 32),
                     A(O_QN, 1024, 0, 32).rearrange("p (a b) -> p a b", a=8),
                     ALU.max), inc=vec)
    Ea.waitev(t_qm)
    t_qs = Ea.op(actv(A(O_CT, 8, 0, 32), A(O_CG, 8, 0, 32), AF.Sqrt), inc=act)
    for s in range(NSLOT):
        tkr = Ev.op(red(A(O_KM + s, 1, 0, 8), A(O_KN, PFX[s], 0, 8), ALU.max),
                    inc=(vec if s == NSLOT - 1 else None))
    Ea.waitev(tkr)
    t_ks = Ea.op(actv(A(O_CT + 8, 8, 0, 8), A(O_KM, 8, 0, 8), AF.Sqrt), inc=act)
    g_wait(Ep, "pp6")
    Ep.waitev(t_ks)
    t_kx = Ep.op(mm(pp[6][0:32, 0:8], A(O_IKQ, 32, 0, 8), A(O_CT + 8, 8, 0, 8)),
                 inc=pe)
    Ev.waitev(t_kx)
    Ev.waitev(t_qs)
    Ev.op(lambda e: e.memset(A(O_CT + 16, 128), 0.0))
    t_c1 = Ev.op(tt(A(O_CT + 16, 8, 0, 32), A(O_CT, 8, 0, 32), pp[6][0:32, 0:8],
                    ALU.mult), inc=vec)
    g_set("pp6", t_c1)
    g_wait(Ep, "pp6")
    Ep.waitev(t_c1)
    t_tr1 = Ep.op(trn(pp[6][:, 0:128], A(O_CT + 16, 128)), inc=pe)
    Ev.waitev(t_tr1)
    Ev.op(lambda e: e.memset(A(O_CT + 16, 128), 0.0))
    t_gm = Ev.op(red(A(O_CT + 16, 8, 0, 8),
                     pp[6][0:8, 0:32].rearrange("p (a b) -> p a b", a=8),
                     ALU.max), inc=vec)
    g_set("pp6", t_gm)
    g_wait(Ep, "pp6")
    Ep.waitev(t_gm)
    t_tr2 = Ep.op(trn(pp[6][:, 0:128], A(O_CT + 16, 128)), inc=pe)
    Ev.waitev(t_tr2)
    t_c2 = Ev.op(ts(A(O_CT + 144, 8, 0, 8), pp[6][0:8, 0:8], -0.125,
                    ALU.mult), inc=vec)
    g_set("pp6", t_c2)
    g_wait(Ep, "pp7")
    Ep.waitev(t_c2)
    for g8 in range(8):
        t_bb = Ep.op(mm(pp[7][:, g8 * 8:g8 * 8 + 8],
                        A(O_INDG + g8 * 128, 128, 0, 8),
                        A(O_CT + 144, 8, 0, 8)),
                     inc=(pe if g8 == 7 else None))
    Ev.waitev(t_bb)
    t_bias = Ev.op(cp(A(O_BIAS, 64), pp[7][:, 0:64]), inc=vec)
    g_set("pp7", t_bias)

    if DBG_SCRATCH[0]:
        Eg.wait(vec, vec.n)
        Eg.wait(pe, pe.n)
        Eg.dma(dbgA[:, 0:64], A(O_BIAS, 64), st)
        Eg.dma(dbgA[0:8, 192:200], A(O_KM, 8, 0, 8), st)
        Eg.dma(dbgA[0:8, 200:232], A(O_KN, 32, 0, 8), st)
        Eg.dma(dbgA[0:32, 232:240], A(O_CG, 8, 0, 32), st)
        Eg.dma(dbgA[:, 704:1728], A(O_QT, 1024), st)
    barrier()
    qt_my = None
    ocnt = 0
    for sg in range(2):
        barrier()
        g_wait(Es, "maskarea")
        for si in range(4):
            for j in range(4):
                t_mk = Es.dma(Ab(O_MASK + (si * 4 + j) * 256, 512),
                              masksI[sg * 4 + si][j], ld)
        if DBG_SCRATCH[0] and sg == 0:
            Es.wait(ld, t_mk[1])
            Es.dma(dbgA[:, 2752:3776], A(O_MASK, 1024), st)
        t_kgvg = None
        for G in range(4):
            g_wait(Es, "kgvg")
            Es.wait(st, t_c_spill)
            Es.dma(A(O_KG, 4096), kt_sc[G], ld)
            t_kgvg = Es.dma(A(O_VG, 4160), vt_sc[G], ld)
            for grp in range(2):
                for si in range(4):
                    pump_pass1(8)
                    s = sg * 4 + si
                    po = pp[4 + ocnt % 2]
                    g_wait(Ep, f"pp{4 + ocnt % 2}")
                    def emit_L(kt, s=s, G=G, grp=grp, si=si,
                               t_kgvg=t_kgvg, t_mk=t_mk):
                        Lp = pp[kt % 2]
                        g_wait(Ep, f"pp{kt % 2}")
                        Ep.wait(ld, t_kgvg[1])
                        Ep.wait(ld, t_mk[1])
                        for r in range(4):
                            t_L = Ep.op(
                                mm(Lp[:, r * 128:r * 128 + 128],
                                   A(O_KG + kt * 128, 128, grp * 64, 64),
                                   A(O_QT + (4 * G + r) * 1024 + s * 128, 128,
                                     grp * 64, 64)),
                                inc=(pe if r == 3 else None))
                        ev = t_L
                        if kt >= PFX[s] - 4:
                            j = kt - (PFX[s] - 4)
                            Ev.waitev(t_L)
                            ev = Ev.op(
                                tt(Lp[:, 0:512], Lp[:, 0:512],
                                   Ab(O_MASK + (si * 4 + j) * 256, 512),
                                   ALU.add), inc=vec)
                        return ev

                    pre = emit_L(0)
                    for kt in range(PFX[s]):
                        nxt_pre = (emit_L(kt + 1) if kt + 1 < PFX[s]
                                   else None)
                        g_wait(Ea, f"pbuf{kt % 2}")
                        Ea.waitev(pre)
                        t_exp = Ea.op(
                            actv(A(O_PB + (kt % 2) * 512, 512),
                                 pp[kt % 2][:, 0:512], AF.Exp,
                                 bias=A(O_BIAS + (G * 2 + grp) * 8 + s, 1),
                                 scale=0.125), inc=act)
                        g_set(f"pp{kt % 2}", t_exp)
                        Ep.waitev(t_exp)
                        for r in range(4):
                            t_av = Ep.op(
                                mm(po[0:65, r * 128:r * 128 + 128],
                                   A(O_VG + kt * 130 + grp * 65, 65),
                                   A(O_PB + (kt % 2) * 512 + r * 128, 128),
                                   start=(kt == 0 and r == 0),
                                   stop=(kt == PFX[s] - 1)),
                                inc=(pe if r == 3 else None))
                        g_set(f"pbuf{kt % 2}", t_av)
                        pre = nxt_pre
                    g_wait(Ev, "oev")
                    Ev.waitev(t_av)
                    t_oe = Ev.op(cp(A(O_EV, 512, 0, 65), po[0:65, 0:512]),
                                 inc=vec)
                    g_set(f"pp{4 + ocnt % 2}", t_oe)
                    for r in range(4):
                        h = grp * 4 + r
                        j = 4 * G + r
                        dst_cols = (O_OSB + j * 512 + si * 128, 128)
                        if grp == 0:
                            t_op = Ev.op(cp(A(dst_cols[0], 128, 0, 64),
                                            A(O_EV + r * 128, 128, 0, 64)),
                                         inc=vec)
                        else:
                            Ea.waitev(t_oe)
                            t_op = Ea.dma(A(dst_cols[0], 128, 64, 64),
                                          A(O_EV + r * 128, 128, 0, 64), st)
                        g_wait(Ep, "pp6")
                        Ep.waitev(t_oe)
                        t_tr = Ep.op(trn(pp[6][:, r * 128:r * 128 + 128],
                                         A(O_EV + r * 128, 128)),
                                     inc=(pe if r == 3 else None))
                    Ev.waitev(t_tr)
                    t_dens = None
                    for r in range(4):
                        h = 8 * G + grp * 4 + r
                        t_dens = Ev.op(cp(A(O_DEN + si * 32 + h, 1),
                                          pp[6][:, r * 128 + 64:r * 128 + 65]),
                                       inc=(vec if r == 3 else None))
                    g_set("pp6", t_dens)
                    g_set("oev", t_dens, t_op)
                    ocnt += 1
            g_set("kgvg", (pe, pe.n))
        for si in range(4):
            Ev.op(lambda e: e.memset(A(O_CT + 32, 96), 0.0))
            vsep()
            t_rc = Ev.op(recip(A(O_CT, 32), A(O_DEN + si * 32, 32)), inc=vec)
            vsep()
            g_wait(Ep, "pp6")
            Ep.waitev(t_rc)
            t_tr = Ep.op(trn(pp[6][:, 0:128], A(O_CT, 128)), inc=pe)
            Ev.waitev(t_tr)
            t_dT = Ev.op(cp(A(O_CT + 128, 128, 0, 32), pp[6][0:32, 0:128]),
                         inc=vec)
            g_set("pp6", t_dT)
            for j in range(16):
                g_wait(Ep, "pp7")
                Ep.waitev(t_dT)
                t_rp = Ep.op(mm(pp[7][:, 0:128], A(O_INDR + j * 128, 128, 0, 32),
                                A(O_CT + 128, 128, 0, 32)), inc=pe)
                Ev.waitev(t_rp)
                t_on = Ev.op(tt(A(O_OSB + j * 512 + si * 128, 128),
                                A(O_OSB + j * 512 + si * 128, 128),
                                pp[7][:, 0:128], ALU.mult), inc=vec)
                g_set("pp7", t_on)
        if DBG_SCRATCH[0] and sg == 0:
            Eg.wait(vec, vec.n)
            Eg.dma(dbgA[:, 64:192], A(O_DEN, 128), st)
            Eg.dma(dbgA[:, 1728:2240], A(O_OSB, 512), st)
            Eg.dma(dbgA[:, 2240:2752], A(O_OSB + 512, 512), st)
        t_y_last = None
        for o in range(16):
            wbuf = A(O_WQS + (o % 2) * 2048, 2048)
            g_wait(Es, f"wos{o % 2}")
            Es.wait(vec, vec.n)
            t_w = Es.dma(wbuf, wot[o], ld)
            ppj = pp[o % 2]
            g_wait(Ep, f"pp{o % 2}")
            Ep.wait(ld, t_w[1])
            for k in range(16):
                t_pe = Ep.op(mm(ppj[:, 0:512], wbuf[:, k * 128:k * 128 + 128],
                                A(O_OSB + k * 512, 512),
                                start=(k == 0), stop=(k == 15)),
                             inc=(pe if k == 15 else None))
            g_set(f"wos{o % 2}", t_pe)
            xb = A(O_PB + (o % 2) * 512, 512)
            g_wait(Es, f"xres{o % 2}")
            t_xr = Es.dma(xb, xTm[o][:, sg * 512:sg * 512 + 512], ld)
            Ev.waitev(t_pe)
            Ev.wait(ld, t_xr[1])
            g_wait(Ev, "dbgtap")
            t_v = Ev.op(tt(A(O_YSB + o * 512, 512), ppj[:, 0:512], xb, ALU.add),
                        inc=vec)
            g_set(f"pp{o % 2}", t_v)
            g_set(f"xres{o % 2}", t_v)
            Ea.waitev(t_v)
            g_wait(Ea, "EVsq")
            t_sq = Ea.op(actv(A(O_EV, 512), A(O_YSB + o * 512, 512), AF.Square),
                         inc=act)
            if o == 0:
                g_wait(Ep, "pp2")
                g_wait(Ep, "pp3")
            Ep.waitev(t_v)
            Ep.op(mm(pp[2][0:1, 0:512], ONESC, A(O_YSB + o * 512, 512),
                     start=(o == 0), stop=(o == 15)))
            Ep.waitev(t_sq)
            t_st2 = Ep.op(mm(pp[3][0:1, 0:512], ONESC, A(O_EV, 512),
                             start=(o == 0), stop=(o == 15)), inc=pe)
            g_set("EVsq", t_st2)
            t_y_last = t_st2
        Ev.waitev(t_y_last)
        Ev.op(ts(A(O_ROWA, 512, 0, 1), pp[2][0:1, 0:512], INV_D, ALU.mult))
        Ev.op(ts(A(O_ROWB, 512, 0, 1), pp[3][0:1, 0:512], INV_D, ALU.mult))
        Ev.op(tt(A(O_ROWB + 512, 512, 0, 1), A(O_ROWA, 512, 0, 1),
                 A(O_ROWA, 512, 0, 1), ALU.mult))
        Ev.op(tt(A(O_ROWB, 512, 0, 1), A(O_ROWB, 512, 0, 1),
                 A(O_ROWB + 512, 512, 0, 1), ALU.subtract))
        t_var = Ev.op(ts(A(O_ROWB, 512, 0, 1), A(O_ROWB, 512, 0, 1), 1e-5,
                         ALU.add), inc=vec)
        Ea.waitev(t_var)
        t_sd = Ea.op(actv(A(O_ROWA + 512, 512, 0, 1), A(O_ROWB, 512, 0, 1),
                          AF.Sqrt), inc=act)
        Ev.waitev(t_sd)
        Ev.op(recip(A(O_ROWA + 512, 512, 0, 1), A(O_ROWA + 512, 512, 0, 1)))
        Ev.op(tt(A(O_ROWB + 512, 512, 0, 1), A(O_ROWA + 512, 512, 0, 1),
                 A(O_ROWA + 512, 512, 0, 1), ALU.mult))
        Ev.op(tt(A(O_ROWB + 512, 512, 0, 1), A(O_ROWB + 512, 512, 0, 1),
                 A(O_ROWB, 512, 0, 1), ALU.mult))
        Ev.op(ts(A(O_ROWB + 512, 512, 0, 1), A(O_ROWB + 512, 512, 0, 1),
                 -0.5, ALU.mult, 1.5, ALU.add))
        t_rst = Ev.op(tt(A(O_ROWA + 512, 512, 0, 1), A(O_ROWA + 512, 512, 0, 1),
                         A(O_ROWB + 512, 512, 0, 1), ALU.mult), inc=vec)
        g_wait(Ep, "pp4")
        g_wait(Ep, "pp5")
        Ep.waitev(t_rst)
        t_mr = Ep.op(mm(pp[4][:, 0:512], ONES1, A(O_ROWA, 512, 0, 1)), inc=pe)
        t_rr = Ep.op(mm(pp[5][:, 0:512], ONES1, A(O_ROWA + 512, 512, 0, 1)),
                     inc=pe)
        t_x1 = None
        for o in range(16):
            Ev.waitev(t_rr)
            Ev.op(tt(A(O_YSB + o * 512, 512), A(O_YSB + o * 512, 512),
                     pp[4][:, 0:512], ALU.subtract))
            t_x1 = Ev.op(tt(A(O_YSB + o * 512, 512), A(O_YSB + o * 512, 512),
                            pp[5][:, 0:512], ALU.mult), inc=vec)
            Ea.waitev(t_x1)
            Ea.dma(x1_sc[o][:, sg * 512:sg * 512 + 512], A(O_YSB + o * 512, 512),
                   st)
        g_set("pp4", t_x1)
        g_set("pp5", t_x1)
        Ev.wait(st, st.n)
        for cch in range(16):
            ppj = pp[cch % 2]
            g_wait(Ep, f"pp{cch % 2}")
            Ep.waitev(t_x1)
            t_hh = Ep.op(mm(ppj[:, 0:512], H128ap, A(O_YSB + cch * 512, 512)),
                         inc=pe)
            Ev.waitev(t_hh)
            t_hc = Ev.op(cp(A(O_YSB + cch * 512, 512), ppj[:, 0:512]), inc=vec)
            g_set(f"pp{cch % 2}", t_hc)
        for stg in range(4):
            step = 1 << stg
            for cch in range(16):
                if cch & step:
                    continue
                a = A(O_YSB + cch * 512, 512)
                b = A(O_YSB + (cch + step) * 512, 512)
                Ev.op(tt(a, a, b, ALU.add))
                t_bf = Ev.op(stt(b, b, -2.0, a, ALU.mult, ALU.add), inc=vec)
        t_am = Ev.op(red(A(O_EV, 512), A(O_YSB, 8192).rearrange(
            "p (c t) -> p t c", c=16), ALU.max, absval=True), inc=vec)
        for j in range(4):
            g_wait(Ep, "pp6")
            Ep.waitev(t_am)
            t_tr = Ep.op(trn(pp[6][:, 0:128], A(O_EV + j * 128, 128)), inc=pe)
            Ev.waitev(t_tr)
            t_ac = Ev.op(red(A(O_CT + j, 1), pp[6][:, 0:128], ALU.max),
                         inc=vec)
            g_set("pp6", t_ac)
        vsep()
        Ev.op(ts(A(O_CT + 4, 4), A(O_CT, 4), 1e-5, ALU.max,
                 1.0 / 127.0, ALU.mult))
        vsep()
        t_sc1 = Ev.op(recip(A(O_CT + 8, 4), A(O_CT + 4, 4)), inc=vec)
        vsep()
        for j in range(4):
            g_wait(Ep, "pp6")
            Ep.waitev(t_sc1)
            t_tr = Ep.op(trn(pp[6][:, 0:128], A(O_CT + 8 + j, 128)), inc=pe)
            Ev.waitev(t_tr)
            t_rw = Ev.op(cp(A(O_ROW1 + sg * 512 + j * 128, 128, 0, 1),
                            pp[6][0:1, 0:128]), inc=vec)
            g_set("pp6", t_rw)
        g_wait(Ep, "pp7")
        Ep.waitev(t_rw)
        t_rep = Ep.op(mm(pp[7][:, 0:512], ONES1,
                         A(O_ROW1 + sg * 512, 512, 0, 1)), inc=pe)
        t_q1 = None
        for cch in range(16):
            Ev.waitev(t_rep)
            Ev.op(tt(A(O_PB, 512), A(O_YSB + cch * 512, 512), pp[7][:, 0:512],
                     ALU.mult))
            Ev.op(ts(A(O_PB + 512, 512), A(O_PB, 512), 128.5, ALU.add))
            for j in range(7, -1, -1):
                pw = float(1 << j)
                Ev.op(ts(A(O_EV, 512), A(O_PB + 512, 512), pw, ALU.is_ge,
                         -pw, ALU.mult))
                Ev.op(tt(A(O_PB + 512, 512), A(O_PB + 512, 512), A(O_EV, 512),
                         ALU.add))
            Ev.op(stt(A(O_EV, 512), A(O_PB + 512, 512), -1.0, A(O_PB, 512),
                      ALU.mult, ALU.add))
            t_q1 = Ev.op(ts(Ab(O_MASK + cch * 256, 512), A(O_EV, 512), 0.5,
                            ALU.add), inc=vec)
            Ea.waitev(t_q1)
            Ea.dma(xq1_sc[cch][:, sg * 512:sg * 512 + 512],
                   Ab(O_MASK + cch * 256, 512), st)
        g_set("pp7", t_q1)
        if DBG_SCRATCH[0] and sg == 0:
            Eg.wait(vec, t_q1[1])
            t_tap = Eg.dma(dbgA[:, 3776:3792], A(O_CT, 16), st)
            t_tap = Eg.dma(dbgA[0:1, 3264:3776], A(O_ROW1, 512, 0, 1), st)
            t_tap = Eg.dma(dbgA[:, 2752:3264], A(O_YSB, 512), st)
            g_set("dbgtap", t_tap)
        g_set("maskarea", (st, st.n), t_q1)
    t_d_done = st.n

    pump_pass1(2 * len(p1_loads))
    barrier()
    ev_r = Ev.op(red(A(O_CT, 3),
                     A(O_GACC, 768).rearrange("p (a b) -> p a b", a=3),
                     ALU.add), inc=vec)
    vsep()
    g_wait(Ep, "pp6")
    Ep.waitev(ev_r)
    t_cmb = Ep.op(mm(pp[6][0:1, 0:3], ONESC, A(O_CT, 3)), inc=pe)
    Ev.waitev(t_cmb)
    t_mn = Ev.op(ts(A(O_SVAL, 3, 0, 1), pp[6][0:1, 0:3], 1.0 / (II * D),
                    ALU.mult, 1e-5, ALU.max), inc=vec)
    Ep.waitev(t_mn)
    t_bc = Ep.op(mm(pp[6][:, 0:3], ONES1, A(O_SVAL, 3, 0, 1)), inc=pe)
    Ev.waitev(t_bc)
    Ev.op(ts(A(O_SG, 3), pp[6][:, 0:3], 0.5, ALU.mult))
    t_sg = Ev.op(ts(A(O_SG + 3, 3), pp[6][:, 0:3], -0.5, ALU.mult), inc=vec)
    g_set("pp6", t_sg)
    Ev.op(recip(A(O_ROWA, 1024, 0, 1), A(O_ROW1, 1024, 0, 1)))
    Ev.op(ts(A(O_ROWB, 1024, 0, 1), A(O_ROWA, 1024, 0, 1),
             A(O_SVAL + 1, 1, 0, 1), ALU.mult, INV_SQ1, ALU.mult))
    t_csg = Ev.op(ts(A(O_ROWA, 1024, 0, 1), A(O_ROWA, 1024, 0, 1),
                     A(O_SVAL, 1, 0, 1), ALU.mult, INV_SQ1, ALU.mult), inc=vec)

    if DBG_SCRATCH[0]:
        Eg.wait(vec, vec.n)
        Eg.wait(act, act.n)
        Eg.dma(dbgA[:, 896:1664], A(O_GACC, 768), st)
        Eg.dma(dbgA[:, 1664:1680], A(O_CT, 16), st)
        Eg.dma(dbgA[0:1, 3776:3780], A(O_SVAL, 4, 0, 1), st)
        Eg.dma(dbgA[:, 3780:3788], A(O_SG, 8), st)
        Eg.dma(dbgA[0:1, 3788:3800], A(O_ROWA, 12, 0, 1), st)
        Eg.dma(dbgA[0:1, 3800:3812], A(O_ROWB, 12, 0, 1), st)
        Eg.dma(dbgA[0:1, 3812:3824], A(O_ROW1, 12, 0, 1), st)
    barrier()
    for ha in range(2):
        barrier()
        for k in range(16):
            t_xq = Es.dma(Ab(O_XQ1 + k * 256, 512),
                          xq1_sc[k][:, ha * 512:ha * 512 + 512], ld)
        g_wait(Ep, "pp4")
        Ep.waitev(t_csg)
        t_rg = Ep.op(mm(pp[4][:, 0:512], ONES1,
                        A(O_ROWA + ha * 512, 512, 0, 1)), inc=pe)
        Ev.waitev(t_rg)
        t_rgc = Ev.op(cp(A(O_FS1, 512), pp[4][:, 0:512]), inc=vec)
        g_set("pp4", t_rgc)
        WGo = (O_WG, O_INDR)
        WUo = (O_WU, O_WU1)
        tern_ev = {}

        def f1_load(ic):
            if ic >= 64:
                return None
            g_wait(Es, f"wg{ic % 2}")
            t_wg = Es.dma(A(WGo[ic % 2], 2048), wgt[ic], ld)
            g_wait(Es, f"wu{ic % 2}")
            t_wu = Es.dma(A(WUo[ic % 2], 2048), wut[ic], ld)
            return (t_wg, t_wu)

        def f1_tern(ic, tw):
            if ic >= 64:
                return
            for mat in range(2):
                toff = (WGo if mat == 0 else WUo)[ic % 2]
                src = A(toff, 2048)
                sgc = 0 if mat == 0 else 1
                Ev.wait(ld, tw[mat][1])
                Ev.waitev(t_sg)
                for q in range(4):
                    Ev.op(ts(pp[6][:, 0:512], src[:, q * 512:q * 512 + 512],
                             A(O_SG + sgc, 1), ALU.is_gt))
                    Ev.op(ts(A(O_FS3, 512), src[:, q * 512:q * 512 + 512],
                             A(O_SG + 3 + sgc, 1), ALU.is_ge))
                    t_tn = Ev.op(stt(Ab(toff + q * 256, 512), pp[6][:, 0:512],
                                     -1.0, A(O_FS3, 512), ALU.add, ALU.add),
                                 inc=vec)
                tern_ev[(mat, ic)] = t_tn

        pend = f1_load(0)
        f1_tern(0, pend)
        nxt = f1_load(1)
        for ic in range(64):
            t_tg = tern_ev[(0, ic)]
            t_tu = tern_ev[(1, ic)]
            ppg = pp[ic % 2]
            ppu = pp[2 + (ic % 2)]
            g_wait(Ep, f"ppg{ic % 2}")
            g_wait(Ep, f"ppu{ic % 2}")
            Ep.waitev(t_tg)
            Ep.waitev(t_tu)
            Ep.wait(ld, t_xq[1])
            for k in range(16):
                Ep.op(mm(ppg[:, 0:512],
                         Ab(WGo[ic % 2] + k * 64, 128),
                         Ab(O_XQ1 + k * 256, 512),
                         start=(k == 0), stop=(k == 15)),
                      inc=(pe if k == 15 else None))
            t_gd = (pe, pe.n)
            for k in range(16):
                Ep.op(mm(ppu[:, 0:512], Ab(WUo[ic % 2] + k * 64, 128),
                         Ab(O_XQ1 + k * 256, 512),
                         start=(k == 0), stop=(k == 15)),
                      inc=(pe if k == 15 else None))
            t_ud = (pe, pe.n)
            g_set(f"wg{ic % 2}", t_gd)
            g_set(f"wu{ic % 2}", t_ud)
            Ev.waitev(t_gd)
            Ev.waitev(t_rgc)
            t_gs = Ev.op(tt(A(O_SH + ic * 512, 512), ppg[:, 0:512],
                            A(O_FS1, 512), ALU.mult), inc=vec)
            g_set(f"ppg{ic % 2}", t_gs)
            Ea.waitev(t_gs)
            g_wait(Ea, "FS2")
            t_si = Ea.op(actv(A(O_FS2, 512), A(O_SH + ic * 512, 512), AF.Silu),
                         inc=act)
            f1_tern(ic + 1, nxt)
            nxt = f1_load(ic + 2)
            Ev.waitev(t_si)
            Ev.waitev(t_ud)
            t_h = Ev.op(tt(A(O_SH + ic * 512, 512), A(O_FS2, 512),
                           ppu[:, 0:512], ALU.mult), inc=vec)
            g_set(f"ppu{ic % 2}", t_h)
            g_set("FS2", t_h)
        if DBG_SCRATCH[0] and ha == 0:
            Eg.wait(vec, vec.n)
            t_htap = Eg.dma(dbgA[:, 3264:3776], A(O_SH, 512), st)
            g_set("htap", t_htap)
        Ev.wait(pe, pe.n)
        g_wait(Ev, "htap")
        for cch in range(64):
            ppj = pp[4 + cch % 2]
            g_wait(Ep, f"pp{4 + cch % 2}")
            Ep.wait(vec, vec.n)
            t_hh = Ep.op(mm(ppj[:, 0:512], H128ap, A(O_SH + cch * 512, 512)),
                         inc=pe)
            Ev.waitev(t_hh)
            t_hc = Ev.op(cp(A(O_SH + cch * 512, 512), ppj[:, 0:512]), inc=vec)
            g_set(f"pp{4 + cch % 2}", t_hc)
        for stg in range(6):
            step = 1 << stg
            for cch in range(64):
                if cch & step:
                    continue
                a = A(O_SH + cch * 512, 512)
                b = A(O_SH + (cch + step) * 512, 512)
                Ev.op(tt(a, a, b, ALU.add))
                t_bf = Ev.op(stt(b, b, -2.0, a, ALU.mult, ALU.add), inc=vec)
        Ev.wait(pe, pe.n)
        t_am = Ev.op(red(A(O_STGV, 512), A(O_SH, 32768).rearrange(
            "p (c t) -> p t c", c=64), ALU.max, absval=True), inc=vec)
        for j in range(4):
            g_wait(Ep, "pp6")
            Ep.waitev(t_am)
            t_tr = Ep.op(trn(pp[6][:, 0:128], A(O_STGV + j * 128, 128)), inc=pe)
            Ev.waitev(t_tr)
            t_ac = Ev.op(red(A(O_CT + j, 1), pp[6][:, 0:128], ALU.max), inc=vec)
            g_set("pp6", t_ac)
        vsep()
        Ev.op(ts(A(O_CT + 4, 4), A(O_CT, 4), 1e-5, ALU.max,
                 1.0 / 127.0, ALU.mult))
        vsep()
        t_sc2 = Ev.op(recip(A(O_CT + 8, 4), A(O_CT + 4, 4)), inc=vec)
        vsep()
        for j in range(4):
            g_wait(Ep, "pp6")
            Ep.waitev(t_sc2)
            t_tr = Ep.op(trn(pp[6][:, 0:128], A(O_CT + 8 + j, 128)), inc=pe)
            Ev.waitev(t_tr)
            t_rw = Ev.op(cp(A(O_ROW2 + ha * 512 + j * 128, 128, 0, 1),
                            pp[6][0:1, 0:128]), inc=vec)
            g_set("pp6", t_rw)
        g_wait(Ep, "pp7")
        Ep.waitev(t_rw)
        t_rep = Ep.op(mm(pp[7][:, 0:512], ONES1,
                         A(O_ROW2 + ha * 512, 512, 0, 1)), inc=pe)
        Ev.waitev(t_rep)
        t_repc = Ev.op(cp(A(O_FS1, 512), pp[7][:, 0:512]), inc=vec)
        g_set("pp7", t_repc)
        for cch in range(64):
            Ev.waitev(t_repc)
            Ev.op(tt(A(O_STGV, 512), A(O_SH + cch * 512, 512), A(O_FS1, 512),
                     ALU.mult))
            Ev.op(ts(A(O_STGU, 512), A(O_STGV, 512), 128.5, ALU.add))
            for j in range(7, -1, -1):
                pw = float(1 << j)
                Ev.op(ts(A(O_STGM, 512), A(O_STGU, 512), pw, ALU.is_ge,
                         -pw, ALU.mult))
                Ev.op(tt(A(O_STGU, 512), A(O_STGU, 512), A(O_STGM, 512),
                         ALU.add))
            Ev.op(stt(A(O_STGM, 512), A(O_STGU, 512), -1.0, A(O_STGV, 512),
                      ALU.mult, ALU.add))
            g_wait(Ev, f"stgq{cch % 2}")
            t_q2 = Ev.op(ts(Ab(O_STGQ + (cch % 2) * 256, 512), A(O_STGM, 512),
                            0.5, ALU.add), inc=vec)
            Ea.waitev(t_q2)
            t_q2s = Ea.dma(xq2_sc[cch][:, ha * 512:ha * 512 + 512],
                           Ab(O_STGQ + (cch % 2) * 256, 512), st)
            g_set(f"stgq{cch % 2}", (st, t_q2s[1]))
    t_f2_done = st.n

    barrier()
    Ev.op(recip(A(O_ROWA, 1024, 0, 1), A(O_ROW2, 1024, 0, 1)))
    Ev.op(ts(A(O_ROWA, 1024, 0, 1), A(O_ROWA, 1024, 0, 1),
             A(O_SVAL + 2, 1, 0, 1), ALU.mult, INV_SQ2, ALU.mult))
    t_cd = Ev.op(tt(A(O_ROWA, 1024, 0, 1), A(O_ROWA, 1024, 0, 1),
                    A(O_ROWB, 1024, 0, 1), ALU.mult), inc=vec)

    for oblk in range(4):
        for kg in range(16):
            wd = A(O_WDS + (kg % 2) * 2048, 2048)
            g_wait(Es, f"wds{kg % 2}")
            t_wd = None
            for oo in range(4):
                t_wd = Es.dma(wd[:, oo * 512:oo * 512 + 512],
                              wdt[oblk * 4 + oo][:, kg * 512:kg * 512 + 512], ld)
            Ev.wait(ld, t_wd[1])
            Ev.waitev(t_sg)
            for oo in range(4):
                g_wait(Ev, "lnst")
                Ev.op(ts(A(O_LNS, 512), wd[:, oo * 512:oo * 512 + 512],
                         A(O_SG + 2, 1), ALU.is_gt))
                Ev.op(ts(A(O_LNS + 512, 512), wd[:, oo * 512:oo * 512 + 512],
                         A(O_SG + 5, 1), ALU.is_ge))
                t_tn = Ev.op(stt(Ab(O_WDS + (kg % 2) * 2048 + oo * 512, 512),
                                 A(O_LNS, 512), -1.0, A(O_LNS + 512, 512),
                                 ALU.add, ALU.add), inc=vec)
            g_set(f"terd{kg % 2}", t_tn)
            t_mmlast = None
            for kk in range(4):
                xb = Ab(O_XQ2B + (kk % 2) * 512, 1024)
                g_wait(Es, f"xq2b{kk % 2}")
                t_xq2 = Es.dma(xb, xq2_sc[kg * 4 + kk], ld)
                Ep.wait(ld, t_xq2[1])
                Ep.waitev(t_tn)
                if kg == 0 and kk == 0:
                    for jj in range(8):
                        g_wait(Ep, f"pp{jj}")
                for oo in range(4):
                    for half in range(2):
                        t_mmlast = Ep.op(
                            mm(pp[oo * 2 + half][:, 0:512],
                               Ab(O_WDS + (kg % 2) * 2048 + oo * 512
                                  + kk * 64, 128),
                               xb[:, half * 512:half * 512 + 512],
                               start=(kg == 0 and kk == 0),
                               stop=(kg == 15 and kk == 3)),
                            inc=(pe if (oo == 3 and half == 1) else None))
                g_set(f"xq2b{kk % 2}", t_mmlast)
            g_set(f"wds{kg % 2}", t_mmlast)
        for oo in range(4):
            o = oblk * 4 + oo
            Ev.waitev(t_mmlast)
            for half in range(2):
                t_dn = Ev.op(cp(A(O_DNS + o * 1024 + half * 512, 512),
                                pp[oo * 2 + half][:, 0:512]), inc=vec)
        for jj in range(8):
            g_set(f"pp{jj}", t_dn)
    g_wait(Ep, "pp4")
    Ep.waitev(t_cd)
    Ep.wait(vec, vec.n)
    t_cr = Ep.op(mm(pp[4][:, 0:512], ONES1, A(O_ROWA, 512, 0, 1)), inc=pe)
    t_cr2 = Ep.op(mm(pp[5][:, 0:512], ONES1, A(O_ROWA + 512, 512, 0, 1)),
                  inc=pe)
    t_stats = None
    for o in range(16):
        xb = A(O_X1B + (o % 2) * 1024, 1024)
        g_wait(Es, f"x1b{o % 2}")
        t_x1r = Es.dma(xb, x1_sc[o], ld)
        Ev.waitev(t_cr2)
        Ev.wait(ld, t_x1r[1])
        for half in range(2):
            dn = A(O_DNS + o * 1024 + half * 512, 512)
            Ev.op(tt(dn, dn, pp[4 + half][:, 0:512], ALU.mult))
            t_v2 = Ev.op(tt(dn, dn, xb[:, half * 512:half * 512 + 512],
                            ALU.add), inc=vec)
            g_set(f"x1b{o % 2}", t_v2)
            Ea.waitev(t_v2)
            g_wait(Ea, "lnsq")
            t_sq = Ea.op(actv(A(O_LNS, 512), dn, AF.Square), inc=act)
            if o == 0 and half == 0:
                g_wait(Ep, "pp2")
                g_wait(Ep, "pp3")
                g_wait(Ep, "pp6")
                g_wait(Ep, "pp7")
            Ep.waitev(t_v2)
            Ep.op(mm(pp[2 + half][0:1, 0:512], ONESC, dn,
                     start=(o == 0), stop=(o == 15)))
            Ep.waitev(t_sq)
            t_stats = Ep.op(mm(pp[6 + half][0:1, 0:512], ONESC, A(O_LNS, 512),
                               start=(o == 0), stop=(o == 15)), inc=pe)
            g_set("lnsq", t_stats)
    Ev.waitev(t_stats)
    for half in range(2):
        ro = half * 512
        Ev.op(ts(A(O_ROWA + ro, 512, 0, 1), pp[2 + half][0:1, 0:512], INV_D,
                 ALU.mult))
        Ev.op(ts(A(O_ROWB + ro, 512, 0, 1), pp[6 + half][0:1, 0:512], INV_D,
                 ALU.mult))
        Ev.op(tt(A(O_ROW1 + ro, 512, 0, 1), A(O_ROWA + ro, 512, 0, 1),
                 A(O_ROWA + ro, 512, 0, 1), ALU.mult))
        Ev.op(tt(A(O_ROWB + ro, 512, 0, 1), A(O_ROWB + ro, 512, 0, 1),
                 A(O_ROW1 + ro, 512, 0, 1), ALU.subtract))
        t_var = Ev.op(ts(A(O_ROWB + ro, 512, 0, 1), A(O_ROWB + ro, 512, 0, 1),
                         1e-5, ALU.add), inc=vec)
        Ea.waitev(t_var)
        t_sd = Ea.op(actv(A(O_ROW2 + ro, 512, 0, 1), A(O_ROWB + ro, 512, 0, 1),
                          AF.Sqrt), inc=act)
        Ev.waitev(t_sd)
        Ev.op(recip(A(O_ROW2 + ro, 512, 0, 1), A(O_ROW2 + ro, 512, 0, 1)))
        Ev.op(tt(A(O_ROW1 + ro, 512, 0, 1), A(O_ROW2 + ro, 512, 0, 1),
                 A(O_ROW2 + ro, 512, 0, 1), ALU.mult))
        Ev.op(tt(A(O_ROW1 + ro, 512, 0, 1), A(O_ROW1 + ro, 512, 0, 1),
                 A(O_ROWB + ro, 512, 0, 1), ALU.mult))
        Ev.op(ts(A(O_ROW1 + ro, 512, 0, 1), A(O_ROW1 + ro, 512, 0, 1),
                 -0.5, ALU.mult, 1.5, ALU.add))
        t_rst = Ev.op(tt(A(O_ROW2 + ro, 512, 0, 1), A(O_ROW2 + ro, 512, 0, 1),
                         A(O_ROW1 + ro, 512, 0, 1), ALU.mult), inc=vec)
    g_set("pp2", t_rst)
    g_set("pp3", t_rst)
    g_set("pp6", t_rst)
    g_set("pp7", t_rst)
    g_wait(Ep, "pp2")
    Ep.waitev(t_rst)
    t_mr0 = Ep.op(mm(pp[2][:, 0:512], ONES1, A(O_ROWA, 512, 0, 1)), inc=pe)
    t_mr1 = Ep.op(mm(pp[3][:, 0:512], ONES1, A(O_ROWA + 512, 512, 0, 1)),
                  inc=pe)
    t_rr0 = Ep.op(mm(pp[6][:, 0:512], ONES1, A(O_ROW2, 512, 0, 1)), inc=pe)
    t_rr1 = Ep.op(mm(pp[7][:, 0:512], ONES1, A(O_ROW2 + 512, 512, 0, 1)),
                  inc=pe)
    t_x2 = None
    for o in range(16):
        Ev.waitev(t_rr1)
        for half in range(2):
            dn = A(O_DNS + o * 1024 + half * 512, 512)
            Ev.op(tt(dn, dn, pp[2 + half][:, 0:512], ALU.subtract))
            t_x2 = Ev.op(tt(dn, dn, pp[6 + half][:, 0:512], ALU.mult), inc=vec)
    barrier()
    t_out = None
    for tts in range(8):
        g_wait(Ev, "outa")
        for o in range(16):
            ppj = pp[o % 2]
            g_wait(Ep, f"ppt{o % 2}")
            Ep.waitev(t_x2)
            t_tr = Ep.op(trn(ppj[:, 0:128],
                             A(O_DNS + o * 1024 + tts * 128, 128)), inc=pe)
            Ev.waitev(t_tr)
            t_oc = Ev.op(cp(A(O_OUTA + o * 128, 128), ppj[:, 0:128]), inc=vec)
            g_set(f"ppt{o % 2}", t_oc)
        Eg.waitev(t_oc)
        t_out = Eg.dma(out[tts * 128:tts * 128 + 128, :], A(O_OUTA, 2048), st)
        g_set("outa", t_out)
    Eg.wait(st, st.n)

    with nc.Block() as block:
        @block.sync
        def _(eng):
            for f in Es.ops:
                f(eng)

        @block.tensor
        def _(eng):
            for f in Ep.ops:
                f(eng)

        @block.vector
        def _(eng):
            for f in Ev.ops:
                f(eng)

        @block.scalar
        def _(eng):
            for f in Ea.ops:
                f(eng)

        @block.gpsimd
        def _(eng):
            for f in Eg.ops:
                f(eng)

    ctx.close()
    return nc


_NC_CACHE = []


def kernel(**inputs):
    x = np.asarray(inputs["L0_attn_hidden_states"], dtype=np.float32)
    Wq = np.asarray(inputs["L0_attn_Wq"], dtype=np.float32)
    Wk = np.asarray(inputs["L0_attn_Wk"], dtype=np.float32)
    Wv = np.asarray(inputs["L0_attn_Wv"], dtype=np.float32)
    Wo = np.asarray(inputs["L0_attn_Wo"], dtype=np.float32)
    wg = np.asarray(inputs["L0_ffn_w_gate"], dtype=np.float32)
    wu = np.asarray(inputs["L0_ffn_w_up"], dtype=np.float32)
    wd = np.asarray(inputs["L0_ffn_w_down"], dtype=np.float32)

    nc = build_nc()
    in_maps, order_all = _stage(x, Wq, Wk, Wv, Wo, wg, wu, wd)
    res = run_bass_kernel_spmd(nc, in_maps, core_ids=list(range(8)))
    y = np.zeros((2, S, D), dtype=np.float32)
    for core in range(8):
        b, cols = order_all[core]
        y[b, cols, :] = np.asarray(res.results[core]["out"])
    return y


def _stage(x, Wq, Wk, Wv, Wo, wg, wu, wd):
    perm = _qperm()
    wqtS = _strips(np.ascontiguousarray(Wq.T[:, perm]))
    wktS = _strips(np.ascontiguousarray(Wk.T))
    wvtS = _strips(np.ascontiguousarray(Wv.T))
    wotS = _strips(np.ascontiguousarray(Wo.T[perm, :]))
    wgtS = _strips(np.ascontiguousarray(wg.T))
    wutS = _strips(np.ascontiguousarray(wu.T))
    wdtS = _strips(np.ascontiguousarray(wd.T))
    cosA, sinA = _rope_tables()
    h128 = _hadamard(128).astype(np.float32)
    ident = np.eye(128, dtype=np.float32)
    p64 = np.zeros((64, 64), dtype=np.float32)
    p64[:32, 32:] = -np.eye(32)
    p64[32:, :32] = np.eye(32)
    p2 = np.zeros((128, 128), dtype=np.float32)
    p2[:64, :64] = p64.T
    p2[64:, 64:] = p64.T
    ones1 = np.ones((1, 128), dtype=np.float32)
    onescol = np.ones((128, 1), dtype=np.float32)
    indq = np.zeros((128, 512), dtype=np.float32)
    for j in range(16):
        G, r = j // 4, j % 4
        hA, hB = 8 * G + r, 8 * G + 4 + r
        indq[0:64, j * 32 + hA] = 1.0
        indq[64:128, j * 32 + hB] = 1.0
    indk = np.zeros((128, 32), dtype=np.float32)
    for o in range(4):
        indk[0:64, o * 8 + 2 * o] = 1.0
        indk[64:128, o * 8 + 2 * o + 1] = 1.0
    indkq = np.zeros((8, 32), dtype=np.float32)
    for h in range(32):
        indkq[h // 4, h] = 1.0
    indrep = np.zeros((32, 2048), dtype=np.float32)
    for j in range(16):
        G, r = j // 4, j % 4
        hA, hB = 8 * G + r, 8 * G + 4 + r
        indrep[hA, j * 128 + np.arange(0, 64)] = 1.0
        indrep[hB, j * 128 + np.arange(64, 128)] = 1.0
    indgrp = np.zeros((8, 1024), dtype=np.float32)
    for g8 in range(8):
        indgrp[g8, g8 * 128:(g8 + 1) * 128] = 1.0
    in_maps = []
    order_all = []
    for core in range(8):
        b, c = core // 4, core % 4
        qt = _core_qtiles(c)
        cols = np.concatenate([np.arange(t * 128, (t + 1) * 128) for t in qt])
        order_all.append((b, cols))
        xTb = np.ascontiguousarray(x[b].T)
        xTm = np.ascontiguousarray(xTb[:, cols]).reshape(16, 128, NT)
        xTf = xTb.reshape(16, 128, S)
        in_maps.append({
            "xTm": xTm, "xTf": xTf,
            "wqt": wqtS, "wkt": wktS, "wvt": wvtS, "wot": wotS,
            "wgt": wgtS, "wut": wutS, "wdt": wdtS,
            "cosq": np.ascontiguousarray(cosA[:, cols]),
            "sinq": np.ascontiguousarray(sinA[:, cols]),
            "cosk": cosA, "sink": sinA,
            "masks": _masks_for_core(c),
            "h128": h128, "ident": ident, "p2": p2,
            "ones1": ones1, "onescol": onescol,
            "indq": indq, "indk": indk, "indkq": indkq,
            "indrep": indrep, "indgrp": indgrp,
        })
    return in_maps, order_all


# version 05
# speedup vs baseline: 1.1334x; 1.0098x over previous
from contextlib import ExitStack

import numpy as np
import ml_dtypes

import concourse.bass as bass
import concourse.mybir as mybir
from concourse.bass_utils import run_bass_kernel_spmd

F32 = mybir.dt.float32
BF16 = mybir.dt.bfloat16
AF = mybir.ActivationFunctionType
ALU = mybir.AluOpType
AX = mybir.AxisListType

S, D, HD, II, NT, NSLOT = 4096, 2048, 64, 8192, 1024, 8
PFX = [4, 8, 12, 16, 20, 24, 28, 32]
MASK_NEG = -1.0e30
INV_D = 1.0 / 2048.0
INV_SQ1 = float(2048.0 ** -0.5)
INV_SQ2 = float(8192.0 ** -0.5)
ARENA = 52224

O_H128, O_ID, O_P2, O_ONES1, O_ONESC = 0, 128, 256, 384, 512
O_SVAL = 516
O_SG = 520
O_KM = 528
O_KN = 536
O_CG = 568
O_IKQ = 576
O_INDQ = 640
O_INDR = 1152
O_INDG = 3200
O_BIAS = 4224
O_CT = 4288
O_ROW1 = 4544
O_ROW2 = 5568
O_ROWA = 6592
O_ROWB = 7616
O_GACC = 8640
O_FS1 = 9472
O_FS2 = 9984
O_FS3 = 10496
O_KVW = 11008
O_XS = 27500
O_CK, O_SK, O_VSTG = 35800, 36312, 36824
O_KA, O_KB, O_KC, O_KNS = 37400, 37912, 38424, 38936
O_QT = 11008
O_XMH = 27500
O_WQS = 35800
O_CQ, O_SQ = 39900, 40924
O_QN = 41950
O_SA, O_SB, O_SC = 42980, 43492, 44004
O_OSB = 27500
O_MASK = 35800
O_KG = 39900
O_VG = 44000
O_YSB = 39900
O_PB = 48200
O_EV = 49224
O_DEN = 49740
O_P1 = 50176
O_P1D = 51200
O_XQ1 = 11008
O_WG, O_WU = 15104, 17152
O_WU1 = 3200
O_SH = 19200
O_STGV, O_STGU, O_STGM = 13056, 13568, 14080
O_STGQ = 14592
O_UU, O_MM = 13056, 11008
O_DNS = 11008
O_WDS = 27500
O_XQ2B = 31700
O_X1B = 32800
O_OUTA = 34900
O_LNS = 37000


def _hadamard(n):
    h = np.array([[1.0]])
    while h.shape[0] < n:
        h = np.block([[h, h], [h, -h]])
    return h


def _qperm():
    order = []
    for G in range(4):
        for r in range(4):
            for h in (8 * G + r, 8 * G + 4 + r):
                order.extend(range(h * HD, (h + 1) * HD))
    return np.array(order)


def _core_qtiles(c):
    return [c, 7 - c, 8 + c, 15 - c, 16 + c, 23 - c, 24 + c, 31 - c]


def _strips(aT, pi=128):
    K, M = aT.shape
    t = aT.reshape(K // 128, 128, M // pi, pi)
    return np.ascontiguousarray(
        t.transpose(2, 1, 0, 3).reshape(M // pi, 128, (K // 128) * pi))


def _rope_tables():
    inv = 1.0 / (10000.0 ** (np.arange(0, HD, 2, dtype=np.float64) / HD))
    pos = np.arange(S, dtype=np.float64)
    fr = pos[:, None] * inv[None, :]
    emb = np.concatenate([fr, fr], axis=1)
    cos = np.cos(emb).T.astype(np.float32)
    sin = np.sin(emb).T.astype(np.float32)
    return (np.ascontiguousarray(np.concatenate([cos, cos], 0)),
            np.ascontiguousarray(np.concatenate([sin, sin], 0)))


def _masks_for_core(c):
    qt = _core_qtiles(c)
    m = np.zeros((NSLOT, 4, 128, 512), dtype=np.float32)
    for s in range(NSLOT):
        for j in range(4):
            kt = PFX[s] - 4 + j
            kpos = kt * 128 + np.arange(128)[:, None]
            qpos = qt[s] * 128 + np.arange(128)[None, :]
            m[s, j] = np.tile(np.where(kpos > qpos, MASK_NEG, 0.0), (1, 4))
    return m.astype(ml_dtypes.bfloat16)


class Sem:
    def __init__(self, h):
        self.h = h
        self.n = 0

    def bump(self, k=1):
        self.n += k
        return self.n


class E:
    def __init__(self, name):
        self.name = name
        self.ops = []
        self.seen = {}

    def wait(self, sem, val):
        if val is None or val <= 0:
            return
        if self.seen.get(id(sem), -1) >= val:
            return
        self.seen[id(sem)] = val
        self.ops.append(lambda e, s=sem.h, v=val: e.wait_ge(s, v))

    def waitev(self, ev):
        if ev is not None:
            self.wait(ev[0], ev[1])

    def op(self, fn, inc=None, k=1):
        if inc is None:
            self.ops.append(fn)
            return None
        t = inc.bump(k)
        self.ops.append(lambda e, f=fn, s=inc.h, kk=k: f(e).then_inc(s, kk))
        return (inc, t)

    def dma(self, dst, src, sem):
        return self.op(lambda e, d=dst, s=src: e.dma_start(out=d, in_=s),
                       inc=sem, k=16)


DBG_SCRATCH = [False]


def build_nc():
    nc = bass.Bass("TRN2", target_bir_lowering=False)

    def dinp(name, shape, dt=F32):
        return nc.dram_tensor(name, list(shape), dt, kind="ExternalInput").ap()

    xTm = dinp("xTm", [16, 128, NT])
    xTf = dinp("xTf", [16, 128, S])
    wqt = dinp("wqt", [16, 128, 2048])
    wkt = dinp("wkt", [4, 128, 2048])
    wvt = dinp("wvt", [4, 128, 2048])
    wot = dinp("wot", [16, 128, 2048])
    wgt = dinp("wgt", [64, 128, 2048])
    wut = dinp("wut", [64, 128, 2048])
    wdt = dinp("wdt", [16, 128, 8192])
    cosq = dinp("cosq", [128, NT])
    sinq = dinp("sinq", [128, NT])
    cosk = dinp("cosk", [128, S])
    sink = dinp("sink", [128, S])
    masksI = dinp("masks", [NSLOT, 4, 128, 512], BF16)
    c_h128 = dinp("h128", [128, 128])
    c_ident = dinp("ident", [128, 128])
    c_p2 = dinp("p2", [128, 128])
    c_ones1 = dinp("ones1", [1, 128])
    c_onescol = dinp("onescol", [128, 1])
    c_indq = dinp("indq", [128, 512])
    c_indk = dinp("indk", [128, 32])
    c_indkq = dinp("indkq", [8, 32])
    c_indrep = dinp("indrep", [32, 2048])
    c_indgrp = dinp("indgrp", [8, 1024])

    kd = "ExternalOutput" if DBG_SCRATCH[0] else "Internal"
    kt_sc = nc.dram_tensor("kt_sc", [4, 128, S], F32, kind=kd).ap()
    vt_sc = nc.dram_tensor("vt_sc", [4, 128, 32 * 130], F32, kind=kd).ap()
    x1_sc = nc.dram_tensor("x1_sc", [16, 128, NT], F32, kind=kd).ap()
    xq1_sc = nc.dram_tensor("xq1_sc", [16, 128, NT], BF16, kind=kd).ap()
    xq2_sc = nc.dram_tensor("xq2_sc", [64, 128, NT], BF16, kind=kd).ap()
    out = nc.dram_tensor("out", [NT, D], F32, kind="ExternalOutput").ap()
    dbgA = nc.dram_tensor("dbgA", [128, 4096], F32,
                          kind=("ExternalOutput" if DBG_SCRATCH[0] else "Internal")).ap()

    ctx = ExitStack()
    ar = ctx.enter_context(nc.sbuf_tensor("arena", [128, ARENA], F32))
    s_indk = ctx.enter_context(nc.sbuf_tensor("s_indk", [128, 32], F32))
    arB = ar[:].bitcast(BF16)

    def A(off, cols, p0=0, pn=128):
        return ar[p0:p0 + pn, off:off + cols]

    def Ab(off_f32, cols_bf16, p0=0, pn=128):
        return arB[p0:p0 + pn, 2 * off_f32:2 * off_f32 + cols_bf16]

    pp = [ctx.enter_context(nc.psum_tensor(f"pp{j}", [128, 512], F32))
          for j in range(8)]

    sems = {k: Sem(ctx.enter_context(nc.semaphore(k)))
            for k in ("ld", "st", "pe", "vec", "act", "gp", "gld")}
    ld, st, pe, vec, act, gp, gld = (sems[k] for k in
                                     ("ld", "st", "pe", "vec", "act", "gp", "gld"))

    Es, Ep, Ev, Ea, Eg = E("sync"), E("tensor"), E("vector"), E("scalar"), E("gpsimd")
    ALL_E = (Es, Ep, Ev, Ea, Eg)

    guard = {}

    def g_wait(Eng, key):
        for ev in guard.get(key, ()):
            Eng.waitev(ev)

    def g_set(key, *evs):
        guard[key] = [e for e in evs if e is not None]

    def barrier():
        evs = [(pe, pe.n), (vec, vec.n), (act, act.n), (st, st.n),
               (ld, ld.n), (gp, gp.n)]
        for Eng in ALL_E:
            for ev in evs:
                Eng.waitev(ev)

    def mm(out_ap, lhsT, rhs, start=True, stop=True):
        return lambda e: e.matmul(out_ap, lhsT, rhs, start=start, stop=stop)

    def cp(dst, src):
        return lambda e: e.tensor_copy(dst, src)

    def tt(dst, a, b, op):
        return lambda e: e.tensor_tensor(dst, a, b, op)

    def ts(dst, a, s1, op0, s2=None, op1=None):
        if op1 is None:
            return lambda e: e.tensor_scalar(dst, a, s1, None, op0)
        return lambda e: e.tensor_scalar(dst, a, s1, s2, op0, op1)

    def stt(dst, in0, scalar, in1, op0, op1):
        return lambda e: e.scalar_tensor_tensor(dst, in0, scalar, in1, op0, op1)

    def actv(dst, src, func, bias=0.0, scale=1.0):
        return lambda e: e.activation(out=dst, in_=src, func=func,
                                      bias=bias, scale=scale)

    def red(dst, src, op, absval=False, axis=None):
        ax = AX.X if axis is None else axis
        return lambda e: e.tensor_reduce(dst, src, ax, op,
                                         apply_absolute_value=absval)

    def recip(dst, src):
        return lambda e: e.reciprocal(dst, src)

    def trn(out_ap, in_ap):
        return lambda e: e.transpose(out_ap, in_ap, ar[0:128, O_ID:O_ID + 128])

    def vsep():
        Ev.op(cp(A(O_CT + 254, 1), A(O_ONESC, 1)))

    H128ap = A(O_H128, 128)
    P2ap = A(O_P2, 128)
    ONES1 = A(O_ONES1, 128, 0, 1)
    ONESC = A(O_ONESC, 1)

    for dst, src in [(A(O_H128, 128), c_h128), (A(O_ID, 128), c_ident),
                     (A(O_P2, 128), c_p2), (ONES1, c_ones1), (ONESC, c_onescol),
                     (A(O_INDQ, 512), c_indq), (s_indk[:], c_indk),
                     (A(O_IKQ, 32, 0, 8), c_indkq),
                     (A(O_INDR, 2048, 0, 32), c_indrep),
                     (A(O_INDG, 1024, 0, 8), c_indgrp)]:
        Es.dma(dst, src[:], ld)
    t_const = ld.n
    for r in range(4):
        Eg.op(lambda e, rr=r: e.memset(A(O_VSTG + rr * 130 + 64, 1), 1.0))
        Eg.op(lambda e, rr=r: e.memset(A(O_VSTG + rr * 130 + 129, 1), 1.0))
    Eg.op(lambda e: e.memset(A(O_GACC, 768), 0.0), inc=gp)

    for j, src in enumerate([wkt, wvt]):
        for o in range(4):
            Es.dma(A(O_KVW + (j * 4 + o) * 2048, 2048), src[o], ld)
    t_kvw = ld.n

    def kvw(is_v, o, k):
        return A(O_KVW + ((4 if is_v else 0) + o) * 2048 + k * 128, 128)

    p1_loads = []
    for wsrc, nslab, wcols in ((wgt, 64, 2048), (wut, 64, 2048), (wdt, 16, 8192)):
        for sl in range(nslab):
            for hh in range(wcols // 512):
                p1_loads.append(wsrc[sl][:, hh * 512:hh * 512 + 512])
    p1_state = {"ld": 0, "cons": 0, "consev": {}, "ldev": {}}

    def actv_acc(dst, srcq, func, accum):
        return lambda e: e.activation(out=dst, in_=srcq, func=func,
                                      bias=0.0, scale=1.0, accum_out=accum)

    def pump_pass1(n=1):
        for _ in range(n):
            i = p1_state["ld"]
            if i < len(p1_loads):
                if i >= 2:
                    Eg.waitev(p1_state["consev"].get(i - 2))
                p1_state["ldev"][i] = Eg.dma(A(O_P1 + (i % 2) * 512, 512),
                                             p1_loads[i], gld)
                p1_state["ld"] += 1
            j = p1_state["cons"]
            if j < len(p1_loads) and j < p1_state["ld"]:
                Ea.waitev(p1_state["ldev"][j])
                ev = Ea.op(actv_acc(A(O_P1D, 512), A(O_P1 + (j % 2) * 512, 512),
                                    AF.Abs, A(O_GACC + j, 1)), inc=act)
                p1_state["consev"][j] = ev
                p1_state["cons"] += 1

    def xchunk(k):
        return A(O_XS + (k // 8) * 4096 + (k % 8) * 512, 512)

    Ep.wait(ld, t_kvw)
    last_x_use = None
    for tb in range(8):
        Es.waitev(last_x_use)
        for k in range(16):
            t_x = Es.dma(xchunk(k), xTf[k][:, tb * 512:tb * 512 + 512], ld)
        g_wait(Es, "cksk")
        Es.dma(A(O_CK, 512), cosk[:, tb * 512:tb * 512 + 512], ld)
        t_cs = Es.dma(A(O_SK, 512), sink[:, tb * 512:tb * 512 + 512], ld)
        for o8 in range(8):
            pump_pass1(2)
            is_v = o8 >= 4
            o = o8 % 4
            ppj = pp[o8 % 2]
            g_wait(Ep, f"pp{o8 % 2}")
            Ep.wait(ld, t_x[1])
            for k in range(16):
                t_pe = Ep.op(mm(ppj[:, 0:512], kvw(is_v, o, k), xchunk(k),
                                start=(k == 0), stop=(k == 15)),
                             inc=(pe if k == 15 else None))
            if not is_v:
                g_wait(Ev, "KA")
                Ev.waitev(t_pe)
                t_ev = Ev.op(cp(A(O_KA, 512), ppj[:, 0:512]), inc=vec)
                g_set(f"pp{o8 % 2}", t_ev)
                g_wait(Ep, "pp23")
                Ep.waitev(t_ev)
                t_pr = Ep.op(mm(pp[2][:, 0:512], P2ap, A(O_KA, 512)), inc=pe)
                Ev.wait(ld, t_cs[1])
                Ev.op(tt(A(O_KB, 512), A(O_KA, 512), A(O_CK, 512), ALU.mult))
                Ev.waitev(t_pr)
                Ev.op(tt(A(O_KC, 512), pp[2][:, 0:512], A(O_SK, 512), ALU.mult))
                t_rope = Ev.op(tt(A(O_KA, 512), A(O_KB, 512), A(O_KC, 512),
                                  ALU.add), inc=vec)
                g_set("pp23", t_rope)
                g_set("cksk", t_rope)
                Ea.waitev(t_rope)
                t_sp = Ea.dma(kt_sc[o][:, tb * 512:tb * 512 + 512],
                              A(O_KA, 512), st)
                t_sq = Ea.op(actv(A(O_KNS, 512), A(O_KA, 512), AF.Square),
                             inc=act)
                g_set("KA", (st, t_sp[1]), t_sq)
                g_wait(Ep, "pp4")
                Ep.waitev(t_sq)
                t_kn = Ep.op(mm(pp[4][0:8, 0:512], s_indk[:, o * 8:o * 8 + 8],
                                A(O_KNS, 512)), inc=pe)
                Ev.waitev(t_kn)
                Ev.op(cp(A(O_KB, 512, 0, 8), pp[4][0:8, 0:512]))
                vsep()
                t_knr = Ev.op(red(A(O_KN + tb * 4, 4, 0, 8),
                                  A(O_KB, 512, 0, 8).rearrange(
                                      "p (a b) -> p a b", a=4),
                                  ALU.max), inc=vec)
                g_set("pp4", t_knr)
            else:
                g_wait(Ev, "KB")
                Ev.waitev(t_pe)
                t_ev = Ev.op(cp(A(O_KB, 512), ppj[:, 0:512]), inc=vec)
                g_set(f"pp{o8 % 2}", t_ev)
                g_wait(Ev, "vstg")
                g_wait(Ep, "pp6")
                Ep.waitev(t_ev)
                for r in range(4):
                    t_tr = Ep.op(trn(pp[6][:, r * 128:r * 128 + 128],
                                     A(O_KB + r * 128, 128)),
                                 inc=(pe if r == 3 else None))
                Ev.waitev(t_tr)
                for r in range(4):
                    Ev.op(cp(A(O_VSTG + r * 130, 64),
                             pp[6][:, r * 128:r * 128 + 64]))
                    t_vc = Ev.op(cp(A(O_VSTG + r * 130 + 65, 64),
                                    pp[6][:, r * 128 + 64:r * 128 + 128]),
                                 inc=vec)
                g_set("pp6", t_vc)
                g_set("KB", t_vc)
                Ea.waitev(t_vc)
                t_vs = Ea.dma(vt_sc[o][:, tb * 4 * 130:(tb * 4 + 4) * 130],
                              A(O_VSTG, 520), st)
                g_set("vstg", (st, t_vs[1]))
        last_x_use = (pe, pe.n)
    t_c_spill = st.n

    barrier()
    Es.dma(A(O_CQ, 1024), cosq[:], ld)
    t_cq = Es.dma(A(O_SQ, 1024), sinq[:], ld)

    def QT(j):
        return A(O_QT + j * 1024, 1024)

    t_qn = None
    for half in range(2):
        g_wait(Es, "xmh")
        for k in range(16):
            t_xm = Es.dma(A(O_XMH + k * 512, 512),
                          xTm[k][:, half * 512:half * 512 + 512], ld)
        t_xm_last_use = None
        for o in range(16):
            pump_pass1(4)
            wbuf = A(O_WQS + (o % 2) * 2048, 2048)
            g_wait(Es, f"wqs{o % 2}")
            t_w = Es.dma(wbuf, wqt[o], ld)
            ppj = pp[o % 2]
            g_wait(Ep, f"pp{o % 2}")
            Ep.wait(ld, t_xm[1])
            Ep.wait(ld, t_w[1])
            for k in range(16):
                t_pe = Ep.op(mm(ppj[:, 0:512], wbuf[:, k * 128:k * 128 + 128],
                                A(O_XMH + k * 512, 512),
                                start=(k == 0), stop=(k == 15)),
                             inc=(pe if k == 15 else None))
            g_set(f"wqs{o % 2}", t_pe)
            t_xm_last_use = t_pe
            qdst = QT(o)[:, half * 512:half * 512 + 512]
            Ev.waitev(t_pe)
            t_ev = Ev.op(cp(qdst, ppj[:, 0:512]), inc=vec)
            g_set(f"pp{o % 2}", t_ev)
            g_wait(Ep, "pp23")
            Ep.waitev(t_ev)
            t_pr = Ep.op(mm(pp[2][:, 0:512], P2ap, qdst), inc=pe)
            Ev.wait(ld, t_cq[1])
            Ev.op(tt(A(O_SA, 512), qdst, A(O_CQ + half * 512, 512), ALU.mult))
            Ev.waitev(t_pr)
            Ev.op(tt(A(O_SB, 512), pp[2][:, 0:512],
                     A(O_SQ + half * 512, 512), ALU.mult))
            t_rope = Ev.op(tt(qdst, A(O_SA, 512), A(O_SB, 512), ALU.add),
                           inc=vec)
            g_set("pp23", t_rope)
            g_wait(Ea, "SC")
            Ea.waitev(t_rope)
            t_sq = Ea.op(actv(A(O_SC, 512), qdst, AF.Square), inc=act)
            if o == 0:
                g_wait(Ep, f"pp{4 + half}")
            Ep.waitev(t_sq)
            t_qn = Ep.op(mm(pp[4 + half][0:32, 0:512],
                            A(O_INDQ + o * 32, 32), A(O_SC, 512),
                            start=(o == 0), stop=(o == 15)), inc=pe)
            g_set("SC", t_qn)
        g_set("xmh", t_xm_last_use)
        Ev.waitev(t_qn)
        t_qe = Ev.op(cp(A(O_QN + half * 512, 512, 0, 32),
                        pp[4 + half][0:32, 0:512]), inc=vec)
        g_set(f"pp{4 + half}", t_qe)

    t_qm = Ev.op(red(A(O_CG, 8, 0, 32),
                     A(O_QN, 1024, 0, 32).rearrange("p (a b) -> p a b", a=8),
                     ALU.max), inc=vec)
    Ea.waitev(t_qm)
    t_qs = Ea.op(actv(A(O_CT, 8, 0, 32), A(O_CG, 8, 0, 32), AF.Sqrt), inc=act)
    for s in range(NSLOT):
        tkr = Ev.op(red(A(O_KM + s, 1, 0, 8), A(O_KN, PFX[s], 0, 8), ALU.max),
                    inc=(vec if s == NSLOT - 1 else None))
    Ea.waitev(tkr)
    t_ks = Ea.op(actv(A(O_CT + 8, 8, 0, 8), A(O_KM, 8, 0, 8), AF.Sqrt), inc=act)
    g_wait(Ep, "pp6")
    Ep.waitev(t_ks)
    t_kx = Ep.op(mm(pp[6][0:32, 0:8], A(O_IKQ, 32, 0, 8), A(O_CT + 8, 8, 0, 8)),
                 inc=pe)
    Ev.waitev(t_kx)
    Ev.waitev(t_qs)
    Ev.op(lambda e: e.memset(A(O_CT + 16, 128), 0.0))
    t_c1 = Ev.op(tt(A(O_CT + 16, 8, 0, 32), A(O_CT, 8, 0, 32), pp[6][0:32, 0:8],
                    ALU.mult), inc=vec)
    g_set("pp6", t_c1)
    g_wait(Ep, "pp6")
    Ep.waitev(t_c1)
    t_tr1 = Ep.op(trn(pp[6][:, 0:128], A(O_CT + 16, 128)), inc=pe)
    Ev.waitev(t_tr1)
    Ev.op(lambda e: e.memset(A(O_CT + 16, 128), 0.0))
    t_gm = Ev.op(red(A(O_CT + 16, 8, 0, 8),
                     pp[6][0:8, 0:32].rearrange("p (a b) -> p a b", a=8),
                     ALU.max), inc=vec)
    g_set("pp6", t_gm)
    g_wait(Ep, "pp6")
    Ep.waitev(t_gm)
    t_tr2 = Ep.op(trn(pp[6][:, 0:128], A(O_CT + 16, 128)), inc=pe)
    Ev.waitev(t_tr2)
    t_c2 = Ev.op(ts(A(O_CT + 144, 8, 0, 8), pp[6][0:8, 0:8], -0.125,
                    ALU.mult), inc=vec)
    g_set("pp6", t_c2)
    g_wait(Ep, "pp7")
    Ep.waitev(t_c2)
    for g8 in range(8):
        t_bb = Ep.op(mm(pp[7][:, g8 * 8:g8 * 8 + 8],
                        A(O_INDG + g8 * 128, 128, 0, 8),
                        A(O_CT + 144, 8, 0, 8)),
                     inc=(pe if g8 == 7 else None))
    Ev.waitev(t_bb)
    t_bias = Ev.op(cp(A(O_BIAS, 64), pp[7][:, 0:64]), inc=vec)
    g_set("pp7", t_bias)

    if DBG_SCRATCH[0]:
        Eg.wait(vec, vec.n)
        Eg.wait(pe, pe.n)
        Eg.dma(dbgA[:, 0:64], A(O_BIAS, 64), st)
        Eg.dma(dbgA[0:8, 192:200], A(O_KM, 8, 0, 8), st)
        Eg.dma(dbgA[0:8, 200:232], A(O_KN, 32, 0, 8), st)
        Eg.dma(dbgA[0:32, 232:240], A(O_CG, 8, 0, 32), st)
        Eg.dma(dbgA[:, 704:1728], A(O_QT, 1024), st)
    barrier()
    qt_my = None
    ocnt = 0
    for sg in range(2):
        barrier()
        g_wait(Es, "maskarea")
        for si in range(4):
            for j in range(4):
                t_mk = Es.dma(Ab(O_MASK + (si * 4 + j) * 256, 512),
                              masksI[sg * 4 + si][j], ld)
        if DBG_SCRATCH[0] and sg == 0:
            Es.wait(ld, t_mk[1])
            Es.dma(dbgA[:, 2752:3776], A(O_MASK, 1024), st)
        t_kgvg = None
        for G in range(4):
            g_wait(Es, "kgvg")
            Es.wait(st, t_c_spill)
            Es.dma(A(O_KG, 4096), kt_sc[G], ld)
            t_kgvg = Es.dma(A(O_VG, 4160), vt_sc[G], ld)
            for grp in range(2):
                for si in range(4):
                    pump_pass1(8)
                    s = sg * 4 + si
                    po = pp[4 + ocnt % 2]
                    g_wait(Ep, f"pp{4 + ocnt % 2}")
                    def emit_L(kt, s=s, G=G, grp=grp, si=si,
                               t_kgvg=t_kgvg, t_mk=t_mk):
                        Lp = pp[kt % 2]
                        g_wait(Ep, f"pp{kt % 2}")
                        Ep.wait(ld, t_kgvg[1])
                        Ep.wait(ld, t_mk[1])
                        for r in range(4):
                            t_L = Ep.op(
                                mm(Lp[:, r * 128:r * 128 + 128],
                                   A(O_KG + kt * 128, 128, grp * 64, 64),
                                   A(O_QT + (4 * G + r) * 1024 + s * 128, 128,
                                     grp * 64, 64)),
                                inc=(pe if r == 3 else None))
                        ev = t_L
                        if kt >= PFX[s] - 4:
                            j = kt - (PFX[s] - 4)
                            Ev.waitev(t_L)
                            ev = Ev.op(
                                tt(Lp[:, 0:512], Lp[:, 0:512],
                                   Ab(O_MASK + (si * 4 + j) * 256, 512),
                                   ALU.add), inc=vec)
                        return ev

                    pre = emit_L(0)
                    for kt in range(PFX[s]):
                        nxt_pre = (emit_L(kt + 1) if kt + 1 < PFX[s]
                                   else None)
                        g_wait(Ea, f"pbuf{kt % 2}")
                        Ea.waitev(pre)
                        t_exp = Ea.op(
                            actv(A(O_PB + (kt % 2) * 512, 512),
                                 pp[kt % 2][:, 0:512], AF.Exp,
                                 bias=A(O_BIAS + (G * 2 + grp) * 8 + s, 1),
                                 scale=0.125), inc=act)
                        g_set(f"pp{kt % 2}", t_exp)
                        Ep.waitev(t_exp)
                        for r in range(4):
                            t_av = Ep.op(
                                mm(po[0:65, r * 128:r * 128 + 128],
                                   A(O_VG + kt * 130 + grp * 65, 65),
                                   A(O_PB + (kt % 2) * 512 + r * 128, 128),
                                   start=(kt == 0 and r == 0),
                                   stop=(kt == PFX[s] - 1)),
                                inc=(pe if r == 3 else None))
                        g_set(f"pbuf{kt % 2}", t_av)
                        pre = nxt_pre
                    g_wait(Ev, "oev")
                    Ev.waitev(t_av)
                    t_oe = Ev.op(cp(A(O_EV, 512, 0, 65), po[0:65, 0:512]),
                                 inc=vec)
                    g_set(f"pp{4 + ocnt % 2}", t_oe)
                    for r in range(4):
                        h = grp * 4 + r
                        j = 4 * G + r
                        dst_cols = (O_OSB + j * 512 + si * 128, 128)
                        if grp == 0:
                            t_op = Ev.op(cp(A(dst_cols[0], 128, 0, 64),
                                            A(O_EV + r * 128, 128, 0, 64)),
                                         inc=vec)
                        else:
                            Ea.waitev(t_oe)
                            t_op = Ea.dma(A(dst_cols[0], 128, 64, 64),
                                          A(O_EV + r * 128, 128, 0, 64), st)
                        g_wait(Ep, "pp6")
                        Ep.waitev(t_oe)
                        t_tr = Ep.op(trn(pp[6][:, r * 128:r * 128 + 128],
                                         A(O_EV + r * 128, 128)),
                                     inc=(pe if r == 3 else None))
                    Ev.waitev(t_tr)
                    t_dens = None
                    for r in range(4):
                        h = 8 * G + grp * 4 + r
                        t_dens = Ev.op(cp(A(O_DEN + si * 32 + h, 1),
                                          pp[6][:, r * 128 + 64:r * 128 + 65]),
                                       inc=(vec if r == 3 else None))
                    g_set("pp6", t_dens)
                    g_set("oev", t_dens, t_op)
                    ocnt += 1
            g_set("kgvg", (pe, pe.n))
        for si in range(4):
            Ev.op(lambda e: e.memset(A(O_CT + 32, 96), 0.0))
            vsep()
            t_rc = Ev.op(recip(A(O_CT, 32), A(O_DEN + si * 32, 32)), inc=vec)
            vsep()
            g_wait(Ep, "pp6")
            Ep.waitev(t_rc)
            t_tr = Ep.op(trn(pp[6][:, 0:128], A(O_CT, 128)), inc=pe)
            Ev.waitev(t_tr)
            t_dT = Ev.op(cp(A(O_CT + 128, 128, 0, 32), pp[6][0:32, 0:128]),
                         inc=vec)
            g_set("pp6", t_dT)
            for j in range(16):
                g_wait(Ep, "pp7")
                Ep.waitev(t_dT)
                t_rp = Ep.op(mm(pp[7][:, 0:128], A(O_INDR + j * 128, 128, 0, 32),
                                A(O_CT + 128, 128, 0, 32)), inc=pe)
                Ev.waitev(t_rp)
                t_on = Ev.op(tt(A(O_OSB + j * 512 + si * 128, 128),
                                A(O_OSB + j * 512 + si * 128, 128),
                                pp[7][:, 0:128], ALU.mult), inc=vec)
                g_set("pp7", t_on)
        if DBG_SCRATCH[0] and sg == 0:
            Eg.wait(vec, vec.n)
            Eg.dma(dbgA[:, 64:192], A(O_DEN, 128), st)
            Eg.dma(dbgA[:, 1728:2240], A(O_OSB, 512), st)
            Eg.dma(dbgA[:, 2240:2752], A(O_OSB + 512, 512), st)
        t_y_last = None
        for o in range(16):
            wbuf = A(O_WQS + (o % 2) * 2048, 2048)
            g_wait(Es, f"wos{o % 2}")
            Es.wait(vec, vec.n)
            t_w = Es.dma(wbuf, wot[o], ld)
            ppj = pp[o % 2]
            g_wait(Ep, f"pp{o % 2}")
            Ep.wait(ld, t_w[1])
            for k in range(16):
                t_pe = Ep.op(mm(ppj[:, 0:512], wbuf[:, k * 128:k * 128 + 128],
                                A(O_OSB + k * 512, 512),
                                start=(k == 0), stop=(k == 15)),
                             inc=(pe if k == 15 else None))
            g_set(f"wos{o % 2}", t_pe)
            xb = A(O_PB + (o % 2) * 512, 512)
            g_wait(Es, f"xres{o % 2}")
            t_xr = Es.dma(xb, xTm[o][:, sg * 512:sg * 512 + 512], ld)
            Ev.waitev(t_pe)
            Ev.wait(ld, t_xr[1])
            g_wait(Ev, "dbgtap")
            t_v = Ev.op(tt(A(O_YSB + o * 512, 512), ppj[:, 0:512], xb, ALU.add),
                        inc=vec)
            g_set(f"pp{o % 2}", t_v)
            g_set(f"xres{o % 2}", t_v)
            Ea.waitev(t_v)
            g_wait(Ea, "EVsq")
            t_sq = Ea.op(actv(A(O_EV, 512), A(O_YSB + o * 512, 512), AF.Square),
                         inc=act)
            if o == 0:
                g_wait(Ep, "pp2")
                g_wait(Ep, "pp3")
            Ep.waitev(t_v)
            Ep.op(mm(pp[2][0:1, 0:512], ONESC, A(O_YSB + o * 512, 512),
                     start=(o == 0), stop=(o == 15)))
            Ep.waitev(t_sq)
            t_st2 = Ep.op(mm(pp[3][0:1, 0:512], ONESC, A(O_EV, 512),
                             start=(o == 0), stop=(o == 15)), inc=pe)
            g_set("EVsq", t_st2)
            t_y_last = t_st2
        Ev.waitev(t_y_last)
        Ev.op(ts(A(O_ROWA, 512, 0, 1), pp[2][0:1, 0:512], INV_D, ALU.mult))
        Ev.op(ts(A(O_ROWB, 512, 0, 1), pp[3][0:1, 0:512], INV_D, ALU.mult))
        Ev.op(tt(A(O_ROWB + 512, 512, 0, 1), A(O_ROWA, 512, 0, 1),
                 A(O_ROWA, 512, 0, 1), ALU.mult))
        Ev.op(tt(A(O_ROWB, 512, 0, 1), A(O_ROWB, 512, 0, 1),
                 A(O_ROWB + 512, 512, 0, 1), ALU.subtract))
        t_var = Ev.op(ts(A(O_ROWB, 512, 0, 1), A(O_ROWB, 512, 0, 1), 1e-5,
                         ALU.add), inc=vec)
        Ea.waitev(t_var)
        t_sd = Ea.op(actv(A(O_ROWA + 512, 512, 0, 1), A(O_ROWB, 512, 0, 1),
                          AF.Sqrt), inc=act)
        Ev.waitev(t_sd)
        Ev.op(recip(A(O_ROWA + 512, 512, 0, 1), A(O_ROWA + 512, 512, 0, 1)))
        Ev.op(tt(A(O_ROWB + 512, 512, 0, 1), A(O_ROWA + 512, 512, 0, 1),
                 A(O_ROWA + 512, 512, 0, 1), ALU.mult))
        Ev.op(tt(A(O_ROWB + 512, 512, 0, 1), A(O_ROWB + 512, 512, 0, 1),
                 A(O_ROWB, 512, 0, 1), ALU.mult))
        Ev.op(ts(A(O_ROWB + 512, 512, 0, 1), A(O_ROWB + 512, 512, 0, 1),
                 -0.5, ALU.mult, 1.5, ALU.add))
        t_rst = Ev.op(tt(A(O_ROWA + 512, 512, 0, 1), A(O_ROWA + 512, 512, 0, 1),
                         A(O_ROWB + 512, 512, 0, 1), ALU.mult), inc=vec)
        g_wait(Ep, "pp4")
        g_wait(Ep, "pp5")
        Ep.waitev(t_rst)
        t_mr = Ep.op(mm(pp[4][:, 0:512], ONES1, A(O_ROWA, 512, 0, 1)), inc=pe)
        t_rr = Ep.op(mm(pp[5][:, 0:512], ONES1, A(O_ROWA + 512, 512, 0, 1)),
                     inc=pe)
        t_x1 = None
        for o in range(16):
            Ev.waitev(t_rr)
            Ev.op(tt(A(O_YSB + o * 512, 512), A(O_YSB + o * 512, 512),
                     pp[4][:, 0:512], ALU.subtract))
            t_x1 = Ev.op(tt(A(O_YSB + o * 512, 512), A(O_YSB + o * 512, 512),
                            pp[5][:, 0:512], ALU.mult), inc=vec)
            Ea.waitev(t_x1)
            Ea.dma(x1_sc[o][:, sg * 512:sg * 512 + 512], A(O_YSB + o * 512, 512),
                   st)
        g_set("pp4", t_x1)
        g_set("pp5", t_x1)
        Ev.wait(st, st.n)
        for cch in range(16):
            ppj = pp[cch % 2]
            g_wait(Ep, f"pp{cch % 2}")
            Ep.waitev(t_x1)
            t_hh = Ep.op(mm(ppj[:, 0:512], H128ap, A(O_YSB + cch * 512, 512)),
                         inc=pe)
            Ev.waitev(t_hh)
            t_hc = Ev.op(cp(A(O_YSB + cch * 512, 512), ppj[:, 0:512]), inc=vec)
            g_set(f"pp{cch % 2}", t_hc)
        for stg in range(4):
            step = 1 << stg
            run = min(step, 8)
            for cb in range(0, 16, 2 * step):
                for sub in range(0, step, run):
                    a = A(O_YSB + (cb + sub) * 512, run * 512)
                    b = A(O_YSB + (cb + step + sub) * 512, run * 512)
                    Ev.op(tt(a, a, b, ALU.add))
                    t_bf = Ev.op(stt(b, b, -2.0, a, ALU.mult, ALU.add),
                                 inc=vec)
        t_am = Ev.op(red(A(O_EV, 512), A(O_YSB, 8192).rearrange(
            "p (c t) -> p t c", c=16), ALU.max, absval=True), inc=vec)
        for j in range(4):
            g_wait(Ep, "pp6")
            Ep.waitev(t_am)
            t_tr = Ep.op(trn(pp[6][:, 0:128], A(O_EV + j * 128, 128)), inc=pe)
            Ev.waitev(t_tr)
            t_ac = Ev.op(red(A(O_CT + j, 1), pp[6][:, 0:128], ALU.max),
                         inc=vec)
            g_set("pp6", t_ac)
        vsep()
        Ev.op(ts(A(O_CT + 4, 4), A(O_CT, 4), 1e-5, ALU.max,
                 1.0 / 127.0, ALU.mult))
        vsep()
        t_sc1 = Ev.op(recip(A(O_CT + 8, 4), A(O_CT + 4, 4)), inc=vec)
        vsep()
        for j in range(4):
            g_wait(Ep, "pp6")
            Ep.waitev(t_sc1)
            t_tr = Ep.op(trn(pp[6][:, 0:128], A(O_CT + 8 + j, 128)), inc=pe)
            Ev.waitev(t_tr)
            t_rw = Ev.op(cp(A(O_ROW1 + sg * 512 + j * 128, 128, 0, 1),
                            pp[6][0:1, 0:128]), inc=vec)
            g_set("pp6", t_rw)
        g_wait(Ep, "pp7")
        Ep.waitev(t_rw)
        t_rep = Ep.op(mm(pp[7][:, 0:512], ONES1,
                         A(O_ROW1 + sg * 512, 512, 0, 1)), inc=pe)
        t_q1 = None
        for cch in range(16):
            Ev.waitev(t_rep)
            t_q1 = Ev.op(tt(A(O_YSB + cch * 512, 512),
                            A(O_YSB + cch * 512, 512), pp[7][:, 0:512],
                            ALU.mult), inc=(vec if cch == 15 else None))
        g_set("pp7", t_q1)
        for grpq in range(4):
            base = O_YSB + grpq * 2048
            Ev.op(ts(A(O_MASK, 2048), A(base, 2048), 128.5, ALU.add))
            for j in range(7, -1, -1):
                pw = float(1 << j)
                Ev.op(ts(A(O_MASK + 2048, 2048), A(O_MASK, 2048), pw,
                         ALU.is_ge, -pw, ALU.mult))
                Ev.op(tt(A(O_MASK, 2048), A(O_MASK, 2048),
                         A(O_MASK + 2048, 2048), ALU.add))
            Ev.op(tt(A(base, 2048), A(base, 2048), A(O_MASK, 2048),
                     ALU.subtract))
            for q4 in range(4):
                cch = grpq * 4 + q4
                g_wait(Ev, f"q1s{cch % 2}")
                t_q1 = Ev.op(ts(Ab(O_FS2 + (cch % 2) * 256, 512),
                                A(base + q4 * 512, 512), 0.5, ALU.add),
                             inc=vec)
                Ea.waitev(t_q1)
                t_q1s = Ea.dma(xq1_sc[cch][:, sg * 512:sg * 512 + 512],
                               Ab(O_FS2 + (cch % 2) * 256, 512), st)
                g_set(f"q1s{cch % 2}", (st, t_q1s[1]))
        if DBG_SCRATCH[0] and sg == 0:
            Eg.wait(vec, t_q1[1])
            t_tap = Eg.dma(dbgA[:, 3776:3792], A(O_CT, 16), st)
            t_tap = Eg.dma(dbgA[0:1, 3264:3776], A(O_ROW1, 512, 0, 1), st)
            t_tap = Eg.dma(dbgA[:, 2752:3264], A(O_YSB, 512), st)
            g_set("dbgtap", t_tap)
        g_set("maskarea", (st, st.n), t_q1)
    t_d_done = st.n

    pump_pass1(2 * len(p1_loads))
    barrier()
    ev_r = Ev.op(red(A(O_CT, 3),
                     A(O_GACC, 768).rearrange("p (a b) -> p a b", a=3),
                     ALU.add), inc=vec)
    vsep()
    g_wait(Ep, "pp6")
    Ep.waitev(ev_r)
    t_cmb = Ep.op(mm(pp[6][0:1, 0:3], ONESC, A(O_CT, 3)), inc=pe)
    Ev.waitev(t_cmb)
    t_mn = Ev.op(ts(A(O_SVAL, 3, 0, 1), pp[6][0:1, 0:3], 1.0 / (II * D),
                    ALU.mult, 1e-5, ALU.max), inc=vec)
    Ep.waitev(t_mn)
    t_bc = Ep.op(mm(pp[6][:, 0:3], ONES1, A(O_SVAL, 3, 0, 1)), inc=pe)
    Ev.waitev(t_bc)
    Ev.op(ts(A(O_SG, 3), pp[6][:, 0:3], 0.5, ALU.mult))
    t_sg = Ev.op(ts(A(O_SG + 3, 3), pp[6][:, 0:3], -0.5, ALU.mult), inc=vec)
    g_set("pp6", t_sg)
    Ev.op(recip(A(O_ROWA, 1024, 0, 1), A(O_ROW1, 1024, 0, 1)))
    Ev.op(ts(A(O_ROWB, 1024, 0, 1), A(O_ROWA, 1024, 0, 1),
             A(O_SVAL + 1, 1, 0, 1), ALU.mult, INV_SQ1, ALU.mult))
    t_csg = Ev.op(ts(A(O_ROWA, 1024, 0, 1), A(O_ROWA, 1024, 0, 1),
                     A(O_SVAL, 1, 0, 1), ALU.mult, INV_SQ1, ALU.mult), inc=vec)

    if DBG_SCRATCH[0]:
        Eg.wait(vec, vec.n)
        Eg.wait(act, act.n)
        Eg.dma(dbgA[:, 896:1664], A(O_GACC, 768), st)
        Eg.dma(dbgA[:, 1664:1680], A(O_CT, 16), st)
        Eg.dma(dbgA[0:1, 3776:3780], A(O_SVAL, 4, 0, 1), st)
        Eg.dma(dbgA[:, 3780:3788], A(O_SG, 8), st)
        Eg.dma(dbgA[0:1, 3788:3800], A(O_ROWA, 12, 0, 1), st)
        Eg.dma(dbgA[0:1, 3800:3812], A(O_ROWB, 12, 0, 1), st)
        Eg.dma(dbgA[0:1, 3812:3824], A(O_ROW1, 12, 0, 1), st)
    barrier()
    for ha in range(2):
        barrier()
        for k in range(16):
            t_xq = Es.dma(Ab(O_XQ1 + k * 256, 512),
                          xq1_sc[k][:, ha * 512:ha * 512 + 512], ld)
        g_wait(Ep, "pp4")
        Ep.waitev(t_csg)
        t_rg = Ep.op(mm(pp[4][:, 0:512], ONES1,
                        A(O_ROWA + ha * 512, 512, 0, 1)), inc=pe)
        Ev.waitev(t_rg)
        t_rgc = Ev.op(cp(A(O_FS1, 512), pp[4][:, 0:512]), inc=vec)
        g_set("pp4", t_rgc)
        WGo = (O_WG, O_INDR)
        WUo = (O_WU, O_WU1)
        tern_ev = {}

        def f1_load(ic):
            if ic >= 64:
                return None
            g_wait(Es, f"wg{ic % 2}")
            t_wg = Es.dma(A(WGo[ic % 2], 2048), wgt[ic], ld)
            g_wait(Es, f"wu{ic % 2}")
            t_wu = Es.dma(A(WUo[ic % 2], 2048), wut[ic], ld)
            return (t_wg, t_wu)

        def f1_tern(ic, tw):
            if ic >= 64:
                return
            for mat in range(2):
                toff = (WGo if mat == 0 else WUo)[ic % 2]
                src = A(toff, 2048)
                sgc = 0 if mat == 0 else 1
                Ev.wait(ld, tw[mat][1])
                Ev.waitev(t_sg)
                for q in range(4):
                    Ev.op(ts(pp[6][:, 0:512], src[:, q * 512:q * 512 + 512],
                             A(O_SG + sgc, 1), ALU.is_gt))
                    Ev.op(ts(A(O_FS3, 512), src[:, q * 512:q * 512 + 512],
                             A(O_SG + 3 + sgc, 1), ALU.is_ge))
                    t_tn = Ev.op(stt(Ab(toff + q * 256, 512), pp[6][:, 0:512],
                                     -1.0, A(O_FS3, 512), ALU.add, ALU.add),
                                 inc=vec)
                tern_ev[(mat, ic)] = t_tn

        pend = f1_load(0)
        f1_tern(0, pend)
        nxt = f1_load(1)
        for ic in range(64):
            t_tg = tern_ev[(0, ic)]
            t_tu = tern_ev[(1, ic)]
            ppg = pp[ic % 2]
            ppu = pp[2 + (ic % 2)]
            g_wait(Ep, f"ppg{ic % 2}")
            g_wait(Ep, f"ppu{ic % 2}")
            Ep.waitev(t_tg)
            Ep.waitev(t_tu)
            Ep.wait(ld, t_xq[1])
            for k in range(16):
                Ep.op(mm(ppg[:, 0:512],
                         Ab(WGo[ic % 2] + k * 64, 128),
                         Ab(O_XQ1 + k * 256, 512),
                         start=(k == 0), stop=(k == 15)),
                      inc=(pe if k == 15 else None))
            t_gd = (pe, pe.n)
            for k in range(16):
                Ep.op(mm(ppu[:, 0:512], Ab(WUo[ic % 2] + k * 64, 128),
                         Ab(O_XQ1 + k * 256, 512),
                         start=(k == 0), stop=(k == 15)),
                      inc=(pe if k == 15 else None))
            t_ud = (pe, pe.n)
            g_set(f"wg{ic % 2}", t_gd)
            g_set(f"wu{ic % 2}", t_ud)
            Ev.waitev(t_gd)
            Ev.waitev(t_rgc)
            t_gs = Ev.op(tt(A(O_SH + ic * 512, 512), ppg[:, 0:512],
                            A(O_FS1, 512), ALU.mult), inc=vec)
            g_set(f"ppg{ic % 2}", t_gs)
            Ea.waitev(t_gs)
            g_wait(Ea, "FS2")
            t_si = Ea.op(actv(A(O_FS2, 512), A(O_SH + ic * 512, 512), AF.Silu),
                         inc=act)
            f1_tern(ic + 1, nxt)
            nxt = f1_load(ic + 2)
            Ev.waitev(t_si)
            Ev.waitev(t_ud)
            t_h = Ev.op(tt(A(O_SH + ic * 512, 512), A(O_FS2, 512),
                           ppu[:, 0:512], ALU.mult), inc=vec)
            g_set(f"ppu{ic % 2}", t_h)
            g_set("FS2", t_h)
        if DBG_SCRATCH[0] and ha == 0:
            Eg.wait(vec, vec.n)
            t_htap = Eg.dma(dbgA[:, 3264:3776], A(O_SH, 512), st)
            g_set("htap", t_htap)
        Ev.wait(pe, pe.n)
        g_wait(Ev, "htap")
        for cch in range(64):
            ppj = pp[4 + cch % 2]
            g_wait(Ep, f"pp{4 + cch % 2}")
            Ep.wait(vec, vec.n)
            t_hh = Ep.op(mm(ppj[:, 0:512], H128ap, A(O_SH + cch * 512, 512)),
                         inc=pe)
            Ev.waitev(t_hh)
            t_hc = Ev.op(cp(A(O_SH + cch * 512, 512), ppj[:, 0:512]), inc=vec)
            g_set(f"pp{4 + cch % 2}", t_hc)
        for stg in range(6):
            step = 1 << stg
            run = min(step, 8)
            for cb in range(0, 64, 2 * step):
                for sub in range(0, step, run):
                    a = A(O_SH + (cb + sub) * 512, run * 512)
                    b = A(O_SH + (cb + step + sub) * 512, run * 512)
                    Ev.op(tt(a, a, b, ALU.add))
                    t_bf = Ev.op(stt(b, b, -2.0, a, ALU.mult, ALU.add),
                                 inc=vec)
        Ev.wait(pe, pe.n)
        t_am = Ev.op(red(A(O_STGV, 512), A(O_SH, 32768).rearrange(
            "p (c t) -> p t c", c=64), ALU.max, absval=True), inc=vec)
        for j in range(4):
            g_wait(Ep, "pp6")
            Ep.waitev(t_am)
            t_tr = Ep.op(trn(pp[6][:, 0:128], A(O_STGV + j * 128, 128)), inc=pe)
            Ev.waitev(t_tr)
            t_ac = Ev.op(red(A(O_CT + j, 1), pp[6][:, 0:128], ALU.max), inc=vec)
            g_set("pp6", t_ac)
        vsep()
        Ev.op(ts(A(O_CT + 4, 4), A(O_CT, 4), 1e-5, ALU.max,
                 1.0 / 127.0, ALU.mult))
        vsep()
        t_sc2 = Ev.op(recip(A(O_CT + 8, 4), A(O_CT + 4, 4)), inc=vec)
        vsep()
        for j in range(4):
            g_wait(Ep, "pp6")
            Ep.waitev(t_sc2)
            t_tr = Ep.op(trn(pp[6][:, 0:128], A(O_CT + 8 + j, 128)), inc=pe)
            Ev.waitev(t_tr)
            t_rw = Ev.op(cp(A(O_ROW2 + ha * 512 + j * 128, 128, 0, 1),
                            pp[6][0:1, 0:128]), inc=vec)
            g_set("pp6", t_rw)
        g_wait(Ep, "pp7")
        Ep.waitev(t_rw)
        t_rep = Ep.op(mm(pp[7][:, 0:512], ONES1,
                         A(O_ROW2 + ha * 512, 512, 0, 1)), inc=pe)
        Ev.waitev(t_rep)
        t_repc = Ev.op(cp(A(O_FS1, 512), pp[7][:, 0:512]), inc=vec)
        g_set("pp7", t_repc)
        for cch in range(64):
            Ev.waitev(t_repc)
            Ev.op(tt(A(O_SH + cch * 512, 512), A(O_SH + cch * 512, 512),
                     A(O_FS1, 512), ALU.mult))
        for grpq in range(16):
            base = O_SH + grpq * 2048
            Ev.op(ts(A(O_UU, 2048), A(base, 2048), 128.5, ALU.add))
            for j in range(7, -1, -1):
                pw = float(1 << j)
                Ev.op(ts(A(O_MM, 2048), A(O_UU, 2048), pw, ALU.is_ge,
                         -pw, ALU.mult))
                Ev.op(tt(A(O_UU, 2048), A(O_UU, 2048), A(O_MM, 2048),
                         ALU.add))
            Ev.op(tt(A(base, 2048), A(base, 2048), A(O_UU, 2048),
                     ALU.subtract))
            for q4 in range(4):
                cch = grpq * 4 + q4
                g_wait(Ev, f"q2s{cch % 2}")
                t_q2 = Ev.op(ts(Ab(O_FS2 + (cch % 2) * 256, 512),
                                A(base + q4 * 512, 512), 0.5, ALU.add),
                             inc=vec)
                Ea.waitev(t_q2)
                t_q2s = Ea.dma(xq2_sc[cch][:, ha * 512:ha * 512 + 512],
                               Ab(O_FS2 + (cch % 2) * 256, 512), st)
                g_set(f"q2s{cch % 2}", (st, t_q2s[1]))
    t_f2_done = st.n

    barrier()
    Ev.op(recip(A(O_ROWA, 1024, 0, 1), A(O_ROW2, 1024, 0, 1)))
    Ev.op(ts(A(O_ROWA, 1024, 0, 1), A(O_ROWA, 1024, 0, 1),
             A(O_SVAL + 2, 1, 0, 1), ALU.mult, INV_SQ2, ALU.mult))
    t_cd = Ev.op(tt(A(O_ROWA, 1024, 0, 1), A(O_ROWA, 1024, 0, 1),
                    A(O_ROWB, 1024, 0, 1), ALU.mult), inc=vec)

    for oblk in range(4):
        for kg in range(16):
            wd = A(O_WDS + (kg % 2) * 2048, 2048)
            g_wait(Es, f"wds{kg % 2}")
            t_wd = None
            for oo in range(4):
                t_wd = Es.dma(wd[:, oo * 512:oo * 512 + 512],
                              wdt[oblk * 4 + oo][:, kg * 512:kg * 512 + 512], ld)
            Ev.wait(ld, t_wd[1])
            Ev.waitev(t_sg)
            for oo in range(4):
                g_wait(Ev, "lnst")
                Ev.op(ts(A(O_LNS, 512), wd[:, oo * 512:oo * 512 + 512],
                         A(O_SG + 2, 1), ALU.is_gt))
                Ev.op(ts(A(O_LNS + 512, 512), wd[:, oo * 512:oo * 512 + 512],
                         A(O_SG + 5, 1), ALU.is_ge))
                t_tn = Ev.op(stt(Ab(O_WDS + (kg % 2) * 2048 + oo * 512, 512),
                                 A(O_LNS, 512), -1.0, A(O_LNS + 512, 512),
                                 ALU.add, ALU.add), inc=vec)
            g_set(f"terd{kg % 2}", t_tn)
            t_mmlast = None
            for kk in range(4):
                xb = Ab(O_XQ2B + (kk % 2) * 512, 1024)
                g_wait(Es, f"xq2b{kk % 2}")
                t_xq2 = Es.dma(xb, xq2_sc[kg * 4 + kk], ld)
                Ep.wait(ld, t_xq2[1])
                Ep.waitev(t_tn)
                if kg == 0 and kk == 0:
                    for jj in range(8):
                        g_wait(Ep, f"pp{jj}")
                for oo in range(4):
                    for half in range(2):
                        t_mmlast = Ep.op(
                            mm(pp[oo * 2 + half][:, 0:512],
                               Ab(O_WDS + (kg % 2) * 2048 + oo * 512
                                  + kk * 64, 128),
                               xb[:, half * 512:half * 512 + 512],
                               start=(kg == 0 and kk == 0),
                               stop=(kg == 15 and kk == 3)),
                            inc=(pe if (oo == 3 and half == 1) else None))
                g_set(f"xq2b{kk % 2}", t_mmlast)
            g_set(f"wds{kg % 2}", t_mmlast)
        for oo in range(4):
            o = oblk * 4 + oo
            Ev.waitev(t_mmlast)
            for half in range(2):
                t_dn = Ev.op(cp(A(O_DNS + o * 1024 + half * 512, 512),
                                pp[oo * 2 + half][:, 0:512]), inc=vec)
        for jj in range(8):
            g_set(f"pp{jj}", t_dn)
    g_wait(Ep, "pp4")
    Ep.waitev(t_cd)
    Ep.wait(vec, vec.n)
    t_cr = Ep.op(mm(pp[4][:, 0:512], ONES1, A(O_ROWA, 512, 0, 1)), inc=pe)
    t_cr2 = Ep.op(mm(pp[5][:, 0:512], ONES1, A(O_ROWA + 512, 512, 0, 1)),
                  inc=pe)
    t_stats = None
    for o in range(16):
        xb = A(O_X1B + (o % 2) * 1024, 1024)
        g_wait(Es, f"x1b{o % 2}")
        t_x1r = Es.dma(xb, x1_sc[o], ld)
        Ev.waitev(t_cr2)
        Ev.wait(ld, t_x1r[1])
        for half in range(2):
            dn = A(O_DNS + o * 1024 + half * 512, 512)
            Ev.op(tt(dn, dn, pp[4 + half][:, 0:512], ALU.mult))
            t_v2 = Ev.op(tt(dn, dn, xb[:, half * 512:half * 512 + 512],
                            ALU.add), inc=vec)
            g_set(f"x1b{o % 2}", t_v2)
            Ea.waitev(t_v2)
            g_wait(Ea, "lnsq")
            t_sq = Ea.op(actv(A(O_LNS, 512), dn, AF.Square), inc=act)
            if o == 0 and half == 0:
                g_wait(Ep, "pp2")
                g_wait(Ep, "pp3")
                g_wait(Ep, "pp6")
                g_wait(Ep, "pp7")
            Ep.waitev(t_v2)
            Ep.op(mm(pp[2 + half][0:1, 0:512], ONESC, dn,
                     start=(o == 0), stop=(o == 15)))
            Ep.waitev(t_sq)
            t_stats = Ep.op(mm(pp[6 + half][0:1, 0:512], ONESC, A(O_LNS, 512),
                               start=(o == 0), stop=(o == 15)), inc=pe)
            g_set("lnsq", t_stats)
    Ev.waitev(t_stats)
    for half in range(2):
        ro = half * 512
        Ev.op(ts(A(O_ROWA + ro, 512, 0, 1), pp[2 + half][0:1, 0:512], INV_D,
                 ALU.mult))
        Ev.op(ts(A(O_ROWB + ro, 512, 0, 1), pp[6 + half][0:1, 0:512], INV_D,
                 ALU.mult))
        Ev.op(tt(A(O_ROW1 + ro, 512, 0, 1), A(O_ROWA + ro, 512, 0, 1),
                 A(O_ROWA + ro, 512, 0, 1), ALU.mult))
        Ev.op(tt(A(O_ROWB + ro, 512, 0, 1), A(O_ROWB + ro, 512, 0, 1),
                 A(O_ROW1 + ro, 512, 0, 1), ALU.subtract))
        t_var = Ev.op(ts(A(O_ROWB + ro, 512, 0, 1), A(O_ROWB + ro, 512, 0, 1),
                         1e-5, ALU.add), inc=vec)
        Ea.waitev(t_var)
        t_sd = Ea.op(actv(A(O_ROW2 + ro, 512, 0, 1), A(O_ROWB + ro, 512, 0, 1),
                          AF.Sqrt), inc=act)
        Ev.waitev(t_sd)
        Ev.op(recip(A(O_ROW2 + ro, 512, 0, 1), A(O_ROW2 + ro, 512, 0, 1)))
        Ev.op(tt(A(O_ROW1 + ro, 512, 0, 1), A(O_ROW2 + ro, 512, 0, 1),
                 A(O_ROW2 + ro, 512, 0, 1), ALU.mult))
        Ev.op(tt(A(O_ROW1 + ro, 512, 0, 1), A(O_ROW1 + ro, 512, 0, 1),
                 A(O_ROWB + ro, 512, 0, 1), ALU.mult))
        Ev.op(ts(A(O_ROW1 + ro, 512, 0, 1), A(O_ROW1 + ro, 512, 0, 1),
                 -0.5, ALU.mult, 1.5, ALU.add))
        t_rst = Ev.op(tt(A(O_ROW2 + ro, 512, 0, 1), A(O_ROW2 + ro, 512, 0, 1),
                         A(O_ROW1 + ro, 512, 0, 1), ALU.mult), inc=vec)
    g_set("pp2", t_rst)
    g_set("pp3", t_rst)
    g_set("pp6", t_rst)
    g_set("pp7", t_rst)
    g_wait(Ep, "pp2")
    Ep.waitev(t_rst)
    t_mr0 = Ep.op(mm(pp[2][:, 0:512], ONES1, A(O_ROWA, 512, 0, 1)), inc=pe)
    t_mr1 = Ep.op(mm(pp[3][:, 0:512], ONES1, A(O_ROWA + 512, 512, 0, 1)),
                  inc=pe)
    t_rr0 = Ep.op(mm(pp[6][:, 0:512], ONES1, A(O_ROW2, 512, 0, 1)), inc=pe)
    t_rr1 = Ep.op(mm(pp[7][:, 0:512], ONES1, A(O_ROW2 + 512, 512, 0, 1)),
                  inc=pe)
    t_x2 = None
    for o in range(16):
        Ev.waitev(t_rr1)
        for half in range(2):
            dn = A(O_DNS + o * 1024 + half * 512, 512)
            Ev.op(tt(dn, dn, pp[2 + half][:, 0:512], ALU.subtract))
            t_x2 = Ev.op(tt(dn, dn, pp[6 + half][:, 0:512], ALU.mult), inc=vec)
    barrier()
    t_out = None
    for tts in range(8):
        g_wait(Ev, "outa")
        for o in range(16):
            ppj = pp[o % 2]
            g_wait(Ep, f"ppt{o % 2}")
            Ep.waitev(t_x2)
            t_tr = Ep.op(trn(ppj[:, 0:128],
                             A(O_DNS + o * 1024 + tts * 128, 128)), inc=pe)
            Ev.waitev(t_tr)
            t_oc = Ev.op(cp(A(O_OUTA + o * 128, 128), ppj[:, 0:128]), inc=vec)
            g_set(f"ppt{o % 2}", t_oc)
        Eg.waitev(t_oc)
        t_out = Eg.dma(out[tts * 128:tts * 128 + 128, :], A(O_OUTA, 2048), st)
        g_set("outa", t_out)
    Eg.wait(st, st.n)

    with nc.Block() as block:
        @block.sync
        def _(eng):
            for f in Es.ops:
                f(eng)

        @block.tensor
        def _(eng):
            for f in Ep.ops:
                f(eng)

        @block.vector
        def _(eng):
            for f in Ev.ops:
                f(eng)

        @block.scalar
        def _(eng):
            for f in Ea.ops:
                f(eng)

        @block.gpsimd
        def _(eng):
            for f in Eg.ops:
                f(eng)

    ctx.close()
    return nc


_NC_CACHE = []


def kernel(**inputs):
    x = np.asarray(inputs["L0_attn_hidden_states"], dtype=np.float32)
    Wq = np.asarray(inputs["L0_attn_Wq"], dtype=np.float32)
    Wk = np.asarray(inputs["L0_attn_Wk"], dtype=np.float32)
    Wv = np.asarray(inputs["L0_attn_Wv"], dtype=np.float32)
    Wo = np.asarray(inputs["L0_attn_Wo"], dtype=np.float32)
    wg = np.asarray(inputs["L0_ffn_w_gate"], dtype=np.float32)
    wu = np.asarray(inputs["L0_ffn_w_up"], dtype=np.float32)
    wd = np.asarray(inputs["L0_ffn_w_down"], dtype=np.float32)

    nc = build_nc()
    in_maps, order_all = _stage(x, Wq, Wk, Wv, Wo, wg, wu, wd)
    res = run_bass_kernel_spmd(nc, in_maps, core_ids=list(range(8)))
    y = np.zeros((2, S, D), dtype=np.float32)
    for core in range(8):
        b, cols = order_all[core]
        y[b, cols, :] = np.asarray(res.results[core]["out"])
    return y


def _stage(x, Wq, Wk, Wv, Wo, wg, wu, wd):
    perm = _qperm()
    wqtS = _strips(np.ascontiguousarray(Wq.T[:, perm]))
    wktS = _strips(np.ascontiguousarray(Wk.T))
    wvtS = _strips(np.ascontiguousarray(Wv.T))
    wotS = _strips(np.ascontiguousarray(Wo.T[perm, :]))
    wgtS = _strips(np.ascontiguousarray(wg.T))
    wutS = _strips(np.ascontiguousarray(wu.T))
    wdtS = _strips(np.ascontiguousarray(wd.T))
    cosA, sinA = _rope_tables()
    h128 = _hadamard(128).astype(np.float32)
    ident = np.eye(128, dtype=np.float32)
    p64 = np.zeros((64, 64), dtype=np.float32)
    p64[:32, 32:] = -np.eye(32)
    p64[32:, :32] = np.eye(32)
    p2 = np.zeros((128, 128), dtype=np.float32)
    p2[:64, :64] = p64.T
    p2[64:, 64:] = p64.T
    ones1 = np.ones((1, 128), dtype=np.float32)
    onescol = np.ones((128, 1), dtype=np.float32)
    indq = np.zeros((128, 512), dtype=np.float32)
    for j in range(16):
        G, r = j // 4, j % 4
        hA, hB = 8 * G + r, 8 * G + 4 + r
        indq[0:64, j * 32 + hA] = 1.0
        indq[64:128, j * 32 + hB] = 1.0
    indk = np.zeros((128, 32), dtype=np.float32)
    for o in range(4):
        indk[0:64, o * 8 + 2 * o] = 1.0
        indk[64:128, o * 8 + 2 * o + 1] = 1.0
    indkq = np.zeros((8, 32), dtype=np.float32)
    for h in range(32):
        indkq[h // 4, h] = 1.0
    indrep = np.zeros((32, 2048), dtype=np.float32)
    for j in range(16):
        G, r = j // 4, j % 4
        hA, hB = 8 * G + r, 8 * G + 4 + r
        indrep[hA, j * 128 + np.arange(0, 64)] = 1.0
        indrep[hB, j * 128 + np.arange(64, 128)] = 1.0
    indgrp = np.zeros((8, 1024), dtype=np.float32)
    for g8 in range(8):
        indgrp[g8, g8 * 128:(g8 + 1) * 128] = 1.0
    in_maps = []
    order_all = []
    for core in range(8):
        b, c = core // 4, core % 4
        qt = _core_qtiles(c)
        cols = np.concatenate([np.arange(t * 128, (t + 1) * 128) for t in qt])
        order_all.append((b, cols))
        xTb = np.ascontiguousarray(x[b].T)
        xTm = np.ascontiguousarray(xTb[:, cols]).reshape(16, 128, NT)
        xTf = xTb.reshape(16, 128, S)
        in_maps.append({
            "xTm": xTm, "xTf": xTf,
            "wqt": wqtS, "wkt": wktS, "wvt": wvtS, "wot": wotS,
            "wgt": wgtS, "wut": wutS, "wdt": wdtS,
            "cosq": np.ascontiguousarray(cosA[:, cols]),
            "sinq": np.ascontiguousarray(sinA[:, cols]),
            "cosk": cosA, "sink": sinA,
            "masks": _masks_for_core(c),
            "h128": h128, "ident": ident, "p2": p2,
            "ones1": ones1, "onescol": onescol,
            "indq": indq, "indk": indk, "indkq": indkq,
            "indrep": indrep, "indgrp": indgrp,
        })
    return in_maps, order_all


# version 06
# speedup vs baseline: 1.1558x; 1.0198x over previous
from contextlib import ExitStack

import numpy as np
import ml_dtypes

import concourse.bass as bass
import concourse.mybir as mybir
from concourse.bass_utils import run_bass_kernel_spmd

F32 = mybir.dt.float32
BF16 = mybir.dt.bfloat16
AF = mybir.ActivationFunctionType
ALU = mybir.AluOpType
AX = mybir.AxisListType

S, D, HD, II, NT, NSLOT = 4096, 2048, 64, 8192, 1024, 8
PFX = [4, 8, 12, 16, 20, 24, 28, 32]
MASK_NEG = -1.0e30
INV_D = 1.0 / 2048.0
INV_SQ1 = float(2048.0 ** -0.5)
INV_SQ2 = float(8192.0 ** -0.5)
ARENA = 52224

O_H128, O_ID, O_P2, O_ONES1, O_ONESC = 0, 128, 256, 384, 512
O_SVAL = 516
O_SG = 520
O_KM = 528
O_KN = 536
O_CG = 568
O_IKQ = 576
O_INDQ = 640
O_INDR = 1152
O_INDG = 3200
O_BIAS = 4224
O_CT = 4288
O_ROW1 = 4544
O_ROW2 = 5568
O_ROWA = 6592
O_ROWB = 7616
O_GACC = 8640
O_FS1 = 9472
O_FS2 = 9984
O_FS3 = 10496
O_KVW = 11008
O_XS = 27500
O_CK, O_SK, O_VSTG = 35800, 36312, 36824
O_KA, O_KB, O_KC, O_KNS = 37400, 37912, 38424, 38936
O_QT = 11008
O_XMH = 27500
O_WQS = 35800
O_CQ, O_SQ = 39900, 40924
O_QN = 41950
O_SA, O_SB, O_SC = 42980, 43492, 44004
O_OSB = 27500
O_MASK = 35800
O_KG = 39900
O_VG = 44000
O_YSB = 39900
O_PB = 48200
O_EV = 49224
O_DEN = 49740
O_P1 = 50176
O_P1D = 51200
O_XQ1 = 11008
O_WG, O_WU = 15104, 17152
O_WU1 = 3200
O_SH = 19200
O_STGV, O_STGU, O_STGM = 13056, 13568, 14080
O_STGQ = 14592
O_UU, O_MM = 13056, 11008
O_RS2 = 6080
O_DNS = 11008
O_WDS = 27500
O_XQ2B = 31700
O_X1B = 32800
O_OUTA = 34900
O_LNS = 37000


def _hadamard(n):
    h = np.array([[1.0]])
    while h.shape[0] < n:
        h = np.block([[h, h], [h, -h]])
    return h


def _qperm():
    order = []
    for G in range(4):
        for r in range(4):
            for h in (8 * G + r, 8 * G + 4 + r):
                order.extend(range(h * HD, (h + 1) * HD))
    return np.array(order)


def _core_qtiles(c):
    return [c, 7 - c, 8 + c, 15 - c, 16 + c, 23 - c, 24 + c, 31 - c]


def _strips(aT, pi=128):
    K, M = aT.shape
    t = aT.reshape(K // 128, 128, M // pi, pi)
    return np.ascontiguousarray(
        t.transpose(2, 1, 0, 3).reshape(M // pi, 128, (K // 128) * pi))


def _rope_tables():
    inv = 1.0 / (10000.0 ** (np.arange(0, HD, 2, dtype=np.float64) / HD))
    pos = np.arange(S, dtype=np.float64)
    fr = pos[:, None] * inv[None, :]
    emb = np.concatenate([fr, fr], axis=1)
    cos = np.cos(emb).T.astype(np.float32)
    sin = np.sin(emb).T.astype(np.float32)
    return (np.ascontiguousarray(np.concatenate([cos, cos], 0)),
            np.ascontiguousarray(np.concatenate([sin, sin], 0)))


def _masks_for_core(c):
    qt = _core_qtiles(c)
    m = np.zeros((NSLOT, 4, 128, 512), dtype=np.float32)
    for s in range(NSLOT):
        for j in range(4):
            kt = PFX[s] - 4 + j
            kpos = kt * 128 + np.arange(128)[:, None]
            qpos = qt[s] * 128 + np.arange(128)[None, :]
            m[s, j] = np.tile(np.where(kpos > qpos, MASK_NEG, 0.0), (1, 4))
    return m.astype(ml_dtypes.bfloat16)


class Sem:
    def __init__(self, h):
        self.h = h
        self.n = 0

    def bump(self, k=1):
        self.n += k
        return self.n


class E:
    def __init__(self, name):
        self.name = name
        self.ops = []
        self.seen = {}

    def wait(self, sem, val):
        if val is None or val <= 0:
            return
        if self.seen.get(id(sem), -1) >= val:
            return
        self.seen[id(sem)] = val
        self.ops.append(lambda e, s=sem.h, v=val: e.wait_ge(s, v))

    def waitev(self, ev):
        if ev is not None:
            self.wait(ev[0], ev[1])

    def op(self, fn, inc=None, k=1):
        if inc is None:
            self.ops.append(fn)
            return None
        t = inc.bump(k)
        self.ops.append(lambda e, f=fn, s=inc.h, kk=k: f(e).then_inc(s, kk))
        return (inc, t)

    def dma(self, dst, src, sem):
        return self.op(lambda e, d=dst, s=src: e.dma_start(out=d, in_=s),
                       inc=sem, k=16)


DBG_SCRATCH = [False]


def build_nc():
    nc = bass.Bass("TRN2", target_bir_lowering=False)

    def dinp(name, shape, dt=F32):
        return nc.dram_tensor(name, list(shape), dt, kind="ExternalInput").ap()

    xTm = dinp("xTm", [16, 128, NT])
    xTf = dinp("xTf", [16, 128, S])
    wqt = dinp("wqt", [16, 128, 2048])
    wkt = dinp("wkt", [4, 128, 2048])
    wvt = dinp("wvt", [4, 128, 2048])
    wot = dinp("wot", [16, 128, 2048])
    wgt = dinp("wgt", [64, 128, 2048])
    wut = dinp("wut", [64, 128, 2048])
    wdt = dinp("wdt", [16, 128, 8192])
    cosq = dinp("cosq", [128, NT])
    sinq = dinp("sinq", [128, NT])
    cosk = dinp("cosk", [128, S])
    sink = dinp("sink", [128, S])
    masksI = dinp("masks", [NSLOT, 4, 128, 512], BF16)
    c_h128 = dinp("h128", [128, 128])
    c_ident = dinp("ident", [128, 128])
    c_p2 = dinp("p2", [128, 128])
    c_ones1 = dinp("ones1", [1, 128])
    c_onescol = dinp("onescol", [128, 1])
    c_indq = dinp("indq", [128, 512])
    c_indk = dinp("indk", [128, 32])
    c_indkq = dinp("indkq", [8, 32])
    c_indrep = dinp("indrep", [32, 2048])
    c_indgrp = dinp("indgrp", [8, 1024])

    kd = "ExternalOutput" if DBG_SCRATCH[0] else "Internal"
    kt_sc = nc.dram_tensor("kt_sc", [4, 128, S], F32, kind=kd).ap()
    vt_sc = nc.dram_tensor("vt_sc", [4, 128, 32 * 130], F32, kind=kd).ap()
    x1_sc = nc.dram_tensor("x1_sc", [16, 128, NT], F32, kind=kd).ap()
    xq1_sc = nc.dram_tensor("xq1_sc", [16, 128, NT], BF16, kind=kd).ap()
    xq2_sc = nc.dram_tensor("xq2_sc", [64, 128, NT], BF16, kind=kd).ap()
    out = nc.dram_tensor("out", [NT, D], F32, kind="ExternalOutput").ap()
    dbgA = nc.dram_tensor("dbgA", [128, 4096], F32,
                          kind=("ExternalOutput" if DBG_SCRATCH[0] else "Internal")).ap()

    ctx = ExitStack()
    ar = ctx.enter_context(nc.sbuf_tensor("arena", [128, ARENA], F32))
    s_indk = ctx.enter_context(nc.sbuf_tensor("s_indk", [128, 32], F32))
    arB = ar[:].bitcast(BF16)

    def A(off, cols, p0=0, pn=128):
        return ar[p0:p0 + pn, off:off + cols]

    def Ab(off_f32, cols_bf16, p0=0, pn=128):
        return arB[p0:p0 + pn, 2 * off_f32:2 * off_f32 + cols_bf16]

    pp = [ctx.enter_context(nc.psum_tensor(f"pp{j}", [128, 512], F32))
          for j in range(8)]

    sems = {k: Sem(ctx.enter_context(nc.semaphore(k)))
            for k in ("ld", "st", "pe", "vec", "act", "gp", "gld")}
    ld, st, pe, vec, act, gp, gld = (sems[k] for k in
                                     ("ld", "st", "pe", "vec", "act", "gp", "gld"))

    Es, Ep, Ev, Ea, Eg = E("sync"), E("tensor"), E("vector"), E("scalar"), E("gpsimd")
    ALL_E = (Es, Ep, Ev, Ea, Eg)

    guard = {}

    def g_wait(Eng, key):
        for ev in guard.get(key, ()):
            Eng.waitev(ev)

    def g_set(key, *evs):
        guard[key] = [e for e in evs if e is not None]

    def barrier():
        evs = [(pe, pe.n), (vec, vec.n), (act, act.n), (st, st.n),
               (ld, ld.n), (gp, gp.n)]
        for Eng in ALL_E:
            for ev in evs:
                Eng.waitev(ev)

    def mm(out_ap, lhsT, rhs, start=True, stop=True):
        return lambda e: e.matmul(out_ap, lhsT, rhs, start=start, stop=stop)

    def cp(dst, src):
        return lambda e: e.tensor_copy(dst, src)

    def tt(dst, a, b, op):
        return lambda e: e.tensor_tensor(dst, a, b, op)

    def ts(dst, a, s1, op0, s2=None, op1=None):
        if op1 is None:
            return lambda e: e.tensor_scalar(dst, a, s1, None, op0)
        return lambda e: e.tensor_scalar(dst, a, s1, s2, op0, op1)

    def stt(dst, in0, scalar, in1, op0, op1):
        return lambda e: e.scalar_tensor_tensor(dst, in0, scalar, in1, op0, op1)

    def actv(dst, src, func, bias=0.0, scale=1.0):
        return lambda e: e.activation(out=dst, in_=src, func=func,
                                      bias=bias, scale=scale)

    def red(dst, src, op, absval=False, axis=None):
        ax = AX.X if axis is None else axis
        return lambda e: e.tensor_reduce(dst, src, ax, op,
                                         apply_absolute_value=absval)

    def recip(dst, src):
        return lambda e: e.reciprocal(dst, src)

    def trn(out_ap, in_ap):
        return lambda e: e.transpose(out_ap, in_ap, ar[0:128, O_ID:O_ID + 128])

    def vsep():
        Ev.op(cp(A(O_CT + 254, 1), A(O_ONESC, 1)))

    H128ap = A(O_H128, 128)
    P2ap = A(O_P2, 128)
    ONES1 = A(O_ONES1, 128, 0, 1)
    ONESC = A(O_ONESC, 1)

    for dst, src in [(A(O_H128, 128), c_h128), (A(O_ID, 128), c_ident),
                     (A(O_P2, 128), c_p2), (ONES1, c_ones1), (ONESC, c_onescol),
                     (A(O_INDQ, 512), c_indq), (s_indk[:], c_indk),
                     (A(O_IKQ, 32, 0, 8), c_indkq),
                     (A(O_INDR, 2048, 0, 32), c_indrep),
                     (A(O_INDG, 1024, 0, 8), c_indgrp)]:
        Es.dma(dst, src[:], ld)
    t_const = ld.n
    for r in range(4):
        Eg.op(lambda e, rr=r: e.memset(A(O_VSTG + rr * 130 + 64, 1), 1.0))
        Eg.op(lambda e, rr=r: e.memset(A(O_VSTG + rr * 130 + 129, 1), 1.0))
    Eg.op(lambda e: e.memset(A(O_GACC, 768), 0.0), inc=gp)

    for j, src in enumerate([wkt, wvt]):
        for o in range(4):
            Es.dma(A(O_KVW + (j * 4 + o) * 2048, 2048), src[o], ld)
    t_kvw = ld.n

    def kvw(is_v, o, k):
        return A(O_KVW + ((4 if is_v else 0) + o) * 2048 + k * 128, 128)

    p1_loads = []
    for wsrc, nslab, wcols in ((wgt, 64, 2048), (wut, 64, 2048), (wdt, 16, 8192)):
        for sl in range(nslab):
            for hh in range(wcols // 512):
                p1_loads.append(wsrc[sl][:, hh * 512:hh * 512 + 512])
    p1_state = {"ld": 0, "cons": 0, "consev": {}, "ldev": {}}

    def actv_acc(dst, srcq, func, accum):
        return lambda e: e.activation(out=dst, in_=srcq, func=func,
                                      bias=0.0, scale=1.0, accum_out=accum)

    def pump_pass1(n=1):
        for _ in range(n):
            i = p1_state["ld"]
            if i < len(p1_loads):
                if i >= 2:
                    Eg.waitev(p1_state["consev"].get(i - 2))
                p1_state["ldev"][i] = Eg.dma(A(O_P1 + (i % 2) * 512, 512),
                                             p1_loads[i], gld)
                p1_state["ld"] += 1
            j = p1_state["cons"]
            if j < len(p1_loads) and j < p1_state["ld"]:
                Ea.waitev(p1_state["ldev"][j])
                ev = Ea.op(actv_acc(A(O_P1D, 512), A(O_P1 + (j % 2) * 512, 512),
                                    AF.Abs, A(O_GACC + j, 1)), inc=act)
                p1_state["consev"][j] = ev
                p1_state["cons"] += 1

    def xchunk(k):
        return A(O_XS + (k // 8) * 4096 + (k % 8) * 512, 512)

    Ep.wait(ld, t_kvw)
    last_x_use = None
    for tb in range(8):
        Es.waitev(last_x_use)
        for k in range(16):
            t_x = Es.dma(xchunk(k), xTf[k][:, tb * 512:tb * 512 + 512], ld)
        g_wait(Es, "cksk")
        Es.dma(A(O_CK, 512), cosk[:, tb * 512:tb * 512 + 512], ld)
        t_cs = Es.dma(A(O_SK, 512), sink[:, tb * 512:tb * 512 + 512], ld)
        for o8 in range(8):
            pump_pass1(2)
            is_v = o8 >= 4
            o = o8 % 4
            ppj = pp[o8 % 2]
            g_wait(Ep, f"pp{o8 % 2}")
            Ep.wait(ld, t_x[1])
            for k in range(16):
                t_pe = Ep.op(mm(ppj[:, 0:512], kvw(is_v, o, k), xchunk(k),
                                start=(k == 0), stop=(k == 15)),
                             inc=(pe if k == 15 else None))
            if not is_v:
                g_wait(Ev, "KA")
                Ev.waitev(t_pe)
                t_ev = Ev.op(cp(A(O_KA, 512), ppj[:, 0:512]), inc=vec)
                g_set(f"pp{o8 % 2}", t_ev)
                g_wait(Ep, "pp23")
                Ep.waitev(t_ev)
                t_pr = Ep.op(mm(pp[2][:, 0:512], P2ap, A(O_KA, 512)), inc=pe)
                Ev.wait(ld, t_cs[1])
                Ev.op(tt(A(O_KB, 512), A(O_KA, 512), A(O_CK, 512), ALU.mult))
                Ev.waitev(t_pr)
                Ev.op(tt(A(O_KC, 512), pp[2][:, 0:512], A(O_SK, 512), ALU.mult))
                t_rope = Ev.op(tt(A(O_KA, 512), A(O_KB, 512), A(O_KC, 512),
                                  ALU.add), inc=vec)
                g_set("pp23", t_rope)
                g_set("cksk", t_rope)
                Ea.waitev(t_rope)
                t_sp = Ea.dma(kt_sc[o][:, tb * 512:tb * 512 + 512],
                              A(O_KA, 512), st)
                t_sq = Ea.op(actv(A(O_KNS, 512), A(O_KA, 512), AF.Square),
                             inc=act)
                g_set("KA", (st, t_sp[1]), t_sq)
                g_wait(Ep, "pp4")
                Ep.waitev(t_sq)
                t_kn = Ep.op(mm(pp[4][0:8, 0:512], s_indk[:, o * 8:o * 8 + 8],
                                A(O_KNS, 512)), inc=pe)
                Ev.waitev(t_kn)
                Ev.op(cp(A(O_KB, 512, 0, 8), pp[4][0:8, 0:512]))
                vsep()
                t_knr = Ev.op(red(A(O_KN + tb * 4, 4, 0, 8),
                                  A(O_KB, 512, 0, 8).rearrange(
                                      "p (a b) -> p a b", a=4),
                                  ALU.max), inc=vec)
                g_set("pp4", t_knr)
            else:
                g_wait(Ev, "KB")
                Ev.waitev(t_pe)
                t_ev = Ev.op(cp(A(O_KB, 512), ppj[:, 0:512]), inc=vec)
                g_set(f"pp{o8 % 2}", t_ev)
                g_wait(Ev, "vstg")
                g_wait(Ep, "pp6")
                Ep.waitev(t_ev)
                for r in range(4):
                    t_tr = Ep.op(trn(pp[6][:, r * 128:r * 128 + 128],
                                     A(O_KB + r * 128, 128)),
                                 inc=(pe if r == 3 else None))
                Ev.waitev(t_tr)
                for r in range(4):
                    Ev.op(cp(A(O_VSTG + r * 130, 64),
                             pp[6][:, r * 128:r * 128 + 64]))
                    t_vc = Ev.op(cp(A(O_VSTG + r * 130 + 65, 64),
                                    pp[6][:, r * 128 + 64:r * 128 + 128]),
                                 inc=vec)
                g_set("pp6", t_vc)
                g_set("KB", t_vc)
                Ea.waitev(t_vc)
                t_vs = Ea.dma(vt_sc[o][:, tb * 4 * 130:(tb * 4 + 4) * 130],
                              A(O_VSTG, 520), st)
                g_set("vstg", (st, t_vs[1]))
        last_x_use = (pe, pe.n)
    t_c_spill = st.n

    barrier()
    Es.dma(A(O_CQ, 1024), cosq[:], ld)
    t_cq = Es.dma(A(O_SQ, 1024), sinq[:], ld)

    def QT(j):
        return A(O_QT + j * 1024, 1024)

    t_qn = None
    for half in range(2):
        g_wait(Es, "xmh")
        for k in range(16):
            t_xm = Es.dma(A(O_XMH + k * 512, 512),
                          xTm[k][:, half * 512:half * 512 + 512], ld)
        t_xm_last_use = None
        for o in range(16):
            pump_pass1(4)
            wbuf = A(O_WQS + (o % 2) * 2048, 2048)
            g_wait(Es, f"wqs{o % 2}")
            t_w = Es.dma(wbuf, wqt[o], ld)
            ppj = pp[o % 2]
            g_wait(Ep, f"pp{o % 2}")
            Ep.wait(ld, t_xm[1])
            Ep.wait(ld, t_w[1])
            for k in range(16):
                t_pe = Ep.op(mm(ppj[:, 0:512], wbuf[:, k * 128:k * 128 + 128],
                                A(O_XMH + k * 512, 512),
                                start=(k == 0), stop=(k == 15)),
                             inc=(pe if k == 15 else None))
            g_set(f"wqs{o % 2}", t_pe)
            t_xm_last_use = t_pe
            qdst = QT(o)[:, half * 512:half * 512 + 512]
            Ev.waitev(t_pe)
            t_ev = Ev.op(cp(qdst, ppj[:, 0:512]), inc=vec)
            g_set(f"pp{o % 2}", t_ev)
            g_wait(Ep, "pp23")
            Ep.waitev(t_ev)
            t_pr = Ep.op(mm(pp[2][:, 0:512], P2ap, qdst), inc=pe)
            Ev.wait(ld, t_cq[1])
            Ev.op(tt(A(O_SA, 512), qdst, A(O_CQ + half * 512, 512), ALU.mult))
            Ev.waitev(t_pr)
            Ev.op(tt(A(O_SB, 512), pp[2][:, 0:512],
                     A(O_SQ + half * 512, 512), ALU.mult))
            t_rope = Ev.op(tt(qdst, A(O_SA, 512), A(O_SB, 512), ALU.add),
                           inc=vec)
            g_set("pp23", t_rope)
            g_wait(Ea, "SC")
            Ea.waitev(t_rope)
            t_sq = Ea.op(actv(A(O_SC, 512), qdst, AF.Square), inc=act)
            if o == 0:
                g_wait(Ep, f"pp{4 + half}")
            Ep.waitev(t_sq)
            t_qn = Ep.op(mm(pp[4 + half][0:32, 0:512],
                            A(O_INDQ + o * 32, 32), A(O_SC, 512),
                            start=(o == 0), stop=(o == 15)), inc=pe)
            g_set("SC", t_qn)
        g_set("xmh", t_xm_last_use)
        Ev.waitev(t_qn)
        t_qe = Ev.op(cp(A(O_QN + half * 512, 512, 0, 32),
                        pp[4 + half][0:32, 0:512]), inc=vec)
        g_set(f"pp{4 + half}", t_qe)

    t_qm = Ev.op(red(A(O_CG, 8, 0, 32),
                     A(O_QN, 1024, 0, 32).rearrange("p (a b) -> p a b", a=8),
                     ALU.max), inc=vec)
    Ea.waitev(t_qm)
    t_qs = Ea.op(actv(A(O_CT, 8, 0, 32), A(O_CG, 8, 0, 32), AF.Sqrt), inc=act)
    for s in range(NSLOT):
        tkr = Ev.op(red(A(O_KM + s, 1, 0, 8), A(O_KN, PFX[s], 0, 8), ALU.max),
                    inc=(vec if s == NSLOT - 1 else None))
    Ea.waitev(tkr)
    t_ks = Ea.op(actv(A(O_CT + 8, 8, 0, 8), A(O_KM, 8, 0, 8), AF.Sqrt), inc=act)
    g_wait(Ep, "pp6")
    Ep.waitev(t_ks)
    t_kx = Ep.op(mm(pp[6][0:32, 0:8], A(O_IKQ, 32, 0, 8), A(O_CT + 8, 8, 0, 8)),
                 inc=pe)
    Ev.waitev(t_kx)
    Ev.waitev(t_qs)
    Ev.op(lambda e: e.memset(A(O_CT + 16, 128), 0.0))
    t_c1 = Ev.op(tt(A(O_CT + 16, 8, 0, 32), A(O_CT, 8, 0, 32), pp[6][0:32, 0:8],
                    ALU.mult), inc=vec)
    g_set("pp6", t_c1)
    g_wait(Ep, "pp6")
    Ep.waitev(t_c1)
    t_tr1 = Ep.op(trn(pp[6][:, 0:128], A(O_CT + 16, 128)), inc=pe)
    Ev.waitev(t_tr1)
    Ev.op(lambda e: e.memset(A(O_CT + 16, 128), 0.0))
    t_gm = Ev.op(red(A(O_CT + 16, 8, 0, 8),
                     pp[6][0:8, 0:32].rearrange("p (a b) -> p a b", a=8),
                     ALU.max), inc=vec)
    g_set("pp6", t_gm)
    g_wait(Ep, "pp6")
    Ep.waitev(t_gm)
    t_tr2 = Ep.op(trn(pp[6][:, 0:128], A(O_CT + 16, 128)), inc=pe)
    Ev.waitev(t_tr2)
    t_c2 = Ev.op(ts(A(O_CT + 144, 8, 0, 8), pp[6][0:8, 0:8], -0.125,
                    ALU.mult), inc=vec)
    g_set("pp6", t_c2)
    g_wait(Ep, "pp7")
    Ep.waitev(t_c2)
    for g8 in range(8):
        t_bb = Ep.op(mm(pp[7][:, g8 * 8:g8 * 8 + 8],
                        A(O_INDG + g8 * 128, 128, 0, 8),
                        A(O_CT + 144, 8, 0, 8)),
                     inc=(pe if g8 == 7 else None))
    Ev.waitev(t_bb)
    t_bias = Ev.op(cp(A(O_BIAS, 64), pp[7][:, 0:64]), inc=vec)
    g_set("pp7", t_bias)

    if DBG_SCRATCH[0]:
        Eg.wait(vec, vec.n)
        Eg.wait(pe, pe.n)
        Eg.dma(dbgA[:, 0:64], A(O_BIAS, 64), st)
        Eg.dma(dbgA[0:8, 192:200], A(O_KM, 8, 0, 8), st)
        Eg.dma(dbgA[0:8, 200:232], A(O_KN, 32, 0, 8), st)
        Eg.dma(dbgA[0:32, 232:240], A(O_CG, 8, 0, 32), st)
        Eg.dma(dbgA[:, 704:1728], A(O_QT, 1024), st)
    barrier()
    qt_my = None
    ocnt = 0
    for sg in range(2):
        barrier()
        g_wait(Es, "maskarea")
        for si in range(4):
            for j in range(4):
                t_mk = Es.dma(Ab(O_MASK + (si * 4 + j) * 256, 512),
                              masksI[sg * 4 + si][j], ld)
        if DBG_SCRATCH[0] and sg == 0:
            Es.wait(ld, t_mk[1])
            Es.dma(dbgA[:, 2752:3776], A(O_MASK, 1024), st)
        t_kgvg = None
        for G in range(4):
            g_wait(Es, "kgvg")
            Es.wait(st, t_c_spill)
            Es.dma(A(O_KG, 4096), kt_sc[G], ld)
            t_kgvg = Es.dma(A(O_VG, 4160), vt_sc[G], ld)
            for grp in range(2):
                for si in range(4):
                    pump_pass1(8)
                    s = sg * 4 + si
                    po = pp[4 + ocnt % 2]
                    g_wait(Ep, f"pp{4 + ocnt % 2}")
                    def emit_L(kt, s=s, G=G, grp=grp, si=si,
                               t_kgvg=t_kgvg, t_mk=t_mk):
                        Lp = pp[kt % 2]
                        g_wait(Ep, f"pp{kt % 2}")
                        Ep.wait(ld, t_kgvg[1])
                        Ep.wait(ld, t_mk[1])
                        for r in range(4):
                            t_L = Ep.op(
                                mm(Lp[:, r * 128:r * 128 + 128],
                                   A(O_KG + kt * 128, 128, grp * 64, 64),
                                   A(O_QT + (4 * G + r) * 1024 + s * 128, 128,
                                     grp * 64, 64)),
                                inc=(pe if r == 3 else None))
                        ev = t_L
                        if kt >= PFX[s] - 4:
                            j = kt - (PFX[s] - 4)
                            Ev.waitev(t_L)
                            ev = Ev.op(
                                tt(Lp[:, 0:512], Lp[:, 0:512],
                                   Ab(O_MASK + (si * 4 + j) * 256, 512),
                                   ALU.add), inc=vec)
                        return ev

                    pre = emit_L(0)
                    for kt in range(PFX[s]):
                        nxt_pre = (emit_L(kt + 1) if kt + 1 < PFX[s]
                                   else None)
                        g_wait(Ea, f"pbuf{kt % 2}")
                        Ea.waitev(pre)
                        t_exp = Ea.op(
                            actv(A(O_PB + (kt % 2) * 512, 512),
                                 pp[kt % 2][:, 0:512], AF.Exp,
                                 bias=A(O_BIAS + (G * 2 + grp) * 8 + s, 1),
                                 scale=0.125), inc=act)
                        g_set(f"pp{kt % 2}", t_exp)
                        Ep.waitev(t_exp)
                        for r in range(4):
                            t_av = Ep.op(
                                mm(po[0:65, r * 128:r * 128 + 128],
                                   A(O_VG + kt * 130 + grp * 65, 65),
                                   A(O_PB + (kt % 2) * 512 + r * 128, 128),
                                   start=(kt == 0 and r == 0),
                                   stop=(kt == PFX[s] - 1)),
                                inc=(pe if r == 3 else None))
                        g_set(f"pbuf{kt % 2}", t_av)
                        pre = nxt_pre
                    g_wait(Ev, "oev")
                    Ev.waitev(t_av)
                    t_oe = Ev.op(cp(A(O_EV, 512, 0, 65), po[0:65, 0:512]),
                                 inc=vec)
                    g_set(f"pp{4 + ocnt % 2}", t_oe)
                    for r in range(4):
                        h = grp * 4 + r
                        j = 4 * G + r
                        dst_cols = (O_OSB + j * 512 + si * 128, 128)
                        if grp == 0:
                            t_op = Ev.op(cp(A(dst_cols[0], 128, 0, 64),
                                            A(O_EV + r * 128, 128, 0, 64)),
                                         inc=vec)
                        else:
                            Ea.waitev(t_oe)
                            t_op = Ea.dma(A(dst_cols[0], 128, 64, 64),
                                          A(O_EV + r * 128, 128, 0, 64), st)
                        g_wait(Ep, "pp6")
                        Ep.waitev(t_oe)
                        t_tr = Ep.op(trn(pp[6][:, r * 128:r * 128 + 128],
                                         A(O_EV + r * 128, 128)),
                                     inc=(pe if r == 3 else None))
                    Ev.waitev(t_tr)
                    t_dens = None
                    for r in range(4):
                        h = 8 * G + grp * 4 + r
                        t_dens = Ev.op(cp(A(O_DEN + si * 32 + h, 1),
                                          pp[6][:, r * 128 + 64:r * 128 + 65]),
                                       inc=(vec if r == 3 else None))
                    g_set("pp6", t_dens)
                    g_set("oev", t_dens, t_op)
                    ocnt += 1
            g_set("kgvg", (pe, pe.n))
        for si in range(4):
            Ev.op(lambda e: e.memset(A(O_CT + 32, 96), 0.0))
            vsep()
            t_rc = Ev.op(recip(A(O_CT, 32), A(O_DEN + si * 32, 32)), inc=vec)
            vsep()
            g_wait(Ep, "pp6")
            Ep.waitev(t_rc)
            t_tr = Ep.op(trn(pp[6][:, 0:128], A(O_CT, 128)), inc=pe)
            Ev.waitev(t_tr)
            t_dT = Ev.op(cp(A(O_CT + 128, 128, 0, 32), pp[6][0:32, 0:128]),
                         inc=vec)
            g_set("pp6", t_dT)
            for j in range(16):
                g_wait(Ep, "pp7")
                Ep.waitev(t_dT)
                t_rp = Ep.op(mm(pp[7][:, 0:128], A(O_INDR + j * 128, 128, 0, 32),
                                A(O_CT + 128, 128, 0, 32)), inc=pe)
                Ev.waitev(t_rp)
                t_on = Ev.op(tt(A(O_OSB + j * 512 + si * 128, 128),
                                A(O_OSB + j * 512 + si * 128, 128),
                                pp[7][:, 0:128], ALU.mult), inc=vec)
                g_set("pp7", t_on)
        if DBG_SCRATCH[0] and sg == 0:
            Eg.wait(vec, vec.n)
            Eg.dma(dbgA[:, 64:192], A(O_DEN, 128), st)
            Eg.dma(dbgA[:, 1728:2240], A(O_OSB, 512), st)
            Eg.dma(dbgA[:, 2240:2752], A(O_OSB + 512, 512), st)
        t_y_last = None
        for o in range(16):
            wbuf = A(O_WQS + (o % 2) * 2048, 2048)
            g_wait(Es, f"wos{o % 2}")
            Es.wait(vec, vec.n)
            t_w = Es.dma(wbuf, wot[o], ld)
            ppj = pp[o % 2]
            g_wait(Ep, f"pp{o % 2}")
            Ep.wait(ld, t_w[1])
            for k in range(16):
                t_pe = Ep.op(mm(ppj[:, 0:512], wbuf[:, k * 128:k * 128 + 128],
                                A(O_OSB + k * 512, 512),
                                start=(k == 0), stop=(k == 15)),
                             inc=(pe if k == 15 else None))
            g_set(f"wos{o % 2}", t_pe)
            xb = A(O_PB + (o % 2) * 512, 512)
            g_wait(Es, f"xres{o % 2}")
            t_xr = Es.dma(xb, xTm[o][:, sg * 512:sg * 512 + 512], ld)
            Ev.waitev(t_pe)
            Ev.wait(ld, t_xr[1])
            g_wait(Ev, "dbgtap")
            t_v = Ev.op(tt(A(O_YSB + o * 512, 512), ppj[:, 0:512], xb, ALU.add),
                        inc=vec)
            g_set(f"pp{o % 2}", t_v)
            g_set(f"xres{o % 2}", t_v)
            Ea.waitev(t_v)
            g_wait(Ea, "EVsq")
            t_sq = Ea.op(actv(A(O_EV, 512), A(O_YSB + o * 512, 512), AF.Square),
                         inc=act)
            if o == 0:
                g_wait(Ep, "pp2")
                g_wait(Ep, "pp3")
            Ep.waitev(t_v)
            Ep.op(mm(pp[2][0:1, 0:512], ONESC, A(O_YSB + o * 512, 512),
                     start=(o == 0), stop=(o == 15)))
            Ep.waitev(t_sq)
            t_st2 = Ep.op(mm(pp[3][0:1, 0:512], ONESC, A(O_EV, 512),
                             start=(o == 0), stop=(o == 15)), inc=pe)
            g_set("EVsq", t_st2)
            t_y_last = t_st2
        Ev.waitev(t_y_last)
        Ev.op(ts(A(O_ROWA, 512, 0, 1), pp[2][0:1, 0:512], INV_D, ALU.mult))
        Ev.op(ts(A(O_ROWB, 512, 0, 1), pp[3][0:1, 0:512], INV_D, ALU.mult))
        Ev.op(tt(A(O_ROWB + 512, 512, 0, 1), A(O_ROWA, 512, 0, 1),
                 A(O_ROWA, 512, 0, 1), ALU.mult))
        Ev.op(tt(A(O_ROWB, 512, 0, 1), A(O_ROWB, 512, 0, 1),
                 A(O_ROWB + 512, 512, 0, 1), ALU.subtract))
        t_var = Ev.op(ts(A(O_ROWB, 512, 0, 1), A(O_ROWB, 512, 0, 1), 1e-5,
                         ALU.add), inc=vec)
        Ea.waitev(t_var)
        t_sd = Ea.op(actv(A(O_ROWA + 512, 512, 0, 1), A(O_ROWB, 512, 0, 1),
                          AF.Sqrt), inc=act)
        Ev.waitev(t_sd)
        Ev.op(recip(A(O_ROWA + 512, 512, 0, 1), A(O_ROWA + 512, 512, 0, 1)))
        Ev.op(tt(A(O_ROWB + 512, 512, 0, 1), A(O_ROWA + 512, 512, 0, 1),
                 A(O_ROWA + 512, 512, 0, 1), ALU.mult))
        Ev.op(tt(A(O_ROWB + 512, 512, 0, 1), A(O_ROWB + 512, 512, 0, 1),
                 A(O_ROWB, 512, 0, 1), ALU.mult))
        Ev.op(ts(A(O_ROWB + 512, 512, 0, 1), A(O_ROWB + 512, 512, 0, 1),
                 -0.5, ALU.mult, 1.5, ALU.add))
        t_rst = Ev.op(tt(A(O_ROWA + 512, 512, 0, 1), A(O_ROWA + 512, 512, 0, 1),
                         A(O_ROWB + 512, 512, 0, 1), ALU.mult), inc=vec)
        g_wait(Ep, "pp4")
        g_wait(Ep, "pp5")
        Ep.waitev(t_rst)
        t_mr = Ep.op(mm(pp[4][:, 0:512], ONES1, A(O_ROWA, 512, 0, 1)), inc=pe)
        t_rr = Ep.op(mm(pp[5][:, 0:512], ONES1, A(O_ROWA + 512, 512, 0, 1)),
                     inc=pe)
        t_x1 = None
        for o in range(16):
            Ev.waitev(t_rr)
            Ev.op(tt(A(O_YSB + o * 512, 512), A(O_YSB + o * 512, 512),
                     pp[4][:, 0:512], ALU.subtract))
            t_x1 = Ev.op(tt(A(O_YSB + o * 512, 512), A(O_YSB + o * 512, 512),
                            pp[5][:, 0:512], ALU.mult), inc=vec)
            Ea.waitev(t_x1)
            Ea.dma(x1_sc[o][:, sg * 512:sg * 512 + 512], A(O_YSB + o * 512, 512),
                   st)
        g_set("pp4", t_x1)
        g_set("pp5", t_x1)
        Ev.wait(st, st.n)
        for cch in range(16):
            ppj = pp[cch % 2]
            g_wait(Ep, f"pp{cch % 2}")
            Ep.waitev(t_x1)
            t_hh = Ep.op(mm(ppj[:, 0:512], H128ap, A(O_YSB + cch * 512, 512)),
                         inc=pe)
            Ev.waitev(t_hh)
            t_hc = Ev.op(cp(A(O_YSB + cch * 512, 512), ppj[:, 0:512]), inc=vec)
            g_set(f"pp{cch % 2}", t_hc)
        for stg in range(4):
            step = 1 << stg
            run = min(step, 8)
            for cb in range(0, 16, 2 * step):
                for sub in range(0, step, run):
                    a = A(O_YSB + (cb + sub) * 512, run * 512)
                    b = A(O_YSB + (cb + step + sub) * 512, run * 512)
                    Ev.op(tt(a, a, b, ALU.add))
                    t_bf = Ev.op(stt(b, b, -2.0, a, ALU.mult, ALU.add),
                                 inc=vec)
        t_am = Ev.op(red(A(O_EV, 512), A(O_YSB, 8192).rearrange(
            "p (c t) -> p t c", c=16), ALU.max, absval=True), inc=vec)
        for j in range(4):
            g_wait(Ep, "pp6")
            Ep.waitev(t_am)
            t_tr = Ep.op(trn(pp[6][:, 0:128], A(O_EV + j * 128, 128)), inc=pe)
            Ev.waitev(t_tr)
            t_ac = Ev.op(red(A(O_CT + j, 1), pp[6][:, 0:128], ALU.max),
                         inc=vec)
            g_set("pp6", t_ac)
        vsep()
        Ev.op(ts(A(O_CT + 4, 4), A(O_CT, 4), 1e-5, ALU.max,
                 1.0 / 127.0, ALU.mult))
        vsep()
        t_sc1 = Ev.op(recip(A(O_CT + 8, 4), A(O_CT + 4, 4)), inc=vec)
        vsep()
        for j in range(4):
            g_wait(Ep, "pp6")
            Ep.waitev(t_sc1)
            t_tr = Ep.op(trn(pp[6][:, 0:128], A(O_CT + 8 + j, 128)), inc=pe)
            Ev.waitev(t_tr)
            t_rw = Ev.op(cp(A(O_ROW1 + sg * 512 + j * 128, 128, 0, 1),
                            pp[6][0:1, 0:128]), inc=vec)
            g_set("pp6", t_rw)
        g_wait(Ep, "pp7")
        Ep.waitev(t_rw)
        t_rep = Ep.op(mm(pp[7][:, 0:512], ONES1,
                         A(O_ROW1 + sg * 512, 512, 0, 1)), inc=pe)
        t_q1 = None
        for cch in range(16):
            Ev.waitev(t_rep)
            t_q1 = Ev.op(tt(A(O_YSB + cch * 512, 512),
                            A(O_YSB + cch * 512, 512), pp[7][:, 0:512],
                            ALU.mult), inc=(vec if cch == 15 else None))
        g_set("pp7", t_q1)
        for grpq in range(4):
            base = O_YSB + grpq * 2048
            Ev.op(ts(A(O_MASK, 2048), A(base, 2048), 128.5, ALU.add))
            for j in range(7, -1, -1):
                pw = float(1 << j)
                Ev.op(ts(A(O_MASK + 2048, 2048), A(O_MASK, 2048), pw,
                         ALU.is_ge, -pw, ALU.mult))
                Ev.op(tt(A(O_MASK, 2048), A(O_MASK, 2048),
                         A(O_MASK + 2048, 2048), ALU.add))
            Ev.op(tt(A(base, 2048), A(base, 2048), A(O_MASK, 2048),
                     ALU.subtract))
            for q4 in range(4):
                cch = grpq * 4 + q4
                g_wait(Ev, f"q1s{cch % 2}")
                t_q1 = Ev.op(ts(Ab(O_FS2 + (cch % 2) * 256, 512),
                                A(base + q4 * 512, 512), 0.5, ALU.add),
                             inc=vec)
                Ea.waitev(t_q1)
                t_q1s = Ea.dma(xq1_sc[cch][:, sg * 512:sg * 512 + 512],
                               Ab(O_FS2 + (cch % 2) * 256, 512), st)
                g_set(f"q1s{cch % 2}", (st, t_q1s[1]))
        if DBG_SCRATCH[0] and sg == 0:
            Eg.wait(vec, t_q1[1])
            t_tap = Eg.dma(dbgA[:, 3776:3792], A(O_CT, 16), st)
            t_tap = Eg.dma(dbgA[0:1, 3264:3776], A(O_ROW1, 512, 0, 1), st)
            t_tap = Eg.dma(dbgA[:, 2752:3264], A(O_YSB, 512), st)
            g_set("dbgtap", t_tap)
        g_set("maskarea", (st, st.n), t_q1)
    t_d_done = st.n

    pump_pass1(2 * len(p1_loads))
    barrier()
    ev_r = Ev.op(red(A(O_CT, 3),
                     A(O_GACC, 768).rearrange("p (a b) -> p a b", a=3),
                     ALU.add), inc=vec)
    vsep()
    g_wait(Ep, "pp6")
    Ep.waitev(ev_r)
    t_cmb = Ep.op(mm(pp[6][0:1, 0:3], ONESC, A(O_CT, 3)), inc=pe)
    Ev.waitev(t_cmb)
    t_mn = Ev.op(ts(A(O_SVAL, 3, 0, 1), pp[6][0:1, 0:3], 1.0 / (II * D),
                    ALU.mult, 1e-5, ALU.max), inc=vec)
    Ep.waitev(t_mn)
    t_bc = Ep.op(mm(pp[6][:, 0:3], ONES1, A(O_SVAL, 3, 0, 1)), inc=pe)
    Ev.waitev(t_bc)
    Ev.op(ts(A(O_SG, 3), pp[6][:, 0:3], 0.5, ALU.mult))
    t_sg = Ev.op(ts(A(O_SG + 3, 3), pp[6][:, 0:3], -0.5, ALU.mult), inc=vec)
    g_set("pp6", t_sg)
    Ev.op(recip(A(O_ROWA, 1024, 0, 1), A(O_ROW1, 1024, 0, 1)))
    Ev.op(ts(A(O_ROWB, 1024, 0, 1), A(O_ROWA, 1024, 0, 1),
             A(O_SVAL + 1, 1, 0, 1), ALU.mult, INV_SQ1 * 0.5, ALU.mult))
    t_csg = Ev.op(ts(A(O_ROWA, 1024, 0, 1), A(O_ROWA, 1024, 0, 1),
                     A(O_SVAL, 1, 0, 1), ALU.mult, INV_SQ1 * 0.5, ALU.mult),
                  inc=vec)

    if DBG_SCRATCH[0]:
        Eg.wait(vec, vec.n)
        Eg.wait(act, act.n)
        Eg.dma(dbgA[:, 896:1664], A(O_GACC, 768), st)
        Eg.dma(dbgA[:, 1664:1680], A(O_CT, 16), st)
        Eg.dma(dbgA[0:1, 3776:3780], A(O_SVAL, 4, 0, 1), st)
        Eg.dma(dbgA[:, 3780:3788], A(O_SG, 8), st)
        Eg.dma(dbgA[0:1, 3788:3800], A(O_ROWA, 12, 0, 1), st)
        Eg.dma(dbgA[0:1, 3800:3812], A(O_ROWB, 12, 0, 1), st)
        Eg.dma(dbgA[0:1, 3812:3824], A(O_ROW1, 12, 0, 1), st)
    barrier()
    for ha in range(2):
        barrier()
        for k in range(16):
            t_xq = Es.dma(Ab(O_XQ1 + k * 256, 512),
                          xq1_sc[k][:, ha * 512:ha * 512 + 512], ld)
        g_wait(Ep, "pp4")
        Ep.waitev(t_csg)
        t_rg = Ep.op(mm(pp[4][:, 0:512], ONES1,
                        A(O_ROWA + ha * 512, 512, 0, 1)), inc=pe)
        Ev.waitev(t_rg)
        t_rgc = Ev.op(cp(A(O_FS1, 512), pp[4][:, 0:512]), inc=vec)
        g_set("pp4", t_rgc)
        WGo = (O_WG, O_INDR)
        WUo = (O_WU, O_WU1)
        tern_ev = {}

        def f1_load(ic):
            if ic >= 64:
                return None
            g_wait(Es, f"wg{ic % 2}")
            t_wg = Es.dma(A(WGo[ic % 2], 2048), wgt[ic], ld)
            g_wait(Es, f"wu{ic % 2}")
            t_wu = Es.dma(A(WUo[ic % 2], 2048), wut[ic], ld)
            return (t_wg, t_wu)

        def f1_tern(ic, tw):
            if ic >= 64:
                return
            for mat in range(2):
                toff = (WGo if mat == 0 else WUo)[ic % 2]
                src = A(toff, 2048)
                sgc = 0 if mat == 0 else 1
                Ea.wait(ld, tw[mat][1])
                Ea.waitev(t_sg)
                for q in range(4):
                    qq = mat * 4 + q
                    s1b = (O_FS3, O_INDQ)[qq % 2]
                    s2b = (O_GACC, O_RS2)[qq % 2]
                    g_wait(Ea, f"sgn{qq % 2}")
                    Ea.op(actv(A(s1b, 512), src[:, q * 512:q * 512 + 512],
                               AF.Sign, bias=A(O_SG + 3 + sgc, 1)))
                    t_sg2 = Ea.op(actv(A(s2b, 512),
                                       src[:, q * 512:q * 512 + 512],
                                       AF.Sign, bias=A(O_SG + sgc, 1)),
                                  inc=act)
                    Ev.waitev(t_sg2)
                    t_tn = Ev.op(tt(Ab(toff + q * 256, 512), A(s1b, 512),
                                    A(s2b, 512), ALU.add), inc=vec)
                    g_set(f"sgn{qq % 2}", t_tn)
                tern_ev[(mat, ic)] = t_tn

        pend = f1_load(0)
        f1_tern(0, pend)
        nxt = f1_load(1)
        for ic in range(64):
            t_tg = tern_ev[(0, ic)]
            t_tu = tern_ev[(1, ic)]
            ppg = pp[ic % 2]
            ppu = pp[2 + (ic % 2)]
            g_wait(Ep, f"ppg{ic % 2}")
            g_wait(Ep, f"ppu{ic % 2}")
            Ep.waitev(t_tg)
            Ep.waitev(t_tu)
            Ep.wait(ld, t_xq[1])
            for k in range(16):
                Ep.op(mm(ppg[:, 0:512],
                         Ab(WGo[ic % 2] + k * 64, 128),
                         Ab(O_XQ1 + k * 256, 512),
                         start=(k == 0), stop=(k == 15)),
                      inc=(pe if k == 15 else None))
            t_gd = (pe, pe.n)
            for k in range(16):
                Ep.op(mm(ppu[:, 0:512], Ab(WUo[ic % 2] + k * 64, 128),
                         Ab(O_XQ1 + k * 256, 512),
                         start=(k == 0), stop=(k == 15)),
                      inc=(pe if k == 15 else None))
            t_ud = (pe, pe.n)
            g_set(f"wg{ic % 2}", t_gd)
            g_set(f"wu{ic % 2}", t_ud)
            Ev.waitev(t_gd)
            Ev.waitev(t_rgc)
            t_gs = Ev.op(tt(A(O_SH + ic * 512, 512), ppg[:, 0:512],
                            A(O_FS1, 512), ALU.mult), inc=vec)
            g_set(f"ppg{ic % 2}", t_gs)
            Ea.waitev(t_gs)
            g_wait(Ea, "FS2")
            t_si = Ea.op(actv(A(O_FS2, 512), A(O_SH + ic * 512, 512), AF.Silu),
                         inc=act)
            f1_tern(ic + 1, nxt)
            nxt = f1_load(ic + 2)
            Ev.waitev(t_si)
            Ev.waitev(t_ud)
            t_h = Ev.op(tt(A(O_SH + ic * 512, 512), A(O_FS2, 512),
                           ppu[:, 0:512], ALU.mult), inc=vec)
            g_set(f"ppu{ic % 2}", t_h)
            g_set("FS2", t_h)
        if DBG_SCRATCH[0] and ha == 0:
            Eg.wait(vec, vec.n)
            t_htap = Eg.dma(dbgA[:, 3264:3776], A(O_SH, 512), st)
            g_set("htap", t_htap)
        Ev.wait(pe, pe.n)
        g_wait(Ev, "htap")
        for cch in range(64):
            ppj = pp[4 + cch % 2]
            g_wait(Ep, f"pp{4 + cch % 2}")
            Ep.wait(vec, vec.n)
            t_hh = Ep.op(mm(ppj[:, 0:512], H128ap, A(O_SH + cch * 512, 512)),
                         inc=pe)
            Ev.waitev(t_hh)
            t_hc = Ev.op(cp(A(O_SH + cch * 512, 512), ppj[:, 0:512]), inc=vec)
            g_set(f"pp{4 + cch % 2}", t_hc)
        for stg in range(6):
            step = 1 << stg
            run = min(step, 8)
            for cb in range(0, 64, 2 * step):
                for sub in range(0, step, run):
                    a = A(O_SH + (cb + sub) * 512, run * 512)
                    b = A(O_SH + (cb + step + sub) * 512, run * 512)
                    Ev.op(tt(a, a, b, ALU.add))
                    t_bf = Ev.op(stt(b, b, -2.0, a, ALU.mult, ALU.add),
                                 inc=vec)
        Ev.wait(pe, pe.n)
        t_am = Ev.op(red(A(O_STGV, 512), A(O_SH, 32768).rearrange(
            "p (c t) -> p t c", c=64), ALU.max, absval=True), inc=vec)
        for j in range(4):
            g_wait(Ep, "pp6")
            Ep.waitev(t_am)
            t_tr = Ep.op(trn(pp[6][:, 0:128], A(O_STGV + j * 128, 128)), inc=pe)
            Ev.waitev(t_tr)
            t_ac = Ev.op(red(A(O_CT + j, 1), pp[6][:, 0:128], ALU.max), inc=vec)
            g_set("pp6", t_ac)
        vsep()
        Ev.op(ts(A(O_CT + 4, 4), A(O_CT, 4), 1e-5, ALU.max,
                 1.0 / 127.0, ALU.mult))
        vsep()
        t_sc2 = Ev.op(recip(A(O_CT + 8, 4), A(O_CT + 4, 4)), inc=vec)
        vsep()
        for j in range(4):
            g_wait(Ep, "pp6")
            Ep.waitev(t_sc2)
            t_tr = Ep.op(trn(pp[6][:, 0:128], A(O_CT + 8 + j, 128)), inc=pe)
            Ev.waitev(t_tr)
            t_rw = Ev.op(cp(A(O_ROW2 + ha * 512 + j * 128, 128, 0, 1),
                            pp[6][0:1, 0:128]), inc=vec)
            g_set("pp6", t_rw)
        g_wait(Ep, "pp7")
        Ep.waitev(t_rw)
        t_rep = Ep.op(mm(pp[7][:, 0:512], ONES1,
                         A(O_ROW2 + ha * 512, 512, 0, 1)), inc=pe)
        Ev.waitev(t_rep)
        t_repc = Ev.op(cp(A(O_FS1, 512), pp[7][:, 0:512]), inc=vec)
        g_set("pp7", t_repc)
        for cch in range(64):
            Ev.waitev(t_repc)
            Ev.op(tt(A(O_SH + cch * 512, 512), A(O_SH + cch * 512, 512),
                     A(O_FS1, 512), ALU.mult))
        for grpq in range(16):
            base = O_SH + grpq * 2048
            Ev.op(ts(A(O_UU, 2048), A(base, 2048), 128.5, ALU.add))
            for j in range(7, -1, -1):
                pw = float(1 << j)
                Ev.op(ts(A(O_MM, 2048), A(O_UU, 2048), pw, ALU.is_ge,
                         -pw, ALU.mult))
                Ev.op(tt(A(O_UU, 2048), A(O_UU, 2048), A(O_MM, 2048),
                         ALU.add))
            Ev.op(tt(A(base, 2048), A(base, 2048), A(O_UU, 2048),
                     ALU.subtract))
            for q4 in range(4):
                cch = grpq * 4 + q4
                g_wait(Ev, f"q2s{cch % 2}")
                t_q2 = Ev.op(ts(Ab(O_FS2 + (cch % 2) * 256, 512),
                                A(base + q4 * 512, 512), 0.5, ALU.add),
                             inc=vec)
                Ea.waitev(t_q2)
                t_q2s = Ea.dma(xq2_sc[cch][:, ha * 512:ha * 512 + 512],
                               Ab(O_FS2 + (cch % 2) * 256, 512), st)
                g_set(f"q2s{cch % 2}", (st, t_q2s[1]))
    t_f2_done = st.n

    barrier()
    Ev.op(recip(A(O_ROWA, 1024, 0, 1), A(O_ROW2, 1024, 0, 1)))
    Ev.op(ts(A(O_ROWA, 1024, 0, 1), A(O_ROWA, 1024, 0, 1),
             A(O_SVAL + 2, 1, 0, 1), ALU.mult, INV_SQ2 * 0.5, ALU.mult))
    t_cd = Ev.op(tt(A(O_ROWA, 1024, 0, 1), A(O_ROWA, 1024, 0, 1),
                    A(O_ROWB, 1024, 0, 1), ALU.mult), inc=vec)

    for oblk in range(4):
        for kg in range(16):
            wd = A(O_WDS + (kg % 2) * 2048, 2048)
            g_wait(Es, f"wds{kg % 2}")
            t_wd = None
            for oo in range(4):
                t_wd = Es.dma(wd[:, oo * 512:oo * 512 + 512],
                              wdt[oblk * 4 + oo][:, kg * 512:kg * 512 + 512], ld)
            Ea.wait(ld, t_wd[1])
            Ea.waitev(t_sg)
            for oo in range(4):
                s1b = 38600 + (oo % 2) * 512
                s2b = 39624 + (oo % 2) * 512
                g_wait(Ea, f"sgd{oo % 2}")
                Ea.op(actv(A(s1b, 512), wd[:, oo * 512:oo * 512 + 512],
                           AF.Sign, bias=A(O_SG + 5, 1)))
                t_sg2 = Ea.op(actv(A(s2b, 512),
                                   wd[:, oo * 512:oo * 512 + 512],
                                   AF.Sign, bias=A(O_SG + 2, 1)), inc=act)
                Ev.waitev(t_sg2)
                t_tn = Ev.op(tt(Ab(O_WDS + (kg % 2) * 2048 + oo * 512, 512),
                                A(s1b, 512), A(s2b, 512), ALU.add), inc=vec)
                g_set(f"sgd{oo % 2}", t_tn)
            g_set(f"terd{kg % 2}", t_tn)
            t_mmlast = None
            for kk in range(4):
                xb = Ab(O_XQ2B + (kk % 2) * 512, 1024)
                g_wait(Es, f"xq2b{kk % 2}")
                t_xq2 = Es.dma(xb, xq2_sc[kg * 4 + kk], ld)
                Ep.wait(ld, t_xq2[1])
                Ep.waitev(t_tn)
                if kg == 0 and kk == 0:
                    for jj in range(8):
                        g_wait(Ep, f"pp{jj}")
                for oo in range(4):
                    for half in range(2):
                        t_mmlast = Ep.op(
                            mm(pp[oo * 2 + half][:, 0:512],
                               Ab(O_WDS + (kg % 2) * 2048 + oo * 512
                                  + kk * 64, 128),
                               xb[:, half * 512:half * 512 + 512],
                               start=(kg == 0 and kk == 0),
                               stop=(kg == 15 and kk == 3)),
                            inc=(pe if (oo == 3 and half == 1) else None))
                g_set(f"xq2b{kk % 2}", t_mmlast)
            g_set(f"wds{kg % 2}", t_mmlast)
        for oo in range(4):
            o = oblk * 4 + oo
            Ev.waitev(t_mmlast)
            for half in range(2):
                t_dn = Ev.op(cp(A(O_DNS + o * 1024 + half * 512, 512),
                                pp[oo * 2 + half][:, 0:512]), inc=vec)
        for jj in range(8):
            g_set(f"pp{jj}", t_dn)
    g_wait(Ep, "pp4")
    Ep.waitev(t_cd)
    Ep.wait(vec, vec.n)
    t_cr = Ep.op(mm(pp[4][:, 0:512], ONES1, A(O_ROWA, 512, 0, 1)), inc=pe)
    t_cr2 = Ep.op(mm(pp[5][:, 0:512], ONES1, A(O_ROWA + 512, 512, 0, 1)),
                  inc=pe)
    t_stats = None
    for o in range(16):
        xb = A(O_X1B + (o % 2) * 1024, 1024)
        g_wait(Es, f"x1b{o % 2}")
        t_x1r = Es.dma(xb, x1_sc[o], ld)
        Ev.waitev(t_cr2)
        Ev.wait(ld, t_x1r[1])
        for half in range(2):
            dn = A(O_DNS + o * 1024 + half * 512, 512)
            Ev.op(tt(dn, dn, pp[4 + half][:, 0:512], ALU.mult))
            t_v2 = Ev.op(tt(dn, dn, xb[:, half * 512:half * 512 + 512],
                            ALU.add), inc=vec)
            g_set(f"x1b{o % 2}", t_v2)
            Ea.waitev(t_v2)
            g_wait(Ea, "lnsq")
            t_sq = Ea.op(actv(A(O_LNS, 512), dn, AF.Square), inc=act)
            if o == 0 and half == 0:
                g_wait(Ep, "pp2")
                g_wait(Ep, "pp3")
                g_wait(Ep, "pp6")
                g_wait(Ep, "pp7")
            Ep.waitev(t_v2)
            Ep.op(mm(pp[2 + half][0:1, 0:512], ONESC, dn,
                     start=(o == 0), stop=(o == 15)))
            Ep.waitev(t_sq)
            t_stats = Ep.op(mm(pp[6 + half][0:1, 0:512], ONESC, A(O_LNS, 512),
                               start=(o == 0), stop=(o == 15)), inc=pe)
            g_set("lnsq", t_stats)
    Ev.waitev(t_stats)
    for half in range(2):
        ro = half * 512
        Ev.op(ts(A(O_ROWA + ro, 512, 0, 1), pp[2 + half][0:1, 0:512], INV_D,
                 ALU.mult))
        Ev.op(ts(A(O_ROWB + ro, 512, 0, 1), pp[6 + half][0:1, 0:512], INV_D,
                 ALU.mult))
        Ev.op(tt(A(O_ROW1 + ro, 512, 0, 1), A(O_ROWA + ro, 512, 0, 1),
                 A(O_ROWA + ro, 512, 0, 1), ALU.mult))
        Ev.op(tt(A(O_ROWB + ro, 512, 0, 1), A(O_ROWB + ro, 512, 0, 1),
                 A(O_ROW1 + ro, 512, 0, 1), ALU.subtract))
        t_var = Ev.op(ts(A(O_ROWB + ro, 512, 0, 1), A(O_ROWB + ro, 512, 0, 1),
                         1e-5, ALU.add), inc=vec)
        Ea.waitev(t_var)
        t_sd = Ea.op(actv(A(O_ROW2 + ro, 512, 0, 1), A(O_ROWB + ro, 512, 0, 1),
                          AF.Sqrt), inc=act)
        Ev.waitev(t_sd)
        Ev.op(recip(A(O_ROW2 + ro, 512, 0, 1), A(O_ROW2 + ro, 512, 0, 1)))
        Ev.op(tt(A(O_ROW1 + ro, 512, 0, 1), A(O_ROW2 + ro, 512, 0, 1),
                 A(O_ROW2 + ro, 512, 0, 1), ALU.mult))
        Ev.op(tt(A(O_ROW1 + ro, 512, 0, 1), A(O_ROW1 + ro, 512, 0, 1),
                 A(O_ROWB + ro, 512, 0, 1), ALU.mult))
        Ev.op(ts(A(O_ROW1 + ro, 512, 0, 1), A(O_ROW1 + ro, 512, 0, 1),
                 -0.5, ALU.mult, 1.5, ALU.add))
        t_rst = Ev.op(tt(A(O_ROW2 + ro, 512, 0, 1), A(O_ROW2 + ro, 512, 0, 1),
                         A(O_ROW1 + ro, 512, 0, 1), ALU.mult), inc=vec)
    g_set("pp2", t_rst)
    g_set("pp3", t_rst)
    g_set("pp6", t_rst)
    g_set("pp7", t_rst)
    g_wait(Ep, "pp2")
    Ep.waitev(t_rst)
    t_mr0 = Ep.op(mm(pp[2][:, 0:512], ONES1, A(O_ROWA, 512, 0, 1)), inc=pe)
    t_mr1 = Ep.op(mm(pp[3][:, 0:512], ONES1, A(O_ROWA + 512, 512, 0, 1)),
                  inc=pe)
    t_rr0 = Ep.op(mm(pp[6][:, 0:512], ONES1, A(O_ROW2, 512, 0, 1)), inc=pe)
    t_rr1 = Ep.op(mm(pp[7][:, 0:512], ONES1, A(O_ROW2 + 512, 512, 0, 1)),
                  inc=pe)
    t_x2 = None
    for o in range(16):
        Ev.waitev(t_rr1)
        for half in range(2):
            dn = A(O_DNS + o * 1024 + half * 512, 512)
            Ev.op(tt(dn, dn, pp[2 + half][:, 0:512], ALU.subtract))
            t_x2 = Ev.op(tt(dn, dn, pp[6 + half][:, 0:512], ALU.mult), inc=vec)
    barrier()
    t_out = None
    for tts in range(8):
        g_wait(Ev, "outa")
        for o in range(16):
            ppj = pp[o % 2]
            g_wait(Ep, f"ppt{o % 2}")
            Ep.waitev(t_x2)
            t_tr = Ep.op(trn(ppj[:, 0:128],
                             A(O_DNS + o * 1024 + tts * 128, 128)), inc=pe)
            Ev.waitev(t_tr)
            t_oc = Ev.op(cp(A(O_OUTA + o * 128, 128), ppj[:, 0:128]), inc=vec)
            g_set(f"ppt{o % 2}", t_oc)
        Eg.waitev(t_oc)
        t_out = Eg.dma(out[tts * 128:tts * 128 + 128, :], A(O_OUTA, 2048), st)
        g_set("outa", t_out)
    Eg.wait(st, st.n)

    with nc.Block() as block:
        @block.sync
        def _(eng):
            for f in Es.ops:
                f(eng)

        @block.tensor
        def _(eng):
            for f in Ep.ops:
                f(eng)

        @block.vector
        def _(eng):
            for f in Ev.ops:
                f(eng)

        @block.scalar
        def _(eng):
            for f in Ea.ops:
                f(eng)

        @block.gpsimd
        def _(eng):
            for f in Eg.ops:
                f(eng)

    ctx.close()
    return nc


_NC_CACHE = []


def kernel(**inputs):
    x = np.asarray(inputs["L0_attn_hidden_states"], dtype=np.float32)
    Wq = np.asarray(inputs["L0_attn_Wq"], dtype=np.float32)
    Wk = np.asarray(inputs["L0_attn_Wk"], dtype=np.float32)
    Wv = np.asarray(inputs["L0_attn_Wv"], dtype=np.float32)
    Wo = np.asarray(inputs["L0_attn_Wo"], dtype=np.float32)
    wg = np.asarray(inputs["L0_ffn_w_gate"], dtype=np.float32)
    wu = np.asarray(inputs["L0_ffn_w_up"], dtype=np.float32)
    wd = np.asarray(inputs["L0_ffn_w_down"], dtype=np.float32)

    nc = build_nc()
    in_maps, order_all = _stage(x, Wq, Wk, Wv, Wo, wg, wu, wd)
    res = run_bass_kernel_spmd(nc, in_maps, core_ids=list(range(8)))
    y = np.zeros((2, S, D), dtype=np.float32)
    for core in range(8):
        b, cols = order_all[core]
        y[b, cols, :] = np.asarray(res.results[core]["out"])
    return y


def _stage(x, Wq, Wk, Wv, Wo, wg, wu, wd):
    perm = _qperm()
    wqtS = _strips(np.ascontiguousarray(Wq.T[:, perm]))
    wktS = _strips(np.ascontiguousarray(Wk.T))
    wvtS = _strips(np.ascontiguousarray(Wv.T))
    wotS = _strips(np.ascontiguousarray(Wo.T[perm, :]))
    wgtS = _strips(np.ascontiguousarray(wg.T))
    wutS = _strips(np.ascontiguousarray(wu.T))
    wdtS = _strips(np.ascontiguousarray(wd.T))
    cosA, sinA = _rope_tables()
    h128 = _hadamard(128).astype(np.float32)
    ident = np.eye(128, dtype=np.float32)
    p64 = np.zeros((64, 64), dtype=np.float32)
    p64[:32, 32:] = -np.eye(32)
    p64[32:, :32] = np.eye(32)
    p2 = np.zeros((128, 128), dtype=np.float32)
    p2[:64, :64] = p64.T
    p2[64:, 64:] = p64.T
    ones1 = np.ones((1, 128), dtype=np.float32)
    onescol = np.ones((128, 1), dtype=np.float32)
    indq = np.zeros((128, 512), dtype=np.float32)
    for j in range(16):
        G, r = j // 4, j % 4
        hA, hB = 8 * G + r, 8 * G + 4 + r
        indq[0:64, j * 32 + hA] = 1.0
        indq[64:128, j * 32 + hB] = 1.0
    indk = np.zeros((128, 32), dtype=np.float32)
    for o in range(4):
        indk[0:64, o * 8 + 2 * o] = 1.0
        indk[64:128, o * 8 + 2 * o + 1] = 1.0
    indkq = np.zeros((8, 32), dtype=np.float32)
    for h in range(32):
        indkq[h // 4, h] = 1.0
    indrep = np.zeros((32, 2048), dtype=np.float32)
    for j in range(16):
        G, r = j // 4, j % 4
        hA, hB = 8 * G + r, 8 * G + 4 + r
        indrep[hA, j * 128 + np.arange(0, 64)] = 1.0
        indrep[hB, j * 128 + np.arange(64, 128)] = 1.0
    indgrp = np.zeros((8, 1024), dtype=np.float32)
    for g8 in range(8):
        indgrp[g8, g8 * 128:(g8 + 1) * 128] = 1.0
    in_maps = []
    order_all = []
    for core in range(8):
        b, c = core // 4, core % 4
        qt = _core_qtiles(c)
        cols = np.concatenate([np.arange(t * 128, (t + 1) * 128) for t in qt])
        order_all.append((b, cols))
        xTb = np.ascontiguousarray(x[b].T)
        xTm = np.ascontiguousarray(xTb[:, cols]).reshape(16, 128, NT)
        xTf = xTb.reshape(16, 128, S)
        in_maps.append({
            "xTm": xTm, "xTf": xTf,
            "wqt": wqtS, "wkt": wktS, "wvt": wvtS, "wot": wotS,
            "wgt": wgtS, "wut": wutS, "wdt": wdtS,
            "cosq": np.ascontiguousarray(cosA[:, cols]),
            "sinq": np.ascontiguousarray(sinA[:, cols]),
            "cosk": cosA, "sink": sinA,
            "masks": _masks_for_core(c),
            "h128": h128, "ident": ident, "p2": p2,
            "ones1": ones1, "onescol": onescol,
            "indq": indq, "indk": indk, "indkq": indkq,
            "indrep": indrep, "indgrp": indgrp,
        })
    return in_maps, order_all


# version 07
# speedup vs baseline: 1.1714x; 1.0134x over previous
from contextlib import ExitStack

import numpy as np
import ml_dtypes

import concourse.bass as bass
import concourse.mybir as mybir
from concourse.bass_utils import run_bass_kernel_spmd

F32 = mybir.dt.float32
BF16 = mybir.dt.bfloat16
AF = mybir.ActivationFunctionType
ALU = mybir.AluOpType
AX = mybir.AxisListType

S, D, HD, II, NT, NSLOT = 4096, 2048, 64, 8192, 1024, 8
PFX = [4, 8, 12, 16, 20, 24, 28, 32]
MASK_NEG = -1.0e30
INV_D = 1.0 / 2048.0
INV_SQ1 = float(2048.0 ** -0.5)
INV_SQ2 = float(8192.0 ** -0.5)
ARENA = 52224

O_H128, O_ID, O_P2, O_ONES1, O_ONESC = 0, 128, 256, 384, 512
O_SVAL = 516
O_SG = 520
O_KM = 528
O_KN = 536
O_CG = 568
O_IKQ = 576
O_INDQ = 640
O_INDR = 1152
O_INDG = 3200
O_BIAS = 4224
O_CT = 4288
O_ROW1 = 4544
O_ROW2 = 5568
O_ROWA = 6592
O_ROWB = 7616
O_GACC = 8640
O_FS1 = 9472
O_FS2 = 9984
O_FS3 = 10496
O_KVW = 11008
O_XS = 27500
O_CK, O_SK, O_VSTG = 35800, 36312, 36824
O_KA, O_KB, O_KC, O_KNS = 37400, 37912, 38424, 38936
O_QT = 11008
O_XMH = 27500
O_WQS = 35800
O_CQ, O_SQ = 39900, 40924
O_QN = 41950
O_SA, O_SB, O_SC = 42980, 43492, 44004
O_OSB = 27500
O_MASK = 35800
O_KG = 39900
O_VG = 44000
O_YSB = 39900
O_PB = 48200
O_EV = 49224
O_DEN = 49740
O_P1 = 50176
O_P1D = 51200
O_XQ1 = 11008
O_WG, O_WU = 15104, 17152
O_WU1 = 3200
O_SH = 19200
O_STGV, O_STGU, O_STGM = 13056, 13568, 14080
O_STGQ = 14592
O_UU, O_MM = 13056, 11008
O_RS2 = 6080
O_DNS = 11008
O_WDS = 27500
O_XQ2B = 31700
O_X1B = 32800
O_OUTA = 34900
O_LNS = 37000


def _hadamard(n):
    h = np.array([[1.0]])
    while h.shape[0] < n:
        h = np.block([[h, h], [h, -h]])
    return h


def _qperm():
    order = []
    for G in range(4):
        for r in range(4):
            for h in (8 * G + r, 8 * G + 4 + r):
                order.extend(range(h * HD, (h + 1) * HD))
    return np.array(order)


def _core_qtiles(c):
    return [c, 7 - c, 8 + c, 15 - c, 16 + c, 23 - c, 24 + c, 31 - c]


def _strips(aT, pi=128):
    K, M = aT.shape
    t = aT.reshape(K // 128, 128, M // pi, pi)
    return np.ascontiguousarray(
        t.transpose(2, 1, 0, 3).reshape(M // pi, 128, (K // 128) * pi))


def _rope_tables():
    inv = 1.0 / (10000.0 ** (np.arange(0, HD, 2, dtype=np.float64) / HD))
    pos = np.arange(S, dtype=np.float64)
    fr = pos[:, None] * inv[None, :]
    emb = np.concatenate([fr, fr], axis=1)
    cos = np.cos(emb).T.astype(np.float32)
    sin = np.sin(emb).T.astype(np.float32)
    return (np.ascontiguousarray(np.concatenate([cos, cos], 0)),
            np.ascontiguousarray(np.concatenate([sin, sin], 0)))


def _masks_for_core(c):
    qt = _core_qtiles(c)
    m = np.zeros((NSLOT, 4, 128, 512), dtype=np.float32)
    for s in range(NSLOT):
        for j in range(4):
            kt = PFX[s] - 4 + j
            kpos = kt * 128 + np.arange(128)[:, None]
            qpos = qt[s] * 128 + np.arange(128)[None, :]
            m[s, j] = np.tile(np.where(kpos > qpos, MASK_NEG, 0.0), (1, 4))
    return m.astype(ml_dtypes.bfloat16)


class Sem:
    def __init__(self, h):
        self.h = h
        self.n = 0

    def bump(self, k=1):
        self.n += k
        return self.n


class E:
    def __init__(self, name):
        self.name = name
        self.ops = []
        self.seen = {}

    def wait(self, sem, val):
        if val is None or val <= 0:
            return
        if self.seen.get(id(sem), -1) >= val:
            return
        self.seen[id(sem)] = val
        self.ops.append(lambda e, s=sem.h, v=val: e.wait_ge(s, v))

    def waitev(self, ev):
        if ev is not None:
            self.wait(ev[0], ev[1])

    def op(self, fn, inc=None, k=1):
        if inc is None:
            self.ops.append(fn)
            return None
        t = inc.bump(k)
        self.ops.append(lambda e, f=fn, s=inc.h, kk=k: f(e).then_inc(s, kk))
        return (inc, t)

    def dma(self, dst, src, sem):
        return self.op(lambda e, d=dst, s=src: e.dma_start(out=d, in_=s),
                       inc=sem, k=16)


DBG_SCRATCH = [False]


def build_nc():
    nc = bass.Bass("TRN2", target_bir_lowering=False)

    def dinp(name, shape, dt=F32):
        return nc.dram_tensor(name, list(shape), dt, kind="ExternalInput").ap()

    xTm = dinp("xTm", [16, 128, NT])
    xTf = dinp("xTf", [16, 128, S])
    wqt = dinp("wqt", [16, 128, 2048])
    wkt = dinp("wkt", [4, 128, 2048])
    wvt = dinp("wvt", [4, 128, 2048])
    wot = dinp("wot", [16, 128, 2048])
    wgt = dinp("wgt", [64, 128, 2048])
    wut = dinp("wut", [64, 128, 2048])
    wdt = dinp("wdt", [16, 128, 8192])
    cosq = dinp("cosq", [128, NT])
    sinq = dinp("sinq", [128, NT])
    cosk = dinp("cosk", [128, S])
    sink = dinp("sink", [128, S])
    masksI = dinp("masks", [NSLOT, 4, 128, 512], BF16)
    c_h128 = dinp("h128", [128, 128])
    c_ident = dinp("ident", [128, 128])
    c_p2 = dinp("p2", [128, 128])
    c_ones1 = dinp("ones1", [1, 128])
    c_onescol = dinp("onescol", [128, 1])
    c_indq = dinp("indq", [128, 512])
    c_indk = dinp("indk", [128, 32])
    c_indkq = dinp("indkq", [8, 32])
    c_indrep = dinp("indrep", [32, 2048])
    c_indgrp = dinp("indgrp", [8, 1024])

    kd = "ExternalOutput" if DBG_SCRATCH[0] else "Internal"
    kt_sc = nc.dram_tensor("kt_sc", [4, 128, S], F32, kind=kd).ap()
    vt_sc = nc.dram_tensor("vt_sc", [4, 128, 32 * 130], F32, kind=kd).ap()
    x1_sc = nc.dram_tensor("x1_sc", [16, 128, NT], F32, kind=kd).ap()
    xq1_sc = nc.dram_tensor("xq1_sc", [16, 128, NT], BF16, kind=kd).ap()
    xq2_sc = nc.dram_tensor("xq2_sc", [64, 128, NT], BF16, kind=kd).ap()
    out = nc.dram_tensor("out", [NT, D], F32, kind="ExternalOutput").ap()
    dbgA = nc.dram_tensor("dbgA", [128, 4096], F32,
                          kind=("ExternalOutput" if DBG_SCRATCH[0] else "Internal")).ap()

    ctx = ExitStack()
    ar = ctx.enter_context(nc.sbuf_tensor("arena", [128, ARENA], F32))
    s_indk = ctx.enter_context(nc.sbuf_tensor("s_indk", [128, 32], F32))
    arB = ar[:].bitcast(BF16)

    def A(off, cols, p0=0, pn=128):
        return ar[p0:p0 + pn, off:off + cols]

    def Ab(off_f32, cols_bf16, p0=0, pn=128):
        return arB[p0:p0 + pn, 2 * off_f32:2 * off_f32 + cols_bf16]

    pp = [ctx.enter_context(nc.psum_tensor(f"pp{j}", [128, 512], F32))
          for j in range(8)]

    sems = {k: Sem(ctx.enter_context(nc.semaphore(k)))
            for k in ("ld", "st", "pe", "vec", "act", "gp", "gld")}
    ld, st, pe, vec, act, gp, gld = (sems[k] for k in
                                     ("ld", "st", "pe", "vec", "act", "gp", "gld"))

    Es, Ep, Ev, Ea, Eg = E("sync"), E("tensor"), E("vector"), E("scalar"), E("gpsimd")
    ALL_E = (Es, Ep, Ev, Ea, Eg)

    guard = {}

    def g_wait(Eng, key):
        for ev in guard.get(key, ()):
            Eng.waitev(ev)

    def g_set(key, *evs):
        guard[key] = [e for e in evs if e is not None]

    def barrier():
        evs = [(pe, pe.n), (vec, vec.n), (act, act.n), (st, st.n),
               (ld, ld.n), (gp, gp.n)]
        for Eng in ALL_E:
            for ev in evs:
                Eng.waitev(ev)

    def mm(out_ap, lhsT, rhs, start=True, stop=True):
        return lambda e: e.matmul(out_ap, lhsT, rhs, start=start, stop=stop)

    def cp(dst, src):
        return lambda e: e.tensor_copy(dst, src)

    def tt(dst, a, b, op):
        return lambda e: e.tensor_tensor(dst, a, b, op)

    def ts(dst, a, s1, op0, s2=None, op1=None):
        if op1 is None:
            return lambda e: e.tensor_scalar(dst, a, s1, None, op0)
        return lambda e: e.tensor_scalar(dst, a, s1, s2, op0, op1)

    def stt(dst, in0, scalar, in1, op0, op1):
        return lambda e: e.scalar_tensor_tensor(dst, in0, scalar, in1, op0, op1)

    def actv(dst, src, func, bias=0.0, scale=1.0):
        return lambda e: e.activation(out=dst, in_=src, func=func,
                                      bias=bias, scale=scale)

    def red(dst, src, op, absval=False, axis=None):
        ax = AX.X if axis is None else axis
        return lambda e: e.tensor_reduce(dst, src, ax, op,
                                         apply_absolute_value=absval)

    def recip(dst, src):
        return lambda e: e.reciprocal(dst, src)

    def trn(out_ap, in_ap):
        return lambda e: e.transpose(out_ap, in_ap, ar[0:128, O_ID:O_ID + 128])

    def vsep():
        Ev.op(cp(A(O_CT + 254, 1), A(O_ONESC, 1)))

    H128ap = A(O_H128, 128)
    P2ap = A(O_P2, 128)
    ONES1 = A(O_ONES1, 128, 0, 1)
    ONESC = A(O_ONESC, 1)

    for dst, src in [(A(O_H128, 128), c_h128), (A(O_ID, 128), c_ident),
                     (A(O_P2, 128), c_p2), (ONES1, c_ones1), (ONESC, c_onescol),
                     (A(O_INDQ, 512), c_indq), (s_indk[:], c_indk),
                     (A(O_IKQ, 32, 0, 8), c_indkq),
                     (A(O_INDR, 2048, 0, 32), c_indrep),
                     (A(O_INDG, 1024, 0, 8), c_indgrp)]:
        Es.dma(dst, src[:], ld)
    t_const = ld.n
    for r in range(4):
        Eg.op(lambda e, rr=r: e.memset(A(O_VSTG + rr * 130 + 64, 1), 1.0))
        Eg.op(lambda e, rr=r: e.memset(A(O_VSTG + rr * 130 + 129, 1), 1.0))
    Eg.op(lambda e: e.memset(A(O_GACC, 768), 0.0), inc=gp)

    for j, src in enumerate([wkt, wvt]):
        for o in range(4):
            Es.dma(A(O_KVW + (j * 4 + o) * 2048, 2048), src[o], ld)
    t_kvw = ld.n

    def kvw(is_v, o, k):
        return A(O_KVW + ((4 if is_v else 0) + o) * 2048 + k * 128, 128)

    p1_loads = []
    for wsrc, nslab, wcols in ((wgt, 64, 2048), (wut, 64, 2048), (wdt, 16, 8192)):
        for sl in range(nslab):
            for hh in range(wcols // 512):
                p1_loads.append(wsrc[sl][:, hh * 512:hh * 512 + 512])
    p1_state = {"ld": 0, "cons": 0, "consev": {}, "ldev": {}}

    def actv_acc(dst, srcq, func, accum):
        return lambda e: e.activation(out=dst, in_=srcq, func=func,
                                      bias=0.0, scale=1.0, accum_out=accum)

    def pump_pass1(n=1):
        for _ in range(n):
            i = p1_state["ld"]
            if i < len(p1_loads):
                if i >= 2:
                    Eg.waitev(p1_state["consev"].get(i - 2))
                p1_state["ldev"][i] = Eg.dma(A(O_P1 + (i % 2) * 512, 512),
                                             p1_loads[i], gld)
                p1_state["ld"] += 1
            j = p1_state["cons"]
            if j < len(p1_loads) and j < p1_state["ld"]:
                Ea.waitev(p1_state["ldev"][j])
                ev = Ea.op(actv_acc(A(O_P1D, 512), A(O_P1 + (j % 2) * 512, 512),
                                    AF.Abs, A(O_GACC + j, 1)), inc=act)
                p1_state["consev"][j] = ev
                p1_state["cons"] += 1

    def xchunk(k):
        return A(O_XS + (k // 8) * 4096 + (k % 8) * 512, 512)

    Ep.wait(ld, t_kvw)
    last_x_use = None
    for tb in range(8):
        Es.waitev(last_x_use)
        Eg.waitev(last_x_use)
        for k in range(8):
            t_x = Es.dma(xchunk(k), xTf[k][:, tb * 512:tb * 512 + 512], ld)
        for k in range(8, 16):
            t_xg = Eg.dma(xchunk(k), xTf[k][:, tb * 512:tb * 512 + 512], gld)
        g_wait(Es, "cksk")
        Es.dma(A(O_CK, 512), cosk[:, tb * 512:tb * 512 + 512], ld)
        t_cs = Es.dma(A(O_SK, 512), sink[:, tb * 512:tb * 512 + 512], ld)
        for o8 in range(8):
            pump_pass1(2)
            is_v = o8 >= 4
            o = o8 % 4
            ppj = pp[o8 % 2]
            g_wait(Ep, f"pp{o8 % 2}")
            Ep.wait(ld, t_x[1])
            Ep.wait(gld, t_xg[1])
            for k in range(16):
                t_pe = Ep.op(mm(ppj[:, 0:512], kvw(is_v, o, k), xchunk(k),
                                start=(k == 0), stop=(k == 15)),
                             inc=(pe if k == 15 else None))
            if not is_v:
                g_wait(Ev, "KA")
                Ev.waitev(t_pe)
                t_ev = Ev.op(cp(A(O_KA, 512), ppj[:, 0:512]), inc=vec)
                g_set(f"pp{o8 % 2}", t_ev)
                g_wait(Ep, "pp23")
                Ep.waitev(t_ev)
                t_pr = Ep.op(mm(pp[2][:, 0:512], P2ap, A(O_KA, 512)), inc=pe)
                Ev.wait(ld, t_cs[1])
                Ev.op(tt(A(O_KB, 512), A(O_KA, 512), A(O_CK, 512), ALU.mult))
                Ev.waitev(t_pr)
                Ev.op(tt(A(O_KC, 512), pp[2][:, 0:512], A(O_SK, 512), ALU.mult))
                t_rope = Ev.op(tt(A(O_KA, 512), A(O_KB, 512), A(O_KC, 512),
                                  ALU.add), inc=vec)
                g_set("pp23", t_rope)
                g_set("cksk", t_rope)
                Ea.waitev(t_rope)
                t_sp = Ea.dma(kt_sc[o][:, tb * 512:tb * 512 + 512],
                              A(O_KA, 512), st)
                t_sq = Ea.op(actv(A(O_KNS, 512), A(O_KA, 512), AF.Square),
                             inc=act)
                g_set("KA", (st, t_sp[1]), t_sq)
                g_wait(Ep, "pp4")
                Ep.waitev(t_sq)
                t_kn = Ep.op(mm(pp[4][0:8, 0:512], s_indk[:, o * 8:o * 8 + 8],
                                A(O_KNS, 512)), inc=pe)
                Ev.waitev(t_kn)
                Ev.op(cp(A(O_KB, 512, 0, 8), pp[4][0:8, 0:512]))
                vsep()
                t_knr = Ev.op(red(A(O_KN + tb * 4, 4, 0, 8),
                                  A(O_KB, 512, 0, 8).rearrange(
                                      "p (a b) -> p a b", a=4),
                                  ALU.max), inc=vec)
                g_set("pp4", t_knr)
            else:
                g_wait(Ev, "KB")
                Ev.waitev(t_pe)
                t_ev = Ev.op(cp(A(O_KB, 512), ppj[:, 0:512]), inc=vec)
                g_set(f"pp{o8 % 2}", t_ev)
                g_wait(Ev, "vstg")
                g_wait(Ep, "pp6")
                Ep.waitev(t_ev)
                for r in range(4):
                    t_tr = Ep.op(trn(pp[6][:, r * 128:r * 128 + 128],
                                     A(O_KB + r * 128, 128)),
                                 inc=(pe if r == 3 else None))
                Ev.waitev(t_tr)
                for r in range(4):
                    Ev.op(cp(A(O_VSTG + r * 130, 64),
                             pp[6][:, r * 128:r * 128 + 64]))
                    t_vc = Ev.op(cp(A(O_VSTG + r * 130 + 65, 64),
                                    pp[6][:, r * 128 + 64:r * 128 + 128]),
                                 inc=vec)
                g_set("pp6", t_vc)
                g_set("KB", t_vc)
                Ea.waitev(t_vc)
                t_vs = Ea.dma(vt_sc[o][:, tb * 4 * 130:(tb * 4 + 4) * 130],
                              A(O_VSTG, 520), st)
                g_set("vstg", (st, t_vs[1]))
        last_x_use = (pe, pe.n)
    t_c_spill = st.n

    barrier()
    Es.dma(A(O_CQ, 1024), cosq[:], ld)
    t_cq = Es.dma(A(O_SQ, 1024), sinq[:], ld)

    def QT(j):
        return A(O_QT + j * 1024, 1024)

    t_qn = None
    for half in range(2):
        g_wait(Es, "xmh")
        for k in range(16):
            t_xm = Es.dma(A(O_XMH + k * 512, 512),
                          xTm[k][:, half * 512:half * 512 + 512], ld)
        t_xm_last_use = None
        for o in range(16):
            pump_pass1(4)
            wbuf = A(O_WQS + (o % 2) * 2048, 2048)
            g_wait(Es, f"wqs{o % 2}")
            t_w = Es.dma(wbuf, wqt[o], ld)
            ppj = pp[o % 2]
            g_wait(Ep, f"pp{o % 2}")
            Ep.wait(ld, t_xm[1])
            Ep.wait(ld, t_w[1])
            for k in range(16):
                t_pe = Ep.op(mm(ppj[:, 0:512], wbuf[:, k * 128:k * 128 + 128],
                                A(O_XMH + k * 512, 512),
                                start=(k == 0), stop=(k == 15)),
                             inc=(pe if k == 15 else None))
            g_set(f"wqs{o % 2}", t_pe)
            t_xm_last_use = t_pe
            qdst = QT(o)[:, half * 512:half * 512 + 512]
            Ev.waitev(t_pe)
            t_ev = Ev.op(cp(qdst, ppj[:, 0:512]), inc=vec)
            g_set(f"pp{o % 2}", t_ev)
            g_wait(Ep, "pp23")
            Ep.waitev(t_ev)
            t_pr = Ep.op(mm(pp[2][:, 0:512], P2ap, qdst), inc=pe)
            Ev.wait(ld, t_cq[1])
            Ev.op(tt(A(O_SA, 512), qdst, A(O_CQ + half * 512, 512), ALU.mult))
            Ev.waitev(t_pr)
            Ev.op(tt(A(O_SB, 512), pp[2][:, 0:512],
                     A(O_SQ + half * 512, 512), ALU.mult))
            t_rope = Ev.op(tt(qdst, A(O_SA, 512), A(O_SB, 512), ALU.add),
                           inc=vec)
            g_set("pp23", t_rope)
            g_wait(Ea, "SC")
            Ea.waitev(t_rope)
            t_sq = Ea.op(actv(A(O_SC, 512), qdst, AF.Square), inc=act)
            if o == 0:
                g_wait(Ep, f"pp{4 + half}")
            Ep.waitev(t_sq)
            t_qn = Ep.op(mm(pp[4 + half][0:32, 0:512],
                            A(O_INDQ + o * 32, 32), A(O_SC, 512),
                            start=(o == 0), stop=(o == 15)), inc=pe)
            g_set("SC", t_qn)
        g_set("xmh", t_xm_last_use)
        Ev.waitev(t_qn)
        t_qe = Ev.op(cp(A(O_QN + half * 512, 512, 0, 32),
                        pp[4 + half][0:32, 0:512]), inc=vec)
        g_set(f"pp{4 + half}", t_qe)

    t_qm = Ev.op(red(A(O_CG, 8, 0, 32),
                     A(O_QN, 1024, 0, 32).rearrange("p (a b) -> p a b", a=8),
                     ALU.max), inc=vec)
    Ea.waitev(t_qm)
    t_qs = Ea.op(actv(A(O_CT, 8, 0, 32), A(O_CG, 8, 0, 32), AF.Sqrt), inc=act)
    for s in range(NSLOT):
        tkr = Ev.op(red(A(O_KM + s, 1, 0, 8), A(O_KN, PFX[s], 0, 8), ALU.max),
                    inc=(vec if s == NSLOT - 1 else None))
    Ea.waitev(tkr)
    t_ks = Ea.op(actv(A(O_CT + 8, 8, 0, 8), A(O_KM, 8, 0, 8), AF.Sqrt), inc=act)
    g_wait(Ep, "pp6")
    Ep.waitev(t_ks)
    t_kx = Ep.op(mm(pp[6][0:32, 0:8], A(O_IKQ, 32, 0, 8), A(O_CT + 8, 8, 0, 8)),
                 inc=pe)
    Ev.waitev(t_kx)
    Ev.waitev(t_qs)
    Ev.op(lambda e: e.memset(A(O_CT + 16, 128), 0.0))
    t_c1 = Ev.op(tt(A(O_CT + 16, 8, 0, 32), A(O_CT, 8, 0, 32), pp[6][0:32, 0:8],
                    ALU.mult), inc=vec)
    g_set("pp6", t_c1)
    g_wait(Ep, "pp6")
    Ep.waitev(t_c1)
    t_tr1 = Ep.op(trn(pp[6][:, 0:128], A(O_CT + 16, 128)), inc=pe)
    Ev.waitev(t_tr1)
    Ev.op(lambda e: e.memset(A(O_CT + 16, 128), 0.0))
    t_gm = Ev.op(red(A(O_CT + 16, 8, 0, 8),
                     pp[6][0:8, 0:32].rearrange("p (a b) -> p a b", a=8),
                     ALU.max), inc=vec)
    g_set("pp6", t_gm)
    g_wait(Ep, "pp6")
    Ep.waitev(t_gm)
    t_tr2 = Ep.op(trn(pp[6][:, 0:128], A(O_CT + 16, 128)), inc=pe)
    Ev.waitev(t_tr2)
    t_c2 = Ev.op(ts(A(O_CT + 144, 8, 0, 8), pp[6][0:8, 0:8], -0.125,
                    ALU.mult), inc=vec)
    g_set("pp6", t_c2)
    g_wait(Ep, "pp7")
    Ep.waitev(t_c2)
    for g8 in range(8):
        t_bb = Ep.op(mm(pp[7][:, g8 * 8:g8 * 8 + 8],
                        A(O_INDG + g8 * 128, 128, 0, 8),
                        A(O_CT + 144, 8, 0, 8)),
                     inc=(pe if g8 == 7 else None))
    Ev.waitev(t_bb)
    t_bias = Ev.op(cp(A(O_BIAS, 64), pp[7][:, 0:64]), inc=vec)
    g_set("pp7", t_bias)

    if DBG_SCRATCH[0]:
        Eg.wait(vec, vec.n)
        Eg.wait(pe, pe.n)
        Eg.dma(dbgA[:, 0:64], A(O_BIAS, 64), st)
        Eg.dma(dbgA[0:8, 192:200], A(O_KM, 8, 0, 8), st)
        Eg.dma(dbgA[0:8, 200:232], A(O_KN, 32, 0, 8), st)
        Eg.dma(dbgA[0:32, 232:240], A(O_CG, 8, 0, 32), st)
        Eg.dma(dbgA[:, 704:1728], A(O_QT, 1024), st)
    barrier()
    qt_my = None
    ocnt = 0
    for sg in range(2):
        barrier()
        g_wait(Es, "maskarea")
        for si in range(4):
            for j in range(4):
                t_mk = Es.dma(Ab(O_MASK + (si * 4 + j) * 256, 512),
                              masksI[sg * 4 + si][j], ld)
        if DBG_SCRATCH[0] and sg == 0:
            Es.wait(ld, t_mk[1])
            Es.dma(dbgA[:, 2752:3776], A(O_MASK, 1024), st)
        t_kgvg = None
        for G in range(4):
            g_wait(Es, "kgvg")
            Es.wait(st, t_c_spill)
            Es.dma(A(O_KG, 4096), kt_sc[G], ld)
            t_kgvg = Es.dma(A(O_VG, 4160), vt_sc[G], ld)
            for grp in range(2):
                for si in range(4):
                    pump_pass1(8)
                    s = sg * 4 + si
                    po = pp[4 + ocnt % 2]
                    g_wait(Ep, f"pp{4 + ocnt % 2}")
                    def emit_L(kt, s=s, G=G, grp=grp, si=si,
                               t_kgvg=t_kgvg, t_mk=t_mk):
                        Lp = pp[kt % 2]
                        g_wait(Ep, f"pp{kt % 2}")
                        Ep.wait(ld, t_kgvg[1])
                        Ep.wait(ld, t_mk[1])
                        for r in range(4):
                            t_L = Ep.op(
                                mm(Lp[:, r * 128:r * 128 + 128],
                                   A(O_KG + kt * 128, 128, grp * 64, 64),
                                   A(O_QT + (4 * G + r) * 1024 + s * 128, 128,
                                     grp * 64, 64)),
                                inc=(pe if r == 3 else None))
                        ev = t_L
                        if kt >= PFX[s] - 4:
                            j = kt - (PFX[s] - 4)
                            Ev.waitev(t_L)
                            ev = Ev.op(
                                tt(Lp[:, 0:512], Lp[:, 0:512],
                                   Ab(O_MASK + (si * 4 + j) * 256, 512),
                                   ALU.add), inc=vec)
                        return ev

                    pre = emit_L(0)
                    for kt in range(PFX[s]):
                        nxt_pre = (emit_L(kt + 1) if kt + 1 < PFX[s]
                                   else None)
                        g_wait(Ea, f"pbuf{kt % 2}")
                        Ea.waitev(pre)
                        t_exp = Ea.op(
                            actv(A(O_PB + (kt % 2) * 512, 512),
                                 pp[kt % 2][:, 0:512], AF.Exp,
                                 bias=A(O_BIAS + (G * 2 + grp) * 8 + s, 1),
                                 scale=0.125), inc=act)
                        g_set(f"pp{kt % 2}", t_exp)
                        Ep.waitev(t_exp)
                        for r in range(4):
                            t_av = Ep.op(
                                mm(po[0:65, r * 128:r * 128 + 128],
                                   A(O_VG + kt * 130 + grp * 65, 65),
                                   A(O_PB + (kt % 2) * 512 + r * 128, 128),
                                   start=(kt == 0 and r == 0),
                                   stop=(kt == PFX[s] - 1)),
                                inc=(pe if r == 3 else None))
                        g_set(f"pbuf{kt % 2}", t_av)
                        pre = nxt_pre
                    g_wait(Ev, "oev")
                    Ev.waitev(t_av)
                    t_oe = Ev.op(cp(A(O_EV, 512, 0, 65), po[0:65, 0:512]),
                                 inc=vec)
                    g_set(f"pp{4 + ocnt % 2}", t_oe)
                    for r in range(4):
                        h = grp * 4 + r
                        j = 4 * G + r
                        dst_cols = (O_OSB + j * 512 + si * 128, 128)
                        if grp == 0:
                            t_op = Ev.op(cp(A(dst_cols[0], 128, 0, 64),
                                            A(O_EV + r * 128, 128, 0, 64)),
                                         inc=vec)
                        else:
                            Ea.waitev(t_oe)
                            t_op = Ea.dma(A(dst_cols[0], 128, 64, 64),
                                          A(O_EV + r * 128, 128, 0, 64), st)
                        g_wait(Ep, "pp6")
                        Ep.waitev(t_oe)
                        t_tr = Ep.op(trn(pp[6][:, r * 128:r * 128 + 128],
                                         A(O_EV + r * 128, 128)),
                                     inc=(pe if r == 3 else None))
                    Ev.waitev(t_tr)
                    t_dens = None
                    for r in range(4):
                        h = 8 * G + grp * 4 + r
                        t_dens = Ev.op(cp(A(O_DEN + si * 32 + h, 1),
                                          pp[6][:, r * 128 + 64:r * 128 + 65]),
                                       inc=(vec if r == 3 else None))
                    g_set("pp6", t_dens)
                    g_set("oev", t_dens, t_op)
                    ocnt += 1
            g_set("kgvg", (pe, pe.n))
        for si in range(4):
            Ev.op(lambda e: e.memset(A(O_CT + 32, 96), 0.0))
            vsep()
            t_rc = Ev.op(recip(A(O_CT, 32), A(O_DEN + si * 32, 32)), inc=vec)
            vsep()
            g_wait(Ep, "pp6")
            Ep.waitev(t_rc)
            t_tr = Ep.op(trn(pp[6][:, 0:128], A(O_CT, 128)), inc=pe)
            Ev.waitev(t_tr)
            t_dT = Ev.op(cp(A(O_CT + 128, 128, 0, 32), pp[6][0:32, 0:128]),
                         inc=vec)
            g_set("pp6", t_dT)
            for j in range(16):
                g_wait(Ep, "pp7")
                Ep.waitev(t_dT)
                t_rp = Ep.op(mm(pp[7][:, 0:128], A(O_INDR + j * 128, 128, 0, 32),
                                A(O_CT + 128, 128, 0, 32)), inc=pe)
                Ev.waitev(t_rp)
                t_on = Ev.op(tt(A(O_OSB + j * 512 + si * 128, 128),
                                A(O_OSB + j * 512 + si * 128, 128),
                                pp[7][:, 0:128], ALU.mult), inc=vec)
                g_set("pp7", t_on)
        if DBG_SCRATCH[0] and sg == 0:
            Eg.wait(vec, vec.n)
            Eg.dma(dbgA[:, 64:192], A(O_DEN, 128), st)
            Eg.dma(dbgA[:, 1728:2240], A(O_OSB, 512), st)
            Eg.dma(dbgA[:, 2240:2752], A(O_OSB + 512, 512), st)
        t_y_last = None
        for o in range(16):
            wbuf = A(O_WQS + (o % 2) * 2048, 2048)
            g_wait(Es, f"wos{o % 2}")
            Es.wait(vec, vec.n)
            t_w = Es.dma(wbuf, wot[o], ld)
            ppj = pp[o % 2]
            g_wait(Ep, f"pp{o % 2}")
            Ep.wait(ld, t_w[1])
            for k in range(16):
                t_pe = Ep.op(mm(ppj[:, 0:512], wbuf[:, k * 128:k * 128 + 128],
                                A(O_OSB + k * 512, 512),
                                start=(k == 0), stop=(k == 15)),
                             inc=(pe if k == 15 else None))
            g_set(f"wos{o % 2}", t_pe)
            xb = A(O_PB + (o % 2) * 512, 512)
            g_wait(Es, f"xres{o % 2}")
            t_xr = Es.dma(xb, xTm[o][:, sg * 512:sg * 512 + 512], ld)
            Ev.waitev(t_pe)
            Ev.wait(ld, t_xr[1])
            g_wait(Ev, "dbgtap")
            t_v = Ev.op(tt(A(O_YSB + o * 512, 512), ppj[:, 0:512], xb, ALU.add),
                        inc=vec)
            g_set(f"pp{o % 2}", t_v)
            g_set(f"xres{o % 2}", t_v)
            Ea.waitev(t_v)
            g_wait(Ea, "EVsq")
            t_sq = Ea.op(actv(A(O_EV, 512), A(O_YSB + o * 512, 512), AF.Square),
                         inc=act)
            if o == 0:
                g_wait(Ep, "pp2")
                g_wait(Ep, "pp3")
            Ep.waitev(t_v)
            Ep.op(mm(pp[2][0:1, 0:512], ONESC, A(O_YSB + o * 512, 512),
                     start=(o == 0), stop=(o == 15)))
            Ep.waitev(t_sq)
            t_st2 = Ep.op(mm(pp[3][0:1, 0:512], ONESC, A(O_EV, 512),
                             start=(o == 0), stop=(o == 15)), inc=pe)
            g_set("EVsq", t_st2)
            t_y_last = t_st2
        Ev.waitev(t_y_last)
        Ev.op(ts(A(O_ROWA, 512, 0, 1), pp[2][0:1, 0:512], INV_D, ALU.mult))
        Ev.op(ts(A(O_ROWB, 512, 0, 1), pp[3][0:1, 0:512], INV_D, ALU.mult))
        Ev.op(tt(A(O_ROWB + 512, 512, 0, 1), A(O_ROWA, 512, 0, 1),
                 A(O_ROWA, 512, 0, 1), ALU.mult))
        Ev.op(tt(A(O_ROWB, 512, 0, 1), A(O_ROWB, 512, 0, 1),
                 A(O_ROWB + 512, 512, 0, 1), ALU.subtract))
        t_var = Ev.op(ts(A(O_ROWB, 512, 0, 1), A(O_ROWB, 512, 0, 1), 1e-5,
                         ALU.add), inc=vec)
        Ea.waitev(t_var)
        t_sd = Ea.op(actv(A(O_ROWA + 512, 512, 0, 1), A(O_ROWB, 512, 0, 1),
                          AF.Sqrt), inc=act)
        Ev.waitev(t_sd)
        Ev.op(recip(A(O_ROWA + 512, 512, 0, 1), A(O_ROWA + 512, 512, 0, 1)))
        Ev.op(tt(A(O_ROWB + 512, 512, 0, 1), A(O_ROWA + 512, 512, 0, 1),
                 A(O_ROWA + 512, 512, 0, 1), ALU.mult))
        Ev.op(tt(A(O_ROWB + 512, 512, 0, 1), A(O_ROWB + 512, 512, 0, 1),
                 A(O_ROWB, 512, 0, 1), ALU.mult))
        Ev.op(ts(A(O_ROWB + 512, 512, 0, 1), A(O_ROWB + 512, 512, 0, 1),
                 -0.5, ALU.mult, 1.5, ALU.add))
        t_rst = Ev.op(tt(A(O_ROWA + 512, 512, 0, 1), A(O_ROWA + 512, 512, 0, 1),
                         A(O_ROWB + 512, 512, 0, 1), ALU.mult), inc=vec)
        g_wait(Ep, "pp4")
        g_wait(Ep, "pp5")
        Ep.waitev(t_rst)
        t_mr = Ep.op(mm(pp[4][:, 0:512], ONES1, A(O_ROWA, 512, 0, 1)), inc=pe)
        t_rr = Ep.op(mm(pp[5][:, 0:512], ONES1, A(O_ROWA + 512, 512, 0, 1)),
                     inc=pe)
        t_x1 = None
        for o in range(16):
            Ev.waitev(t_rr)
            Ev.op(tt(A(O_YSB + o * 512, 512), A(O_YSB + o * 512, 512),
                     pp[4][:, 0:512], ALU.subtract))
            t_x1 = Ev.op(tt(A(O_YSB + o * 512, 512), A(O_YSB + o * 512, 512),
                            pp[5][:, 0:512], ALU.mult), inc=vec)
            Ea.waitev(t_x1)
            Ea.dma(x1_sc[o][:, sg * 512:sg * 512 + 512], A(O_YSB + o * 512, 512),
                   st)
        g_set("pp4", t_x1)
        g_set("pp5", t_x1)
        Ev.wait(st, st.n)
        for cch in range(16):
            ppj = pp[cch % 2]
            g_wait(Ep, f"pp{cch % 2}")
            Ep.waitev(t_x1)
            t_hh = Ep.op(mm(ppj[:, 0:512], H128ap, A(O_YSB + cch * 512, 512)),
                         inc=pe)
            Ev.waitev(t_hh)
            t_hc = Ev.op(cp(A(O_YSB + cch * 512, 512), ppj[:, 0:512]), inc=vec)
            g_set(f"pp{cch % 2}", t_hc)
        for stg in range(4):
            step = 1 << stg
            run = min(step, 8)
            for cb in range(0, 16, 2 * step):
                for sub in range(0, step, run):
                    a = A(O_YSB + (cb + sub) * 512, run * 512)
                    b = A(O_YSB + (cb + step + sub) * 512, run * 512)
                    Ev.op(tt(a, a, b, ALU.add))
                    t_bf = Ev.op(stt(b, b, -2.0, a, ALU.mult, ALU.add),
                                 inc=vec)
        t_am = Ev.op(red(A(O_EV, 512), A(O_YSB, 8192).rearrange(
            "p (c t) -> p t c", c=16), ALU.max, absval=True), inc=vec)
        for j in range(4):
            g_wait(Ep, "pp6")
            Ep.waitev(t_am)
            t_tr = Ep.op(trn(pp[6][:, 0:128], A(O_EV + j * 128, 128)), inc=pe)
            Ev.waitev(t_tr)
            t_ac = Ev.op(red(A(O_CT + j, 1), pp[6][:, 0:128], ALU.max),
                         inc=vec)
            g_set("pp6", t_ac)
        vsep()
        Ev.op(ts(A(O_CT + 4, 4), A(O_CT, 4), 1e-5, ALU.max,
                 1.0 / 127.0, ALU.mult))
        vsep()
        t_sc1 = Ev.op(recip(A(O_CT + 8, 4), A(O_CT + 4, 4)), inc=vec)
        vsep()
        for j in range(4):
            g_wait(Ep, "pp6")
            Ep.waitev(t_sc1)
            t_tr = Ep.op(trn(pp[6][:, 0:128], A(O_CT + 8 + j, 128)), inc=pe)
            Ev.waitev(t_tr)
            t_rw = Ev.op(cp(A(O_ROW1 + sg * 512 + j * 128, 128, 0, 1),
                            pp[6][0:1, 0:128]), inc=vec)
            g_set("pp6", t_rw)
        g_wait(Ep, "pp7")
        Ep.waitev(t_rw)
        t_rep = Ep.op(mm(pp[7][:, 0:512], ONES1,
                         A(O_ROW1 + sg * 512, 512, 0, 1)), inc=pe)
        t_q1 = None
        for cch in range(16):
            Ev.waitev(t_rep)
            t_q1 = Ev.op(tt(A(O_YSB + cch * 512, 512),
                            A(O_YSB + cch * 512, 512), pp[7][:, 0:512],
                            ALU.mult), inc=(vec if cch == 15 else None))
        g_set("pp7", t_q1)
        for grpq in range(4):
            base = O_YSB + grpq * 2048
            Ev.op(ts(A(O_MASK, 2048), A(base, 2048), 128.5, ALU.add))
            for j in range(7, -1, -1):
                pw = float(1 << j)
                Ev.op(ts(A(O_MASK + 2048, 2048), A(O_MASK, 2048), pw,
                         ALU.is_ge, -pw, ALU.mult))
                Ev.op(tt(A(O_MASK, 2048), A(O_MASK, 2048),
                         A(O_MASK + 2048, 2048), ALU.add))
            Ev.op(tt(A(base, 2048), A(base, 2048), A(O_MASK, 2048),
                     ALU.subtract))
            for q4 in range(4):
                cch = grpq * 4 + q4
                g_wait(Ev, f"q1s{cch % 2}")
                t_q1 = Ev.op(ts(Ab(O_FS2 + (cch % 2) * 256, 512),
                                A(base + q4 * 512, 512), 0.5, ALU.add),
                             inc=vec)
                Ea.waitev(t_q1)
                t_q1s = Ea.dma(xq1_sc[cch][:, sg * 512:sg * 512 + 512],
                               Ab(O_FS2 + (cch % 2) * 256, 512), st)
                g_set(f"q1s{cch % 2}", (st, t_q1s[1]))
        if DBG_SCRATCH[0] and sg == 0:
            Eg.wait(vec, t_q1[1])
            t_tap = Eg.dma(dbgA[:, 3776:3792], A(O_CT, 16), st)
            t_tap = Eg.dma(dbgA[0:1, 3264:3776], A(O_ROW1, 512, 0, 1), st)
            t_tap = Eg.dma(dbgA[:, 2752:3264], A(O_YSB, 512), st)
            g_set("dbgtap", t_tap)
        g_set("maskarea", (st, st.n), t_q1)
    t_d_done = st.n

    pump_pass1(2 * len(p1_loads))
    barrier()
    ev_r = Ev.op(red(A(O_CT, 3),
                     A(O_GACC, 768).rearrange("p (a b) -> p a b", a=3),
                     ALU.add), inc=vec)
    vsep()
    g_wait(Ep, "pp6")
    Ep.waitev(ev_r)
    t_cmb = Ep.op(mm(pp[6][0:1, 0:3], ONESC, A(O_CT, 3)), inc=pe)
    Ev.waitev(t_cmb)
    t_mn = Ev.op(ts(A(O_SVAL, 3, 0, 1), pp[6][0:1, 0:3], 1.0 / (II * D),
                    ALU.mult, 1e-5, ALU.max), inc=vec)
    Ep.waitev(t_mn)
    t_bc = Ep.op(mm(pp[6][:, 0:3], ONES1, A(O_SVAL, 3, 0, 1)), inc=pe)
    Ev.waitev(t_bc)
    Ev.op(ts(A(O_SG, 3), pp[6][:, 0:3], 0.5, ALU.mult))
    t_sg = Ev.op(ts(A(O_SG + 3, 3), pp[6][:, 0:3], -0.5, ALU.mult), inc=vec)
    g_set("pp6", t_sg)
    Ev.op(recip(A(O_ROWA, 1024, 0, 1), A(O_ROW1, 1024, 0, 1)))
    Ev.op(ts(A(O_ROWB, 1024, 0, 1), A(O_ROWA, 1024, 0, 1),
             A(O_SVAL + 1, 1, 0, 1), ALU.mult, INV_SQ1 * 0.5, ALU.mult))
    t_csg = Ev.op(ts(A(O_ROWA, 1024, 0, 1), A(O_ROWA, 1024, 0, 1),
                     A(O_SVAL, 1, 0, 1), ALU.mult, INV_SQ1 * 0.5, ALU.mult),
                  inc=vec)

    if DBG_SCRATCH[0]:
        Eg.wait(vec, vec.n)
        Eg.wait(act, act.n)
        Eg.dma(dbgA[:, 896:1664], A(O_GACC, 768), st)
        Eg.dma(dbgA[:, 1664:1680], A(O_CT, 16), st)
        Eg.dma(dbgA[0:1, 3776:3780], A(O_SVAL, 4, 0, 1), st)
        Eg.dma(dbgA[:, 3780:3788], A(O_SG, 8), st)
        Eg.dma(dbgA[0:1, 3788:3800], A(O_ROWA, 12, 0, 1), st)
        Eg.dma(dbgA[0:1, 3800:3812], A(O_ROWB, 12, 0, 1), st)
        Eg.dma(dbgA[0:1, 3812:3824], A(O_ROW1, 12, 0, 1), st)
    barrier()
    for ha in range(2):
        barrier()
        for k in range(16):
            t_xq = Es.dma(Ab(O_XQ1 + k * 256, 512),
                          xq1_sc[k][:, ha * 512:ha * 512 + 512], ld)
        g_wait(Ep, "pp4")
        Ep.waitev(t_csg)
        t_rg = Ep.op(mm(pp[4][:, 0:512], ONES1,
                        A(O_ROWA + ha * 512, 512, 0, 1)), inc=pe)
        Ev.waitev(t_rg)
        t_rgc = Ev.op(cp(A(O_FS1, 512), pp[4][:, 0:512]), inc=vec)
        g_set("pp4", t_rgc)
        WGo = (O_WG, O_INDR)
        WUo = (O_WU, O_WU1)
        tern_ev = {}

        def f1_load(ic):
            if ic >= 64:
                return None
            g_wait(Es, f"wg{ic % 2}")
            t_wg = Es.dma(A(WGo[ic % 2], 2048), wgt[ic], ld)
            g_wait(Es, f"wu{ic % 2}")
            t_wu = Es.dma(A(WUo[ic % 2], 2048), wut[ic], ld)
            return (t_wg, t_wu)

        def f1_tern(ic, tw):
            if ic >= 64:
                return
            for mat in range(2):
                toff = (WGo if mat == 0 else WUo)[ic % 2]
                src = A(toff, 2048)
                sgc = 0 if mat == 0 else 1
                Ea.wait(ld, tw[mat][1])
                Ea.waitev(t_sg)
                for q in range(4):
                    qq = mat * 4 + q
                    s1b = (O_FS3, O_INDQ)[qq % 2]
                    s2b = (O_GACC, O_RS2)[qq % 2]
                    g_wait(Ea, f"sgn{qq % 2}")
                    Ea.op(actv(A(s1b, 512), src[:, q * 512:q * 512 + 512],
                               AF.Sign, bias=A(O_SG + 3 + sgc, 1)))
                    t_sg2 = Ea.op(actv(A(s2b, 512),
                                       src[:, q * 512:q * 512 + 512],
                                       AF.Sign, bias=A(O_SG + sgc, 1)),
                                  inc=act)
                    Ev.waitev(t_sg2)
                    t_tn = Ev.op(tt(Ab(toff + q * 256, 512), A(s1b, 512),
                                    A(s2b, 512), ALU.add), inc=vec)
                    g_set(f"sgn{qq % 2}", t_tn)
                tern_ev[(mat, ic)] = t_tn

        pend = f1_load(0)
        f1_tern(0, pend)
        nxt = f1_load(1)
        for ic in range(64):
            t_tg = tern_ev[(0, ic)]
            t_tu = tern_ev[(1, ic)]
            ppg = pp[ic % 2]
            ppu = pp[2 + (ic % 2)]
            g_wait(Ep, f"ppg{ic % 2}")
            g_wait(Ep, f"ppu{ic % 2}")
            Ep.waitev(t_tg)
            Ep.waitev(t_tu)
            Ep.wait(ld, t_xq[1])
            for k in range(16):
                Ep.op(mm(ppg[:, 0:512],
                         Ab(WGo[ic % 2] + k * 64, 128),
                         Ab(O_XQ1 + k * 256, 512),
                         start=(k == 0), stop=(k == 15)),
                      inc=(pe if k == 15 else None))
            t_gd = (pe, pe.n)
            for k in range(16):
                Ep.op(mm(ppu[:, 0:512], Ab(WUo[ic % 2] + k * 64, 128),
                         Ab(O_XQ1 + k * 256, 512),
                         start=(k == 0), stop=(k == 15)),
                      inc=(pe if k == 15 else None))
            t_ud = (pe, pe.n)
            g_set(f"wg{ic % 2}", t_gd)
            g_set(f"wu{ic % 2}", t_ud)
            Ev.waitev(t_gd)
            Ev.waitev(t_rgc)
            t_gs = Ev.op(tt(A(O_SH + ic * 512, 512), ppg[:, 0:512],
                            A(O_FS1, 512), ALU.mult), inc=vec)
            g_set(f"ppg{ic % 2}", t_gs)
            Ea.waitev(t_gs)
            g_wait(Ea, "FS2")
            t_si = Ea.op(actv(A(O_FS2, 512), A(O_SH + ic * 512, 512), AF.Silu),
                         inc=act)
            f1_tern(ic + 1, nxt)
            nxt = f1_load(ic + 2)
            Ev.waitev(t_si)
            Ev.waitev(t_ud)
            t_h = Ev.op(tt(A(O_SH + ic * 512, 512), A(O_FS2, 512),
                           ppu[:, 0:512], ALU.mult), inc=vec)
            g_set(f"ppu{ic % 2}", t_h)
            g_set("FS2", t_h)
        if DBG_SCRATCH[0] and ha == 0:
            Eg.wait(vec, vec.n)
            t_htap = Eg.dma(dbgA[:, 3264:3776], A(O_SH, 512), st)
            g_set("htap", t_htap)
        Ev.wait(pe, pe.n)
        g_wait(Ev, "htap")
        for cch in range(64):
            ppj = pp[4 + cch % 2]
            g_wait(Ep, f"pp{4 + cch % 2}")
            Ep.wait(vec, vec.n)
            t_hh = Ep.op(mm(ppj[:, 0:512], H128ap, A(O_SH + cch * 512, 512)),
                         inc=pe)
            Ev.waitev(t_hh)
            t_hc = Ev.op(cp(A(O_SH + cch * 512, 512), ppj[:, 0:512]), inc=vec)
            g_set(f"pp{4 + cch % 2}", t_hc)
        for stg in range(6):
            step = 1 << stg
            run = min(step, 8)
            for cb in range(0, 64, 2 * step):
                for sub in range(0, step, run):
                    a = A(O_SH + (cb + sub) * 512, run * 512)
                    b = A(O_SH + (cb + step + sub) * 512, run * 512)
                    Ev.op(tt(a, a, b, ALU.add))
                    t_bf = Ev.op(stt(b, b, -2.0, a, ALU.mult, ALU.add),
                                 inc=vec)
        Ev.wait(pe, pe.n)
        t_am = Ev.op(red(A(O_STGV, 512), A(O_SH, 32768).rearrange(
            "p (c t) -> p t c", c=64), ALU.max, absval=True), inc=vec)
        for j in range(4):
            g_wait(Ep, "pp6")
            Ep.waitev(t_am)
            t_tr = Ep.op(trn(pp[6][:, 0:128], A(O_STGV + j * 128, 128)), inc=pe)
            Ev.waitev(t_tr)
            t_ac = Ev.op(red(A(O_CT + j, 1), pp[6][:, 0:128], ALU.max), inc=vec)
            g_set("pp6", t_ac)
        vsep()
        Ev.op(ts(A(O_CT + 4, 4), A(O_CT, 4), 1e-5, ALU.max,
                 1.0 / 127.0, ALU.mult))
        vsep()
        t_sc2 = Ev.op(recip(A(O_CT + 8, 4), A(O_CT + 4, 4)), inc=vec)
        vsep()
        for j in range(4):
            g_wait(Ep, "pp6")
            Ep.waitev(t_sc2)
            t_tr = Ep.op(trn(pp[6][:, 0:128], A(O_CT + 8 + j, 128)), inc=pe)
            Ev.waitev(t_tr)
            t_rw = Ev.op(cp(A(O_ROW2 + ha * 512 + j * 128, 128, 0, 1),
                            pp[6][0:1, 0:128]), inc=vec)
            g_set("pp6", t_rw)
        g_wait(Ep, "pp7")
        Ep.waitev(t_rw)
        t_rep = Ep.op(mm(pp[7][:, 0:512], ONES1,
                         A(O_ROW2 + ha * 512, 512, 0, 1)), inc=pe)
        Ev.waitev(t_rep)
        t_repc = Ev.op(cp(A(O_FS1, 512), pp[7][:, 0:512]), inc=vec)
        g_set("pp7", t_repc)
        for cch in range(64):
            Ev.waitev(t_repc)
            Ev.op(tt(A(O_SH + cch * 512, 512), A(O_SH + cch * 512, 512),
                     A(O_FS1, 512), ALU.mult))
        for grpq in range(16):
            base = O_SH + grpq * 2048
            Ev.op(ts(A(O_UU, 2048), A(base, 2048), 128.5, ALU.add))
            for j in range(7, -1, -1):
                pw = float(1 << j)
                Ev.op(ts(A(O_MM, 2048), A(O_UU, 2048), pw, ALU.is_ge,
                         -pw, ALU.mult))
                Ev.op(tt(A(O_UU, 2048), A(O_UU, 2048), A(O_MM, 2048),
                         ALU.add))
            Ev.op(tt(A(base, 2048), A(base, 2048), A(O_UU, 2048),
                     ALU.subtract))
            for q4 in range(4):
                cch = grpq * 4 + q4
                g_wait(Ev, f"q2s{cch % 2}")
                t_q2 = Ev.op(ts(Ab(O_FS2 + (cch % 2) * 256, 512),
                                A(base + q4 * 512, 512), 0.5, ALU.add),
                             inc=vec)
                Ea.waitev(t_q2)
                t_q2s = Ea.dma(xq2_sc[cch][:, ha * 512:ha * 512 + 512],
                               Ab(O_FS2 + (cch % 2) * 256, 512), st)
                g_set(f"q2s{cch % 2}", (st, t_q2s[1]))
    t_f2_done = st.n

    barrier()
    Ev.op(recip(A(O_ROWA, 1024, 0, 1), A(O_ROW2, 1024, 0, 1)))
    Ev.op(ts(A(O_ROWA, 1024, 0, 1), A(O_ROWA, 1024, 0, 1),
             A(O_SVAL + 2, 1, 0, 1), ALU.mult, INV_SQ2 * 0.5, ALU.mult))
    t_cd = Ev.op(tt(A(O_ROWA, 1024, 0, 1), A(O_ROWA, 1024, 0, 1),
                    A(O_ROWB, 1024, 0, 1), ALU.mult), inc=vec)

    for oblk in range(4):
        for kg in range(16):
            wd = A(O_WDS + (kg % 2) * 2048, 2048)
            g_wait(Es, f"wds{kg % 2}")
            t_wd = None
            for oo in range(4):
                t_wd = Es.dma(wd[:, oo * 512:oo * 512 + 512],
                              wdt[oblk * 4 + oo][:, kg * 512:kg * 512 + 512], ld)
            Ea.wait(ld, t_wd[1])
            Ea.waitev(t_sg)
            for oo in range(4):
                s1b = 38600 + (oo % 2) * 512
                s2b = 39624 + (oo % 2) * 512
                g_wait(Ea, f"sgd{oo % 2}")
                Ea.op(actv(A(s1b, 512), wd[:, oo * 512:oo * 512 + 512],
                           AF.Sign, bias=A(O_SG + 5, 1)))
                t_sg2 = Ea.op(actv(A(s2b, 512),
                                   wd[:, oo * 512:oo * 512 + 512],
                                   AF.Sign, bias=A(O_SG + 2, 1)), inc=act)
                Ev.waitev(t_sg2)
                t_tn = Ev.op(tt(Ab(O_WDS + (kg % 2) * 2048 + oo * 512, 512),
                                A(s1b, 512), A(s2b, 512), ALU.add), inc=vec)
                g_set(f"sgd{oo % 2}", t_tn)
            g_set(f"terd{kg % 2}", t_tn)
            t_mmlast = None
            for kk in range(4):
                xb = Ab(O_XQ2B + (kk % 2) * 512, 1024)
                g_wait(Eg, f"xq2b{kk % 2}")
                t_xq2 = Eg.dma(xb, xq2_sc[kg * 4 + kk], gld)
                Ep.wait(gld, t_xq2[1])
                Ep.waitev(t_tn)
                if kg == 0 and kk == 0:
                    for jj in range(8):
                        g_wait(Ep, f"pp{jj}")
                for oo in range(4):
                    for half in range(2):
                        t_mmlast = Ep.op(
                            mm(pp[oo * 2 + half][:, 0:512],
                               Ab(O_WDS + (kg % 2) * 2048 + oo * 512
                                  + kk * 64, 128),
                               xb[:, half * 512:half * 512 + 512],
                               start=(kg == 0 and kk == 0),
                               stop=(kg == 15 and kk == 3)),
                            inc=(pe if (oo == 3 and half == 1) else None))
                g_set(f"xq2b{kk % 2}", t_mmlast)
            g_set(f"wds{kg % 2}", t_mmlast)
        for oo in range(4):
            o = oblk * 4 + oo
            Ev.waitev(t_mmlast)
            for half in range(2):
                t_dn = Ev.op(cp(A(O_DNS + o * 1024 + half * 512, 512),
                                pp[oo * 2 + half][:, 0:512]), inc=vec)
        for jj in range(8):
            g_set(f"pp{jj}", t_dn)
    g_wait(Ep, "pp4")
    Ep.waitev(t_cd)
    Ep.wait(vec, vec.n)
    t_cr = Ep.op(mm(pp[4][:, 0:512], ONES1, A(O_ROWA, 512, 0, 1)), inc=pe)
    t_cr2 = Ep.op(mm(pp[5][:, 0:512], ONES1, A(O_ROWA + 512, 512, 0, 1)),
                  inc=pe)
    t_stats = None
    for o in range(16):
        xb = A(O_X1B + (o % 2) * 1024, 1024)
        g_wait(Es, f"x1b{o % 2}")
        t_x1r = Es.dma(xb, x1_sc[o], ld)
        Ev.waitev(t_cr2)
        Ev.wait(ld, t_x1r[1])
        for half in range(2):
            dn = A(O_DNS + o * 1024 + half * 512, 512)
            Ev.op(tt(dn, dn, pp[4 + half][:, 0:512], ALU.mult))
            t_v2 = Ev.op(tt(dn, dn, xb[:, half * 512:half * 512 + 512],
                            ALU.add), inc=vec)
            g_set(f"x1b{o % 2}", t_v2)
            Ea.waitev(t_v2)
            g_wait(Ea, "lnsq")
            t_sq = Ea.op(actv(A(O_LNS, 512), dn, AF.Square), inc=act)
            if o == 0 and half == 0:
                g_wait(Ep, "pp2")
                g_wait(Ep, "pp3")
                g_wait(Ep, "pp6")
                g_wait(Ep, "pp7")
            Ep.waitev(t_v2)
            Ep.op(mm(pp[2 + half][0:1, 0:512], ONESC, dn,
                     start=(o == 0), stop=(o == 15)))
            Ep.waitev(t_sq)
            t_stats = Ep.op(mm(pp[6 + half][0:1, 0:512], ONESC, A(O_LNS, 512),
                               start=(o == 0), stop=(o == 15)), inc=pe)
            g_set("lnsq", t_stats)
    Ev.waitev(t_stats)
    for half in range(2):
        ro = half * 512
        Ev.op(ts(A(O_ROWA + ro, 512, 0, 1), pp[2 + half][0:1, 0:512], INV_D,
                 ALU.mult))
        Ev.op(ts(A(O_ROWB + ro, 512, 0, 1), pp[6 + half][0:1, 0:512], INV_D,
                 ALU.mult))
        Ev.op(tt(A(O_ROW1 + ro, 512, 0, 1), A(O_ROWA + ro, 512, 0, 1),
                 A(O_ROWA + ro, 512, 0, 1), ALU.mult))
        Ev.op(tt(A(O_ROWB + ro, 512, 0, 1), A(O_ROWB + ro, 512, 0, 1),
                 A(O_ROW1 + ro, 512, 0, 1), ALU.subtract))
        t_var = Ev.op(ts(A(O_ROWB + ro, 512, 0, 1), A(O_ROWB + ro, 512, 0, 1),
                         1e-5, ALU.add), inc=vec)
        Ea.waitev(t_var)
        t_sd = Ea.op(actv(A(O_ROW2 + ro, 512, 0, 1), A(O_ROWB + ro, 512, 0, 1),
                          AF.Sqrt), inc=act)
        Ev.waitev(t_sd)
        Ev.op(recip(A(O_ROW2 + ro, 512, 0, 1), A(O_ROW2 + ro, 512, 0, 1)))
        Ev.op(tt(A(O_ROW1 + ro, 512, 0, 1), A(O_ROW2 + ro, 512, 0, 1),
                 A(O_ROW2 + ro, 512, 0, 1), ALU.mult))
        Ev.op(tt(A(O_ROW1 + ro, 512, 0, 1), A(O_ROW1 + ro, 512, 0, 1),
                 A(O_ROWB + ro, 512, 0, 1), ALU.mult))
        Ev.op(ts(A(O_ROW1 + ro, 512, 0, 1), A(O_ROW1 + ro, 512, 0, 1),
                 -0.5, ALU.mult, 1.5, ALU.add))
        t_rst = Ev.op(tt(A(O_ROW2 + ro, 512, 0, 1), A(O_ROW2 + ro, 512, 0, 1),
                         A(O_ROW1 + ro, 512, 0, 1), ALU.mult), inc=vec)
    g_set("pp2", t_rst)
    g_set("pp3", t_rst)
    g_set("pp6", t_rst)
    g_set("pp7", t_rst)
    g_wait(Ep, "pp2")
    Ep.waitev(t_rst)
    t_mr0 = Ep.op(mm(pp[2][:, 0:512], ONES1, A(O_ROWA, 512, 0, 1)), inc=pe)
    t_mr1 = Ep.op(mm(pp[3][:, 0:512], ONES1, A(O_ROWA + 512, 512, 0, 1)),
                  inc=pe)
    t_rr0 = Ep.op(mm(pp[6][:, 0:512], ONES1, A(O_ROW2, 512, 0, 1)), inc=pe)
    t_rr1 = Ep.op(mm(pp[7][:, 0:512], ONES1, A(O_ROW2 + 512, 512, 0, 1)),
                  inc=pe)
    t_x2 = None
    for o in range(16):
        Ev.waitev(t_rr1)
        for half in range(2):
            dn = A(O_DNS + o * 1024 + half * 512, 512)
            Ev.op(tt(dn, dn, pp[2 + half][:, 0:512], ALU.subtract))
            t_x2 = Ev.op(tt(dn, dn, pp[6 + half][:, 0:512], ALU.mult), inc=vec)
    barrier()
    t_out = None
    for tts in range(8):
        g_wait(Ev, "outa")
        for o in range(16):
            ppj = pp[o % 2]
            g_wait(Ep, f"ppt{o % 2}")
            Ep.waitev(t_x2)
            t_tr = Ep.op(trn(ppj[:, 0:128],
                             A(O_DNS + o * 1024 + tts * 128, 128)), inc=pe)
            Ev.waitev(t_tr)
            t_oc = Ev.op(cp(A(O_OUTA + o * 128, 128), ppj[:, 0:128]), inc=vec)
            g_set(f"ppt{o % 2}", t_oc)
        Eg.waitev(t_oc)
        t_out = Eg.dma(out[tts * 128:tts * 128 + 128, :], A(O_OUTA, 2048), st)
        g_set("outa", t_out)
    Eg.wait(st, st.n)

    with nc.Block() as block:
        @block.sync
        def _(eng):
            for f in Es.ops:
                f(eng)

        @block.tensor
        def _(eng):
            for f in Ep.ops:
                f(eng)

        @block.vector
        def _(eng):
            for f in Ev.ops:
                f(eng)

        @block.scalar
        def _(eng):
            for f in Ea.ops:
                f(eng)

        @block.gpsimd
        def _(eng):
            for f in Eg.ops:
                f(eng)

    ctx.close()
    return nc


_NC_CACHE = []


def kernel(**inputs):
    x = np.asarray(inputs["L0_attn_hidden_states"], dtype=np.float32)
    Wq = np.asarray(inputs["L0_attn_Wq"], dtype=np.float32)
    Wk = np.asarray(inputs["L0_attn_Wk"], dtype=np.float32)
    Wv = np.asarray(inputs["L0_attn_Wv"], dtype=np.float32)
    Wo = np.asarray(inputs["L0_attn_Wo"], dtype=np.float32)
    wg = np.asarray(inputs["L0_ffn_w_gate"], dtype=np.float32)
    wu = np.asarray(inputs["L0_ffn_w_up"], dtype=np.float32)
    wd = np.asarray(inputs["L0_ffn_w_down"], dtype=np.float32)

    nc = build_nc()
    in_maps, order_all = _stage(x, Wq, Wk, Wv, Wo, wg, wu, wd)
    res = run_bass_kernel_spmd(nc, in_maps, core_ids=list(range(8)))
    y = np.zeros((2, S, D), dtype=np.float32)
    for core in range(8):
        b, cols = order_all[core]
        y[b, cols, :] = np.asarray(res.results[core]["out"])
    return y


def _stage(x, Wq, Wk, Wv, Wo, wg, wu, wd):
    perm = _qperm()
    wqtS = _strips(np.ascontiguousarray(Wq.T[:, perm]))
    wktS = _strips(np.ascontiguousarray(Wk.T))
    wvtS = _strips(np.ascontiguousarray(Wv.T))
    wotS = _strips(np.ascontiguousarray(Wo.T[perm, :]))
    wgtS = _strips(np.ascontiguousarray(wg.T))
    wutS = _strips(np.ascontiguousarray(wu.T))
    wdtS = _strips(np.ascontiguousarray(wd.T))
    cosA, sinA = _rope_tables()
    h128 = _hadamard(128).astype(np.float32)
    ident = np.eye(128, dtype=np.float32)
    p64 = np.zeros((64, 64), dtype=np.float32)
    p64[:32, 32:] = -np.eye(32)
    p64[32:, :32] = np.eye(32)
    p2 = np.zeros((128, 128), dtype=np.float32)
    p2[:64, :64] = p64.T
    p2[64:, 64:] = p64.T
    ones1 = np.ones((1, 128), dtype=np.float32)
    onescol = np.ones((128, 1), dtype=np.float32)
    indq = np.zeros((128, 512), dtype=np.float32)
    for j in range(16):
        G, r = j // 4, j % 4
        hA, hB = 8 * G + r, 8 * G + 4 + r
        indq[0:64, j * 32 + hA] = 1.0
        indq[64:128, j * 32 + hB] = 1.0
    indk = np.zeros((128, 32), dtype=np.float32)
    for o in range(4):
        indk[0:64, o * 8 + 2 * o] = 1.0
        indk[64:128, o * 8 + 2 * o + 1] = 1.0
    indkq = np.zeros((8, 32), dtype=np.float32)
    for h in range(32):
        indkq[h // 4, h] = 1.0
    indrep = np.zeros((32, 2048), dtype=np.float32)
    for j in range(16):
        G, r = j // 4, j % 4
        hA, hB = 8 * G + r, 8 * G + 4 + r
        indrep[hA, j * 128 + np.arange(0, 64)] = 1.0
        indrep[hB, j * 128 + np.arange(64, 128)] = 1.0
    indgrp = np.zeros((8, 1024), dtype=np.float32)
    for g8 in range(8):
        indgrp[g8, g8 * 128:(g8 + 1) * 128] = 1.0
    in_maps = []
    order_all = []
    for core in range(8):
        b, c = core // 4, core % 4
        qt = _core_qtiles(c)
        cols = np.concatenate([np.arange(t * 128, (t + 1) * 128) for t in qt])
        order_all.append((b, cols))
        xTb = np.ascontiguousarray(x[b].T)
        xTm = np.ascontiguousarray(xTb[:, cols]).reshape(16, 128, NT)
        xTf = xTb.reshape(16, 128, S)
        in_maps.append({
            "xTm": xTm, "xTf": xTf,
            "wqt": wqtS, "wkt": wktS, "wvt": wvtS, "wot": wotS,
            "wgt": wgtS, "wut": wutS, "wdt": wdtS,
            "cosq": np.ascontiguousarray(cosA[:, cols]),
            "sinq": np.ascontiguousarray(sinA[:, cols]),
            "cosk": cosA, "sink": sinA,
            "masks": _masks_for_core(c),
            "h128": h128, "ident": ident, "p2": p2,
            "ones1": ones1, "onescol": onescol,
            "indq": indq, "indk": indk, "indkq": indkq,
            "indrep": indrep, "indgrp": indgrp,
        })
    return in_maps, order_all
